# version 03
# baseline (speedup 1.0000x reference)
import os
import os
import contextlib
import numpy as np
import concourse.bass as bass
import concourse.mybir as mybir
from concourse.bass_utils import run_bass_kernel_spmd

F32 = mybir.dt.float32
BF16 = mybir.dt.bfloat16
AF = mybir.ActivationFunctionType
ALU = mybir.AluOpType
AX = mybir.AxisListType

T = 4096
D = 1024
DI = 2048
H4 = 4096
NST = 16
NH = 16
HD = 64
KW = 31
TC = 512
NTC = T // TC
TB = 1024
NTB = T // TB

ENGS = ("sync", "vector", "scalar", "tensor", "gpsimd")
SEM_SEG_LIMIT = 30000


class Buf:
    __slots__ = ("w", "rs", "dsem", "is_psum")

    def __init__(self):
        self.w = None
        self.rs = []
        self.dsem = None
        self.is_psum = False


class Arena:
    def __init__(self):
        self.regions = []

    def take(self, start, end):
        b = Buf()
        for item in self.regions[:]:
            s0, e0, ob = item
            if s0 < end and start < e0:
                if ob.w is not None:
                    b.rs.append(ob.w)
                b.rs.extend(ob.rs)
                self.regions.remove(item)
                if s0 < start:
                    self.regions.append((s0, start, ob))
                if e0 > end:
                    self.regions.append((end, e0, ob))
        self.regions.append((start, end, b))
        return b


CSEMS = ("vector", "scalar", "tensor", "gpsimd_c")
N_SYNC_SEMS = 40
N_GP_SEMS = 30
DSEMS = tuple(f"d{i}" for i in range(N_SYNC_SEMS)) + tuple(f"g{i}" for i in range(N_GP_SEMS))
SEMS = CSEMS + DSEMS


def _sem_engine(sem):
    if sem == "gpsimd_c":
        return "gpsimd"
    return sem if sem in CSEMS else None


class Prog:
    def __init__(self):
        self.ops = {e: [] for e in ENGS}
        self.cnt = {sm: 0 for sm in SEMS}
        self.seg = {sm: 0 for sm in SEMS}
        self.segfinal = {sm: {} for sm in SEMS}
        self.pending = {e: {} for e in ENGS}
        self.waited = {e: {} for e in ENGS}
        self.dma_rr = {"sync": 0, "gpsimd": 0, "scalar": 0, "vector": 0, "tensor": 0}

    def next_dma_sem(self, q):
        if q == "gpsimd":
            sm = f"g{self.dma_rr[q] % N_GP_SEMS}"
        else:
            sm = f"d{self.dma_rr['sync'] % N_SYNC_SEMS}"
            self.dma_rr['sync'] += 1
            return sm
        self.dma_rr[q] += 1
        return sm

    def _wait(self, e, key):
        sem, seg, val = key
        if val <= 0:
            return
        if _sem_engine(sem) == e and e != "gpsimd":
            if not (seg == self.seg[sem] and val >= self.cnt[sem] - 1):
                return
        if sem not in CSEMS:
            val = self.cnt[sem] if seg == self.seg[sem] else self.segfinal[sem][seg]
        assert seg < self.seg[sem] or (seg == self.seg[sem] and val <= self.cnt[sem])
        if self.waited[e].get((sem, seg), 0) >= val:
            return
        p = self.pending[e]
        p[(sem, seg)] = max(p.get((sem, seg), 0), val)

    def emit(self, e, fn, *, dma=False, inc=True, dmasem=None):
        w = sorted(self.pending[e].items())
        for (ps, v) in w:
            self.waited[e][ps] = max(self.waited[e].get(ps, 0), v)
        self.pending[e] = {}
        if dma:
            sm = dmasem
            assert sm is not None
        else:
            sm = {"sync": None, "gpsimd": "gpsimd_c"}.get(e, e)
        n = (16 if dma else 1) if inc else 0
        if sm is None:
            n = 0
        if n and self.cnt[sm] + n > SEM_SEG_LIMIT:
            assert sm in CSEMS, f"DMA sem overflow on {sm}"
            self.segfinal[sm][self.seg[sm]] = self.cnt[sm]
            self.seg[sm] += 1
            self.cnt[sm] = 0
            assert self.seg[sm] < 2, "increase semaphore segments"
        if n:
            self.cnt[sm] += n
        self.ops[e].append((w, fn, sm, n, self.seg[sm] if sm else 0,
                            self.cnt[sm] if sm else 0))
        if sm is None:
            return ("vector", 0, 0)
        return (sm, self.seg[sm], self.cnt[sm])


def op(P, e, fn, rd=(), wr=(), dma=False, inc=True):
    inc = True
    for b in rd:
        if b.w is not None:
            P._wait(e, b.w)
    for b in wr:
        if b.w is not None:
            P._wait(e, b.w)
        for r in b.rs:
            P._wait(e, r)
    dmasem = None
    if dma:
        b0 = wr[0]
        if b0.dsem is None:
            b0.dsem = {}
        if e not in b0.dsem:
            b0.dsem[e] = P.next_dma_sem(e)
        dmasem = b0.dsem[e]
    key = P.emit(e, fn, dma=dma, inc=inc, dmasem=dmasem)
    for b in rd:
        b.rs.append(key)
    for b in wr:
        b.w = key
        b.rs = []
    return key


def build(debug=False, upto=99):
    nc = bass.Bass("TRN2", target_bir_lowering=False)
    P = Prog()

    def ein(name, shape):
        return nc.dram_tensor(name, shape, F32, kind="ExternalInput").ap()

    x_in = ein("x", [D, T])
    w_ip = ein("w_ip", [D, 2 * DI])
    convw = ein("convw", [128, 64])
    convb = ein("convb", [128, 16])
    w_xp = ein("w_xp", [128, 1536])
    w_dt = ein("w_dt", [64, DI])
    dtb = ein("dtb", [128, 16])
    alog = ein("alog", [128, 256])
    dvec = ein("dvec", [128, 16])
    w_op = ein("w_op", [DI, D])
    normw = ein("normw", [128, 8])
    w1t = ein("w1t", [D, H4])
    b1 = ein("b1", [128, 32])
    w2t = ein("w2t", [H4, D])
    b2 = ein("b2", [128, 8])
    wd2pe = ein("wd2pe", [128, 63])
    wg2pe = ein("wg2pe", [128, 63])
    bpe63d = ein("bpe63d", [128, 1])
    bpe63g = ein("bpe63g", [128, 1])
    stt63 = ein("stt63", [128, 8])
    wfct = ein("wfct", [D, D])
    gamma = ein("gamma", [128, 8])
    beta = ein("beta", [128, 8])
    cnw1 = ein("cnw1", [D, D])
    cnb1 = ein("cnb1", [128, 8])
    cnw2 = ein("cnw2", [D, 512])
    cnb2 = ein("cnb2", [128, 4])
    cnw3 = ein("cnw3", [512, 256])
    cnb3 = ein("cnb3", [128, 2])
    cnw4 = ein("cnw4", [128, 2])
    cnb4 = ein("cnb4", [1, 1])
    gcw1 = ein("gcw1", [D, 512])
    gcb1 = ein("gcb1", [128, 4])
    gcw2 = ein("gcw2", [512, 256])
    gcb2 = ein("gcb2", [128, 2])
    gcw3 = ein("gcw3", [128, 2])
    gcb3 = ein("gcb3", [1, 1])
    arange_ = ein("arange_", [1, 1])

    out_ext = nc.dram_tensor("out", [D, T], F32, kind="ExternalOutput").ap()

    def scr(name, shape, dt=F32):
        if debug:
            return nc.dram_tensor(name, shape, dt, kind="ExternalOutput").ap()
        return nc.dram_tensor(name, shape, dt).ap()

    XN_f = scr("XN_f", [D, T])
    XN_b = scr("XN_b", [D, T], BF16)
    RN_b = scr("RN_b", [D, T], BF16)
    U_f = scr("U_f", [DI, T])
    U_b = scr("U_b", [DI, T], BF16)
    G_f = scr("G_f", [DI, T])
    DLT = scr("DLT", [DI, T])
    DU_b = scr("DU_b", [DI, T], BF16)
    BROW = scr("BROW", [NST, T], BF16)
    CROW = scr("CROW", [NST, T], BF16)
    RIROW = scr("RIROW", [1, T])
    YG_b = scr("YG_b", [DI, T], BF16)
    X1p = scr("X1p", [D, T])
    X1 = scr("X1", [D, T])
    BW1 = scr("BW1", [D, H4], BF16)
    BW2 = scr("BW2", [H4, D], BF16)
    H1 = scr("H1", [H4, T], BF16)
    XN1_f = scr("XN1_f", [D, T])
    XN1_b = scr("XN1_b", [D, T], BF16)
    CIT = scr("CIT", [T, D], BF16)
    VLX = scr("VLX", [D, T])
    OUTCA = scr("OUTCA", [D, T])
    OUTBA = scr("OUTBA", [D, T], BF16)
    OUTCB = scr("OUTCB", [D, T])
    OUTBB = scr("OUTBB", [D, T], BF16)
    ZBB_BS = 128 * 160
    ZBB_HS = 64 + 32 * ZBB_BS
    ZBB = scr("ZBB", [NH, ZBB_HS], BF16)
    ZPB_BS = 128 * 257
    ZPB_HS = 64 + 32 * ZPB_BS
    ZPB = scr("ZPB", [NH, ZPB_HS], BF16)

    dbuf = {}
    dparts = {}
    for t_ in (XN_f, XN_b, RN_b, U_f, U_b, G_f, DLT, DU_b, BROW, CROW, RIROW,
               YG_b, X1p, X1, BW1, BW2, H1, XN1_f, XN1_b, CIT, VLX,
               OUTCA, OUTBA, OUTCB, OUTBB, ZBB, ZPB, out_ext):
        dbuf[id(t_)] = Buf()
    for t_ in (XN_f, XN_b, RN_b, U_f, U_b, G_f, DLT, DU_b, BROW, CROW,
               YG_b, X1p, X1, H1, XN1_f, XN1_b, CIT, VLX,
               OUTCA, OUTBA, OUTCB, OUTBB, out_ext):
        dparts[id(t_)] = [Buf() for _ in range(8)]

    def db(t_):
        return dbuf[id(t_)]

    def dbt(t_, lo, hi):
        ps = dparts.get(id(t_))
        if ps is None:
            return [dbuf[id(t_)]]
        p0 = lo * 8 // T
        p1 = (hi - 1) * 8 // T
        return ps[p0:p1 + 1]

    ctx = contextlib.ExitStack()
    with ctx:
        def sbuf(name, shape, dt=F32):
            return ctx.enter_context(nc.sbuf_tensor(name, shape, dt))
        def psum(name, shape, dt=F32):
            return ctx.enter_context(nc.psum_tensor(name, shape, dt))

        B1 = sbuf("B1", [128, 32768], BF16)
        B2 = sbuf("B2", [128, 32768], BF16)
        FA = sbuf("FA", [128, 6656])
        FB = sbuf("FB", [128, 6144])
        AB1 = Arena()
        AB2 = Arena()
        AFA = Arena()
        AFB = Arena()

        def b1v(o, p_, d1, s1=None, d2=None, s2=None):
            if d2 is None:
                return bass.AP(B1, o, [[32768, p_], [s1 or 1, d1]])
            return bass.AP(B1, o, [[32768, p_], [s1, d1], [s2, d2]])

        def b2v(o, p_, d1, s1=None, d2=None, s2=None):
            if d2 is None:
                return bass.AP(B2, o, [[32768, p_], [s1 or 1, d1]])
            return bass.AP(B2, o, [[32768, p_], [s1, d1], [s2, d2]])

        pers = {}

        def mk(name, shape, dt=F32):
            t_ = sbuf(name, shape, dt)
            pers[name] = Buf()
            return t_

        onesv = mk("onesv", [128, 1])
        ones1 = mk("ones1", [1, 128])
        ones1b = mk("ones1b", [1, 128], BF16)
        onesDIb = mk("onesDIb", [128, 1], BF16)
        onesH4b = mk("onesH4b", [128, 1], BF16)
        idb = mk("idb", [128, 128], BF16)
        a_sb = mk("a_sb", [128, 16, 16])
        d_sb = mk("d_sb", [128, 16])
        cb_sb = mk("cb_sb", [128, 16])
        cw_sb = mk("cw_sb", [128, 16, 4])
        nw_sb = mk("nw_sb", [128, 8])
        g_col = mk("g_col", [128, 8])
        bt_col = mk("bt_col", [128, 8])
        b1_sb = mk("b1_sb", [128, 32])
        b2_sb = mk("b2_sb", [128, 8])
        dtb_sb = mk("dtb_sb", [128, 16])
        wd_sb = mk("wd_sb", [128, 63], BF16)
        wg_sb = mk("wg_sb", [128, 63], BF16)
        bpe_d = mk("bpe_d", [128, 1])
        bpe_g = mk("bpe_g", [128, 1])
        stt_sb = mk("stt_sb", [128, 8])
        ob124 = mk("ob124", [128, 4])
        ob2t_lo = mk("ob2t_lo", [2, 128])
        ob2t_hi = mk("ob2t_hi", [2, 128])
        ritok = mk("ritok", [128, 32])
        wmask = mk("wmask", [128, 30], BF16)
        cn4_sb = mk("cn4_sb", [128, 2], BF16)
        gc3_sb = mk("gc3_sb", [128, 2], BF16)
        cnb1_sb = mk("cnb1_sb", [128, 8])
        cnb2_sb = mk("cnb2_sb", [128, 4])
        cnb3_sb = mk("cnb3_sb", [128, 2])
        gcb1_sb = mk("gcb1_sb", [128, 4])
        gcb2_sb = mk("gcb2_sb", [128, 2])
        sc_sb = mk("sc_sb", [1, 16])
        gvec = mk("gvec", [128, 8])
        gvec_b = mk("gvec_b", [128, 8], BF16)
        tails = mk("tails", [128, 16, 3])
        carry = mk("carry", [128, 16, 16])
        abuf = sbuf("abuf", [128, 2, TB])
        pers["abuf0"] = Buf()
        pers["abuf1"] = Buf()
        dub_sb = mk("dub_sb", [128, TB], BF16)
        pers["rowbuf"] = Buf()
        epst = mk("epst", [128, 4])
        zbig = mk("zbig", [128, 512], BF16)

        PY = psum("PY", [128, 1024])
        P0 = psum("P0", [128, 512])
        P1 = psum("P1", [128, 512])
        P2 = psum("P2", [128, 512])
        P3 = psum("P3", [128, 512])
        P4 = psum("P4", [128, 512])
        PBm = psum("PBm", [128, 1024], BF16)
        pbuf = {id(t_): Buf() for t_ in (PY, P0, P1, P2, P3, P4)}
        pb_half = [Buf(), Buf()]

        def pb(t_):
            return pbuf[id(t_)]

        sems = {}
        for sm in CSEMS:
            sems[sm] = [ctx.enter_context(nc.semaphore(f"sem_{sm}_{i}")) for i in range(2)]
        for sm in DSEMS:
            sems[sm] = [ctx.enter_context(nc.semaphore(f"sem_{sm}"))]
        block = ctx.enter_context(nc.Block())

        for b_ in list(pbuf.values()) + pb_half:
            b_.is_psum = True

        psum_rd = {"e": None, "key": None}

        tr_hist = []
        tr_base = [None]

        def TR(fn, rd=(), wr=(), base=0):
            if tr_hist:
                if base != tr_base[0] and tr_hist[-1][0] is not None:
                    P._wait("tensor", tr_hist[-1][0])
                elif len(tr_hist) >= 2 and tr_hist[-2][0] is not None:
                    P._wait("tensor", tr_hist[-2][0])
            tr_base[0] = base
            k = O("tensor", fn, rd=rd, wr=wr)
            slot = [None]
            tr_hist.append(slot)
            return slot

        def O(e, fn, rd=(), wr=(), dma=False, inc=True):
            touches_psum = any(getattr(b_, "is_psum", False) for b_ in rd)
            if touches_psum and e in ("vector", "scalar"):
                if psum_rd["e"] is not None and psum_rd["e"] != e:
                    P._wait(e, psum_rd["key"])
            k = op(P, e, fn, rd=rd, wr=wr, dma=dma, inc=inc)
            if touches_psum and e in ("vector", "scalar"):
                psum_rd["e"] = e
                psum_rd["key"] = k
            return k

        def LD(dst, src, rd=(), wr=()):
            return O("sync", lambda e, d=dst, s=src: e.dma_start(d, s),
                     rd=rd, wr=wr, dma=True)

        def ST(dst, src, rd=(), wr=(), q="gpsimd"):
            return O(q, lambda e, d=dst, s=src: e.dma_start(d, s),
                     rd=rd, wr=wr, dma=True)

        smalls = [
            (a_sb[:].rearrange("p a b -> p (a b)"), alog, "a_sb"),
            (d_sb[:], dvec, "d_sb"),
            (cb_sb[:], convb, "cb_sb"),
            (cw_sb[:].rearrange("p a b -> p (a b)"), convw, "cw_sb"),
            (nw_sb[:], normw, "nw_sb"),
            (g_col[:], gamma, "g_col"),
            (bt_col[:], beta, "bt_col"),
            (b1_sb[:], b1, "b1_sb"),
            (b2_sb[:], b2, "b2_sb"),
            (dtb_sb[:], dtb, "dtb_sb"),
            (stt_sb[:], stt63, "stt_sb"),
            (bpe_d[:], bpe63d, "bpe_d"),
            (bpe_g[:], bpe63g, "bpe_g"),
            (cnb1_sb[:], cnb1, "cnb1_sb"),
            (cnb2_sb[:], cnb2, "cnb2_sb"),
            (cnb3_sb[:], cnb3, "cnb3_sb"),
            (gcb1_sb[:], gcb1, "gcb1_sb"),
            (gcb2_sb[:], gcb2, "gcb2_sb"),
            (sc_sb[0:1, 4:5], cnb4, "sc_sb"),
            (sc_sb[0:1, 5:6], gcb3, "sc_sb"),
            (sc_sb[0:1, 1:2], arange_, "sc_sb"),
        ]
        for dst, src, nm in smalls:
            LD(dst, src, wr=[pers[nm]])

        for t_, v, nm in ((onesv, 1.0 / 1024.0, "onesv"), (ones1, 1.0, "ones1"),
                          (ones1b, 1.0, "ones1b"), (onesDIb, 1.0 / 1024.0, "onesDIb"),
                          (onesH4b, 1.0 / 4096.0, "onesH4b")):
            O("gpsimd", lambda e, t_=t_, v=v: e.memset(t_[:], v), wr=[pers[nm]])
        O("gpsimd", lambda e: e.memset(idb[:], 0.0), wr=[pers["idb"]])
        O("gpsimd", lambda e: e.affine_select(out=idb[:], in_=idb[:],
                                              compare_op=ALU.not_equal, fill=1.0, base=0,
                                              pattern=[[-1, 128]], channel_multiplier=1),
          wr=[pers["idb"]])
        O("gpsimd", lambda e: e.memset(wmask[:], 1.0), wr=[pers["wmask"]])
        O("gpsimd", lambda e: e.affine_select(
            out=wmask[0:31, :], in_=wmask[0:31, :],
            compare_op=ALU.is_ge, fill=0.0, base=-30,
            pattern=[[1, 30]], channel_multiplier=1), wr=[pers["wmask"]])
        for j in range(1, 4):
            O("gpsimd", lambda e, j=j: e.dma_start(wmask[j * 32:j * 32 + 31, :],
                                                   wmask[0:31, :]),
              rd=[pers["wmask"]], wr=[pers["wmask"]], dma=True)
        O("gpsimd", lambda e: e.memset(ob124[:], 1.0), wr=[pers["ob124"]])
        O("gpsimd", lambda e: e.affine_select(
            out=ob124[:], in_=ob124[:], compare_op=ALU.is_ge, fill=0.0,
            base=0, pattern=[[-32, 4]], channel_multiplier=1), wr=[pers["ob124"]])
        O("gpsimd", lambda e: e.affine_select(
            out=ob124[:], in_=ob124[:], compare_op=ALU.is_ge, fill=0.0,
            base=30, pattern=[[32, 4]], channel_multiplier=-1), wr=[pers["ob124"]])
        O("gpsimd", lambda e: e.memset(ob2t_lo[:], 1.0), wr=[pers["ob2t_lo"]])
        O("gpsimd", lambda e: e.affine_select(
            out=ob2t_lo[:], in_=ob2t_lo[:], compare_op=ALU.is_ge, fill=0.0,
            base=0, pattern=[[1, 128]], channel_multiplier=-32), wr=[pers["ob2t_lo"]])
        O("gpsimd", lambda e: e.affine_select(
            out=ob2t_lo[:], in_=ob2t_lo[:], compare_op=ALU.is_ge, fill=0.0,
            base=30, pattern=[[-1, 128]], channel_multiplier=32), wr=[pers["ob2t_lo"]])
        O("gpsimd", lambda e: e.memset(ob2t_hi[:], 1.0), wr=[pers["ob2t_hi"]])
        O("gpsimd", lambda e: e.affine_select(
            out=ob2t_hi[:], in_=ob2t_hi[:], compare_op=ALU.is_ge, fill=0.0,
            base=-64, pattern=[[1, 128]], channel_multiplier=-32), wr=[pers["ob2t_hi"]])
        O("gpsimd", lambda e: e.affine_select(
            out=ob2t_hi[:], in_=ob2t_hi[:], compare_op=ALU.is_ge, fill=0.0,
            base=94, pattern=[[-1, 128]], channel_multiplier=32), wr=[pers["ob2t_hi"]])
        O("gpsimd", lambda e: e.iota(ritok[:], [[128, 32]], base=1, channel_multiplier=1,
                                     allow_small_or_imprecise_dtypes=True), wr=[pers["ritok"]])
        O("gpsimd", lambda e: e.memset(zbig[:], 0.0), wr=[pers["zbig"]])
        O("gpsimd", lambda e: e.memset(tails[:], 0.0), wr=[pers["tails"]])
        for j, v in ((0, 1e-5), (1, 1e-6), (2, -1.0), (3, 1.0)):
            O("gpsimd", lambda e, j=j, v=v: e.memset(epst[:, j:j + 1], v), wr=[pers["epst"]])

        fari = AFA.take(0, 4096)
        O("gpsimd", lambda e: e.iota(FA[0:1, 0:T], [[1, T]], base=1, channel_multiplier=0,
                                     allow_small_or_imprecise_dtypes=True), wr=[fari])
        O("vector", lambda e: e.reciprocal(FA[0:1, 0:T], FA[0:1, 0:T]), rd=[fari], wr=[fari])
        O("vector", lambda e: e.reciprocal(ritok[:], ritok[:]),
          rd=[pers["ritok"]], wr=[pers["ritok"]])
        ST(RIROW, FA[0:1, 0:T], rd=[fari], wr=[db(RIROW)])

        O("scalar", lambda e: e.activation(a_sb[:].rearrange("p a b -> p (a b)"),
                                           a_sb[:].rearrange("p a b -> p (a b)"), AF.Exp),
          rd=[pers["a_sb"]], wr=[pers["a_sb"]])
        O("vector", lambda e: e.tensor_scalar(a_sb[:].rearrange("p a b -> p (a b)"),
                                              a_sb[:].rearrange("p a b -> p (a b)"),
                                              -1.0, None, ALU.mult),
          rd=[pers["a_sb"]], wr=[pers["a_sb"]])

        fpe = AFA.take(4096, 4300)
        LD(FA[:, 4096:4159], wd2pe, wr=[fpe])
        O("vector", lambda e: e.tensor_copy(wd_sb[:], FA[:, 4096:4159]),
          rd=[fpe], wr=[pers["wd_sb"]])
        LD(FA[:, 4096:4159], wg2pe, wr=[fpe])
        O("vector", lambda e: e.tensor_copy(wg_sb[:], FA[:, 4096:4159]),
          rd=[fpe], wr=[pers["wg_sb"]])
        LD(FA[:, 4160:4162], cnw4, wr=[fpe])
        LD(FA[:, 4162:4164], gcw3, wr=[fpe])
        O("vector", lambda e: e.tensor_copy(cn4_sb[:], FA[:, 4160:4162]),
          rd=[fpe], wr=[pers["cn4_sb"]])
        O("vector", lambda e: e.tensor_copy(gc3_sb[:], FA[:, 4162:4164]),
          rd=[fpe], wr=[pers["gc3_sb"]])

        def binarize(wt, bwout, nchunk, onescol, onesbuf, Hh):
            fw = AFA.take(0, 4096)
            bab = AB1.take(0, nchunk * Hh)
            bsg = AB2.take(0, 4096)
            bar = AB2.take(8192, 8192 + Hh)
            far = AFB.take(0, 4096)
            for c in range(nchunk):
                LD(FA[:, 0:Hh], wt[c * 128:(c + 1) * 128, :], wr=[fw])
                O("scalar", lambda e, c=c, Hh=Hh: e.activation(
                    b1v(c * Hh, 128, Hh), FA[:, 0:Hh], AF.Abs), rd=[fw], wr=[bab])
            nq = Hh // 512
            for q in range(nq):
                pq = (P0, P1)[q % 2]
                for c in range(nchunk):
                    O("tensor", lambda e, c=c, q=q, pq=pq, Hh=Hh: e.matmul(
                        pq[0:1, :], onescol[:], b1v(c * Hh + q * 512, 128, 512),
                        start=(c == 0), stop=(c == nchunk - 1)),
                      rd=[bab, onesbuf], wr=[pb(pq)], inc=(c == nchunk - 1))
                O("vector", lambda e, q=q, pq=pq: e.tensor_copy(
                    FB[0:1, q * 512:(q + 1) * 512], pq[0:1, :]),
                  rd=[pb(pq)], wr=[far])
            O("vector", lambda e, Hh=Hh: e.tensor_copy(
                b2v(8192, 1, Hh), FB[0:1, 0:Hh]), rd=[far], wr=[bar])
            for c in range(nchunk):
                LD(FA[:, 0:Hh], wt[c * 128:(c + 1) * 128, :], wr=[fw])
                O("scalar", lambda e, Hh=Hh: e.activation(
                    b2v(0, 128, Hh), FA[:, 0:Hh], AF.Sign), rd=[fw], wr=[bsg])
                for q in range(Hh // 512):
                    ph = q % 2
                    O("tensor", lambda e, q=q, ph=ph, Hh=Hh: e.matmul(
                        PY[:, ph * 512:(ph + 1) * 512], ones1b[:],
                        b2v(8192 + q * 512, 1, 512), start=True, stop=True),
                      rd=[bar, pers["ones1b"]], wr=[pb(PY)])
                    O("vector", lambda e, q=q, ph=ph: e.tensor_tensor(
                        b2v(q * 512, 128, 512), b2v(q * 512, 128, 512),
                        PY[:, ph * 512:(ph + 1) * 512], ALU.mult),
                      rd=[pb(PY), bsg], wr=[bsg])
                ST(bwout[c * 128:(c + 1) * 128, :], b2v(0, 128, Hh),
                   rd=[bsg], wr=[db(bwout)])

        if upto >= 1:
            binarize(w1t, BW1, 8, onesDIb, pers["onesDIb"], H4)
            binarize(w2t, BW2, 32, onesH4b, pers["onesH4b"], D)

        def layer_norm_phase(src_ap, src_buf, dst_f, dst_b, rn_dst, first):
            fxt = AFA.take(0, 4096)
            fsq = AFB.take(0, 1024)
            fxn = AFB.take(1024, 2048)
            bxb = AB1.take(0, 1024)
            brn = AB2.take(0, 1024)
            rows = AFB.take(2048, 5120)
            for tc in range(NTC):
                for dc in range(8):
                    LD(FA[:, dc * 512:(dc + 1) * 512],
                       src_ap[dc * 128:(dc + 1) * 128, tc * TC:(tc + 1) * TC],
                       rd=(dbt(src_buf, tc * TC, (tc + 1) * TC) if src_buf is not None else ()),
                       wr=[fxt])
                for dc in range(8):
                    sl = dc % 2
                    O("scalar", lambda e, dc=dc, sl=sl: e.activation(
                        FB[:, sl * 512:(sl + 1) * 512], FA[:, dc * 512:(dc + 1) * 512],
                        AF.Square), rd=[fxt], wr=[fsq])
                    O("tensor", lambda e, dc=dc: e.matmul(
                        P0[0:1, :], onesv[:], FA[:, dc * 512:(dc + 1) * 512],
                        start=(dc == 0), stop=(dc == 7)),
                      rd=[fxt, pers["onesv"]], wr=[pb(P0)], inc=(dc == 7))
                    O("tensor", lambda e, dc=dc, sl=sl: e.matmul(
                        P1[0:1, :], onesv[:], FB[:, sl * 512:(sl + 1) * 512],
                        start=(dc == 0), stop=(dc == 7)),
                      rd=[fsq, pers["onesv"]], wr=[pb(P1)], inc=(dc == 7))
                mu = FB[0:1, 2048:2560]
                var = FB[0:1, 2560:3072]
                sd = FB[0:1, 3072:3584]
                rr = FB[0:1, 3584:4096]
                r2 = FB[0:1, 4096:4608]
                t5 = FB[0:1, 4608:5120]
                O("vector", lambda e: e.tensor_copy(mu, P0[0:1, :]), rd=[pb(P0)], wr=[rows])
                O("vector", lambda e: e.tensor_tensor(var, mu, mu, ALU.mult),
                  rd=[rows], wr=[rows])
                O("vector", lambda e: e.tensor_tensor(var, P1[0:1, :], var, ALU.subtract),
                  rd=[pb(P1), rows], wr=[rows])
                O("scalar", lambda e: e.activation(sd, var, AF.Sqrt, bias=epst[0:1, 0:1]),
                  rd=[rows, pers["epst"]], wr=[rows])
                O("vector", lambda e: e.reciprocal(rr, sd), rd=[rows], wr=[rows])
                if first:
                    O("vector", lambda e: e.tensor_tensor(t5, var, rr, ALU.mult),
                      rd=[rows], wr=[rows])
                    O("vector", lambda e: e.tensor_tensor(t5, t5, rr, ALU.mult),
                      rd=[rows], wr=[rows])
                    O("scalar", lambda e: e.activation(t5, t5, AF.Sqrt, bias=epst[0:1, 0:1]),
                      rd=[rows, pers["epst"]], wr=[rows])
                    O("vector", lambda e: e.reciprocal(r2, t5), rd=[rows], wr=[rows])
                O("tensor", lambda e: e.matmul(P2[:, :], ones1[:], mu, start=True, stop=True),
                  rd=[rows, pers["ones1"]], wr=[pb(P2)])
                O("tensor", lambda e: e.matmul(P3[:, :], ones1[:], rr, start=True, stop=True),
                  rd=[rows, pers["ones1"]], wr=[pb(P3)])
                if first:
                    O("tensor", lambda e: e.matmul(P4[:, :], ones1[:], r2,
                                                   start=True, stop=True),
                      rd=[rows, pers["ones1"]], wr=[pb(P4)])
                for dc in range(8):
                    sl = dc % 2
                    xn = FB[:, 1024 + sl * 512:1024 + (sl + 1) * 512]
                    xb = b1v(sl * 512, 128, 512)
                    rn = b2v(sl * 512, 128, 512)
                    O("vector", lambda e, dc=dc, xn=xn: e.tensor_tensor(
                        xn, FA[:, dc * 512:(dc + 1) * 512], P2[:, :], ALU.subtract),
                      rd=[fxt, pb(P2)], wr=[fxn])
                    O("vector", lambda e, xn=xn: e.tensor_tensor(xn, xn, P3[:, :], ALU.mult),
                      rd=[pb(P3), fxn], wr=[fxn])
                    O("scalar", lambda e, xn=xn, xb=xb: e.activation(xb, xn, AF.Copy),
                      rd=[fxn], wr=[bxb])
                    if first:
                        O("vector", lambda e, xn=xn, rn=rn: e.tensor_tensor(
                            rn, xn, P4[:, :], ALU.mult), rd=[fxn, pb(P4)], wr=[brn])
                    ST(dst_f[dc * 128:(dc + 1) * 128, tc * TC:(tc + 1) * TC], xn,
                       rd=[fxn], wr=dbt(dst_f, tc * TC, (tc + 1) * TC))
                    ST(dst_b[dc * 128:(dc + 1) * 128, tc * TC:(tc + 1) * TC], xb,
                       rd=[bxb], wr=dbt(dst_b, tc * TC, (tc + 1) * TC))
                    if first:
                        ST(rn_dst[dc * 128:(dc + 1) * 128, tc * TC:(tc + 1) * TC], rn,
                           rd=[brn], wr=dbt(rn_dst, tc * TC, (tc + 1) * TC))

        if upto >= 2:
            layer_norm_phase(x_in, None, XN_f, XN_b, RN_b, True)

        if upto >= 3:
            bw1p = AB1.take(0, 32768)
            fwl = AFA.take(0, 4096)
            for c in range(8):
                LD(FA[:, 0:4096], w_ip[c * 128:(c + 1) * 128, :], wr=[fwl])
                O("vector", lambda e, c=c: e.tensor_scalar(
                    b1v(c * 4096, 128, 4096), FA[:, 0:4096], nw_sb[:, c:c + 1],
                    None, ALU.mult), rd=[fwl, pers["nw_sb"]], wr=[bw1p])
            brn_t = [AB2.take(s * 4096, (s + 1) * 4096) for s in range(2)]
            bub = AB2.take(8192, 8704)
            fxp = AFA.take(4096, 4616)
            facc = AFA.take(4616, 5128)
            fuo = AFB.take(0, 512)
            fgo = AFB.take(512, 1024)
            banks = [(P0, pb(P0)), (P1, pb(P1)), (P2, pb(P2)), (P3, pb(P3))]
            for tc in range(NTC):
                sl = tc % 2
                for h in range(2):
                    LD(b2v(sl * 4096 + h * 2048, 128, 4, 512, 512, 1),
                       RN_b.rearrange("(a p) t -> p a t", p=128)[:, h * 4:(h + 1) * 4,
                                                                tc * TC:(tc + 1) * TC],
                       rd=dbt(RN_b, tc * TC, (tc + 1) * TC), wr=[brn_t[sl]])
                for m in range(32):
                    bk, bkb = banks[m % 4]
                    for k in range(8):
                        O("tensor", lambda e, m=m, k=k, bk=bk, sl=sl: e.matmul(
                            bk[:, :], b1v(k * 4096 + m * 128, 128, 128),
                            b2v(sl * 4096 + k * 512, 128, 512),
                            start=(k == 0), stop=(k == 7)),
                          rd=[bw1p, brn_t[sl]], wr=[bkb], inc=(k == 7))
                    if m < 16:
                        mc = m
                        O("vector", lambda e, mc=mc: e.tensor_copy(
                            FA[:, 4096:4099], tails[:, mc, :]),
                          rd=[pers["tails"]], wr=[fxp])
                        O("scalar", lambda e, bk=bk: e.activation(
                            FA[:, 4099:4611], bk[:, :], AF.Copy), rd=[bkb], wr=[fxp])
                        O("vector", lambda e, mc=mc: e.tensor_copy(
                            tails[:, mc, :], FA[:, 4608:4611]),
                          rd=[fxp], wr=[pers["tails"]])
                        O("vector", lambda e, mc=mc: e.tensor_scalar(
                            FA[:, 4616:5128], FA[:, 4096:4608], cw_sb[:, mc, 0:1],
                            None, ALU.mult), rd=[fxp, pers["cw_sb"]], wr=[facc])
                        for kk in range(1, 4):
                            O("vector", lambda e, mc=mc, kk=kk: e.scalar_tensor_tensor(
                                FA[:, 4616:5128], FA[:, 4096 + kk:4608 + kk],
                                cw_sb[:, mc, kk:kk + 1], FA[:, 4616:5128],
                                ALU.mult, ALU.add), rd=[fxp, pers["cw_sb"], facc], wr=[facc])
                        O("scalar", lambda e, mc=mc: e.activation(
                            FB[:, 0:512], FA[:, 4616:5128], AF.Silu,
                            bias=cb_sb[:, mc:mc + 1]), rd=[facc, pers["cb_sb"]], wr=[fuo])
                        O("scalar", lambda e, mc=mc: e.activation(
                            b2v(8192, 128, 512), FA[:, 4616:5128], AF.Silu,
                            bias=cb_sb[:, mc:mc + 1]), rd=[facc, pers["cb_sb"]], wr=[bub])
                        ST(U_f[mc * 128:(mc + 1) * 128, tc * TC:(tc + 1) * TC],
                           FB[:, 0:512], rd=[fuo], wr=dbt(U_f, tc * TC, (tc + 1) * TC))
                        ST(U_b[mc * 128:(mc + 1) * 128, tc * TC:(tc + 1) * TC],
                           b2v(8192, 128, 512), rd=[bub], wr=dbt(U_b, tc * TC, (tc + 1) * TC))
                    else:
                        mc = m - 16
                        O("scalar", lambda e, bk=bk: e.activation(
                            FB[:, 512:1024], bk[:, :], AF.Silu), rd=[bkb], wr=[fgo])
                        ST(G_f[mc * 128:(mc + 1) * 128, tc * TC:(tc + 1) * TC],
                           FB[:, 512:1024], rd=[fgo], wr=dbt(G_f, tc * TC, (tc + 1) * TC))

        KSUB = int(os.environ.get("KSUB", "99"))
        KTC = int(os.environ.get("KTC", "8"))
        if upto >= 4:
            bxp = AB1.take(0, 1536)
            bdt = AB1.take(2048, 4096)
            bdtb = AB1.take(4096, 4608)
            bbce = AB1.take(4608, 5120)
            bubt = [AB2.take(s * 8192, (s + 1) * 8192) for s in range(2)]
            bduo = AB2.take(16384, 16896)
            fwl = AFA.take(0, 2048)
            f_ab = AFB.take(0, 512)
            f_ln = AFB.take(512, 1024)
            f_dl = AFB.take(1024, 1536)
            f_uf = AFB.take(1536, 2048)
            LD(FA[:, 0:1536], w_xp, wr=[fwl])
            O("vector", lambda e: e.tensor_copy(b1v(0, 128, 1536), FA[:, 0:1536]),
              rd=[fwl], wr=[bxp])
            LD(FA[0:64, 0:2048], w_dt, wr=[fwl])
            O("vector", lambda e: e.tensor_copy(b1v(2048, 64, 2048), FA[0:64, 0:2048]),
              rd=[fwl], wr=[bdt])
            for tc in range(KTC):
                sl = tc % 2
                for h in range(4):
                    LD(b2v(sl * 8192 + h * 2048, 128, 4, 512, 512, 1),
                       U_b.rearrange("(a p) t -> p a t", p=128)[:, h * 4:(h + 1) * 4,
                                                               tc * TC:(tc + 1) * TC],
                       rd=dbt(U_b, tc * TC, (tc + 1) * TC), wr=[bubt[sl]])
                for k in range(16):
                    O("tensor", lambda e, k=k, sl=sl: e.matmul(
                        P4[0:96, :], b1v(k * 96, 128, 96),
                        b2v(sl * 8192 + k * 512, 128, 512),
                        start=(k == 0), stop=(k == 15)),
                      rd=[bxp, bubt[sl]], wr=[pb(P4)], inc=(k == 15))
                O("scalar", lambda e: e.activation(b1v(4096, 64, 512), P4[0:64, :], AF.Copy),
                  rd=[pb(P4)], wr=[bdtb])
                O("vector", lambda e: e.tensor_copy(b1v(4608, 96, 512)[64:96, :],
                                                    P4[64:96, :]),
                  rd=[pb(P4)], wr=[bbce])
                ST(BROW[:, tc * TC:(tc + 1) * TC], b1v(4608, 96, 512)[64:80, :],
                   rd=[bbce], wr=dbt(BROW, tc * TC, (tc + 1) * TC))
                ST(CROW[:, tc * TC:(tc + 1) * TC], b1v(4608, 96, 512)[80:96, :],
                   rd=[bbce], wr=dbt(CROW, tc * TC, (tc + 1) * TC))
                for mc in range(16 if KSUB >= 1 else 0):
                    pk, pkb = ((P0, pb(P0)), (P1, pb(P1)))[mc % 2]
                    O("tensor", lambda e, mc=mc, pk=pk: e.matmul(
                        pk[:, :], b1v(2048 + mc * 128, 64, 128), b1v(4096, 64, 512),
                        start=True, stop=True), rd=[bdt, bdtb], wr=[pkb])
                    if KSUB < 20:
                        continue
                    if KSUB >= 21:
                        O("scalar", lambda e, pk=pk, mc=mc: e.activation(
                            FB[:, 0:512], pk[:, :], AF.Abs, bias=dtb_sb[:, mc:mc + 1]),
                          rd=[pkb, pers["dtb_sb"]], wr=[f_ab])
                    if KSUB >= 22:
                        O("scalar", lambda e: e.activation(FB[:, 512:1024], FB[:, 0:512],
                                                           AF.Exp, scale=epst[:, 2:3]),
                          rd=[f_ab, pers["epst"]], wr=[f_ln])
                    if KSUB >= 23:
                        O("scalar", lambda e: e.activation(FB[:, 512:1024], FB[:, 512:1024],
                                                           AF.Ln, bias=epst[:, 3:4]),
                          rd=[f_ln, pers["epst"]], wr=[f_ln])
                    if KSUB >= 24:
                        O("vector", lambda e, pk=pk, mc=mc: e.tensor_scalar(
                            FB[:, 1024:1536], pk[:, :], dtb_sb[:, mc:mc + 1], 0.0,
                            ALU.add, ALU.max), rd=[pkb, pers["dtb_sb"]], wr=[f_dl])
                    if KSUB >= 25:
                        O("vector", lambda e: e.tensor_tensor(
                            FB[:, 1024:1536], FB[:, 1024:1536], FB[:, 512:1024], ALU.add),
                          rd=[f_ln, f_dl], wr=[f_dl])
                    if KSUB < 30:
                        continue
                    LD(FB[:, 1536:2048],
                       U_f[mc * 128:(mc + 1) * 128, tc * TC:(tc + 1) * TC],
                       rd=dbt(U_f, tc * TC, (tc + 1) * TC), wr=[f_uf])
                    O("vector", lambda e: e.tensor_tensor(
                        b2v(16384, 128, 512), FB[:, 1024:1536], FB[:, 1536:2048],
                        ALU.mult), rd=[f_dl, f_uf], wr=[bduo])
                    ST(DLT[mc * 128:(mc + 1) * 128, tc * TC:(tc + 1) * TC],
                       FB[:, 1024:1536], rd=[f_dl], wr=dbt(DLT, tc * TC, (tc + 1) * TC))
                    ST(DU_b[mc * 128:(mc + 1) * 128, tc * TC:(tc + 1) * TC],
                       b2v(16384, 128, 512), rd=[bduo], wr=dbt(DU_b, tc * TC, (tc + 1) * TC))

        if upto >= 5:
            bB = AB1.take(0, 16384)
            bX = AB1.take(16384, 32768)
            bC = AB2.take(0, 16384)
            bH = AB2.take(16384, 32768)
            f_d = AFA.take(0, 1024)
            f_u = AFA.take(1024, 2048)
            f_g = AFA.take(2048, 3072)
            f_y = AFA.take(3072, 4096)
            for tb in range(NTB):
                for n in range(16):
                    ST(b1v(n * TB, 128, TB),
                       bass.AP(BROW.tensor, n * T + tb * TB, [[0, 128], [1, TB]]),
                       rd=dbt(BROW, tb * TB, (tb + 1) * TB), wr=[bB], q="scalar")
                    ST(b2v(n * TB, 128, TB),
                       bass.AP(CROW.tensor, n * T + tb * TB, [[0, 128], [1, TB]]),
                       rd=dbt(CROW, tb * TB, (tb + 1) * TB), wr=[bC], q="scalar")
                for dc in range(16):
                    LD(FA[:, 0:1024], DLT[dc * 128:(dc + 1) * 128, tb * TB:(tb + 1) * TB],
                       rd=dbt(DLT, tb * TB, (tb + 1) * TB), wr=[f_d])
                    LD(dub_sb[:], DU_b[dc * 128:(dc + 1) * 128, tb * TB:(tb + 1) * TB],
                       rd=dbt(DU_b, tb * TB, (tb + 1) * TB), wr=[pers["dub_sb"]])
                    LD(FA[:, 1024:2048], U_f[dc * 128:(dc + 1) * 128, tb * TB:(tb + 1) * TB],
                       rd=dbt(U_f, tb * TB, (tb + 1) * TB), wr=[f_u])
                    LD(FA[:, 2048:3072], G_f[dc * 128:(dc + 1) * 128, tb * TB:(tb + 1) * TB],
                       rd=dbt(G_f, tb * TB, (tb + 1) * TB), wr=[f_g])
                    O("vector", lambda e: e.tensor_tensor(
                        b1v(16384, 128, 16, TB, TB, 1),
                        bass.AP(dub_sb, 0, [[TB, 128], [0, 16], [1, TB]]),
                        b1v(0, 128, 16, TB, TB, 1), ALU.mult),
                      rd=[pers["dub_sb"], bB], wr=[bX])
                    for n in range(16):
                        sl = n % 2
                        ab = pers["abuf0"] if sl == 0 else pers["abuf1"]
                        O("scalar", lambda e, sl=sl, dc=dc, n=n: e.activation(
                            abuf[:, sl, :], FA[:, 0:1024], AF.Exp,
                            scale=a_sb[:, dc, n:n + 1]),
                          rd=[f_d, pers["a_sb"]], wr=[ab])
                        if tb == 0:
                            O("vector", lambda e, sl=sl, n=n: e.tensor_tensor_scan(
                                b2v(16384 + n * TB, 128, TB), abuf[:, sl, :],
                                b1v(16384 + n * TB, 128, TB), 0.0, ALU.mult, ALU.add),
                              rd=[ab, bX], wr=[bH])
                        else:
                            O("vector", lambda e, sl=sl, n=n, dc=dc: e.tensor_tensor_scan(
                                b2v(16384 + n * TB, 128, TB), abuf[:, sl, :],
                                b1v(16384 + n * TB, 128, TB),
                                carry[:, dc, n:n + 1], ALU.mult, ALU.add),
                              rd=[ab, bX, pers["carry"]], wr=[bH])
                    O("vector", lambda e: e.drain(), inc=False)
                    O("vector", lambda e, dc=dc: e.tensor_copy(
                        carry[:, dc, :],
                        bass.AP(B2, 16384 + TB - 1, [[32768, 128], [TB, 16]])),
                      rd=[bH], wr=[pers["carry"]])
                    O("vector", lambda e: e.tensor_tensor(
                        b1v(16384, 128, 16, TB, TB, 1), b2v(16384, 128, 16, TB, TB, 1),
                        b2v(0, 128, 16, TB, TB, 1), ALU.mult),
                      rd=[bH, bC], wr=[bX])
                    for n in range(16):
                        for q in range(2):
                            O("tensor", lambda e, n=n, q=q: e.matmul(
                                PY[:, q * 512:(q + 1) * 512], idb[:],
                                b1v(16384 + n * TB + q * 512, 128, 512),
                                start=(n == 0), stop=(n == 15)),
                              rd=[bX, pers["idb"]], wr=[pb(PY)],
                              inc=(n == 15 and q == 1))
                    for q in range(2):
                        O("vector", lambda e, q=q, dc=dc: e.scalar_tensor_tensor(
                            FA[:, 3072 + q * 512:3072 + (q + 1) * 512],
                            FA[:, 1024 + q * 512:1024 + (q + 1) * 512],
                            d_sb[:, dc:dc + 1], PY[:, q * 512:(q + 1) * 512],
                            ALU.mult, ALU.add),
                          rd=[f_u, pers["d_sb"], pb(PY)], wr=[f_y])
                        O("vector", lambda e, q=q: e.tensor_tensor(
                            b2v(16384 + q * 512, 128, 512),
                            FA[:, 3072 + q * 512:3072 + (q + 1) * 512],
                            FA[:, 2048 + q * 512:2048 + (q + 1) * 512], ALU.mult),
                          rd=[f_y, f_g], wr=[bH])
                    ST(YG_b[dc * 128:(dc + 1) * 128, tb * TB:(tb + 1) * TB],
                       b2v(16384, 128, 1024), rd=[bH], wr=dbt(YG_b, tb * TB, (tb + 1) * TB))

        if upto >= 6:
            bwo = AB1.take(0, 16384)
            fwl = AFA.take(0, 1024)
            for c in range(16):
                LD(FA[:, 0:1024], w_op[c * 128:(c + 1) * 128, :], wr=[fwl])
                O("vector", lambda e, c=c: e.tensor_copy(b1v(c * 1024, 128, 1024),
                                                         FA[:, 0:1024]),
                  rd=[fwl], wr=[bwo])
            byg = [AB2.take(s * 8192, (s + 1) * 8192) for s in range(2)]
            f_xn = AFA.take(0, 1024)
            f_x = AFA.take(1024, 2048)
            f_o = AFA.take(2048, 2560)
            for tc in range(NTC):
                sl = tc % 2
                for h in range(4):
                    LD(b2v(sl * 8192 + h * 2048, 128, 4, 512, 512, 1),
                       YG_b.rearrange("(a p) t -> p a t", p=128)[:, h * 4:(h + 1) * 4,
                                                                tc * TC:(tc + 1) * TC],
                       rd=dbt(YG_b, tc * TC, (tc + 1) * TC), wr=[byg[sl]])
                for m in range(8):
                    bk, bkb = [(P0, pb(P0)), (P1, pb(P1)), (P2, pb(P2)), (P3, pb(P3))][m % 4]
                    for k in range(16):
                        O("tensor", lambda e, m=m, k=k, bk=bk, sl=sl: e.matmul(
                            bk[:, :], b1v(k * 1024 + m * 128, 128, 128),
                            b2v(sl * 8192 + k * 512, 128, 512),
                            start=(k == 0), stop=(k == 15)),
                          rd=[bwo, byg[sl]], wr=[bkb], inc=(k == 15))
                    LD(FA[:, 0:512], XN_f[m * 128:(m + 1) * 128, tc * TC:(tc + 1) * TC],
                       rd=dbt(XN_f, tc * TC, (tc + 1) * TC), wr=[f_xn])
                    LD(FA[:, 1024:1536], x_in[m * 128:(m + 1) * 128, tc * TC:(tc + 1) * TC],
                       wr=[f_x])
                    O("vector", lambda e, bk=bk: e.tensor_tensor(
                        FA[:, 2048:2560], bk[:, :], FA[:, 0:512], ALU.add),
                      rd=[bkb, f_xn], wr=[f_o])
                    O("vector", lambda e: e.tensor_tensor(
                        FA[:, 2048:2560], FA[:, 2048:2560], FA[:, 1024:1536], ALU.add),
                      rd=[f_x, f_o], wr=[f_o])
                    ST(X1p[m * 128:(m + 1) * 128, tc * TC:(tc + 1) * TC], FA[:, 2048:2560],
                       rd=[f_o], wr=dbt(X1p, tc * TC, (tc + 1) * TC))

        if upto >= 7:
            bw = AB1.take(0, 32768)
            for c in range(8):
                LD(b1v(c * 4096, 128, 4096), BW1[c * 128:(c + 1) * 128, :],
                   rd=[db(BW1)], wr=[bw])
            bxn = [AB2.take(s * 4096, (s + 1) * 4096) for s in range(2)]
            bh1o = AB2.take(8192, 8704)
            banks = [(P0, pb(P0)), (P1, pb(P1)), (P2, pb(P2)), (P3, pb(P3))]
            for tc in range(NTC):
                sl = tc % 2
                for h in range(2):
                    LD(b2v(sl * 4096 + h * 2048, 128, 4, 512, 512, 1),
                       XN_b.rearrange("(a p) t -> p a t", p=128)[:, h * 4:(h + 1) * 4,
                                                                tc * TC:(tc + 1) * TC],
                       rd=dbt(XN_b, tc * TC, (tc + 1) * TC), wr=[bxn[sl]])
                for m in range(32):
                    bk, bkb = banks[m % 4]
                    for k in range(8):
                        O("tensor", lambda e, m=m, k=k, bk=bk, sl=sl: e.matmul(
                            bk[:, :], b1v(k * 4096 + m * 128, 128, 128),
                            b2v(sl * 4096 + k * 512, 128, 512),
                            start=(k == 0), stop=(k == 7)),
                          rd=[bw, bxn[sl]], wr=[bkb], inc=(k == 7))
                    O("scalar", lambda e, bk=bk, m=m: e.activation(
                        b2v(8192, 128, 512), bk[:, :], AF.Gelu, bias=b1_sb[:, m:m + 1]),
                      rd=[bkb, pers["b1_sb"]], wr=[bh1o])
                    ST(H1[m * 128:(m + 1) * 128, tc * TC:(tc + 1) * TC],
                       b2v(8192, 128, 512), rd=[bh1o], wr=dbt(H1, tc * TC, (tc + 1) * TC))

        if upto >= 8:
            bw = AB1.take(0, 32768)
            for c in range(32):
                LD(b1v(c * 1024, 128, 1024), BW2[c * 128:(c + 1) * 128, :],
                   rd=[db(BW2)], wr=[bw])
            bh1 = AB2.take(0, 16384)
            f_xp = AFA.take(0, 512)
            f_o = AFA.take(512, 1024)
            for tc in range(NTC):
                for h in range(8):
                    LD(b2v(h * 2048, 128, 4, 512, 512, 1),
                       H1.rearrange("(a p) t -> p a t", p=128)[:, h * 4:(h + 1) * 4,
                                                              tc * TC:(tc + 1) * TC],
                       rd=dbt(H1, tc * TC, (tc + 1) * TC), wr=[bh1])
                for m in range(8):
                    bk, bkb = [(P0, pb(P0)), (P1, pb(P1)), (P2, pb(P2)), (P3, pb(P3))][m % 4]
                    for k in range(32):
                        O("tensor", lambda e, m=m, k=k, bk=bk: e.matmul(
                            bk[:, :], b1v(k * 1024 + m * 128, 128, 128),
                            b2v(k * 512, 128, 512),
                            start=(k == 0), stop=(k == 31)),
                          rd=[bw, bh1], wr=[bkb], inc=(k == 31))
                    LD(FA[:, 0:512], X1p[m * 128:(m + 1) * 128, tc * TC:(tc + 1) * TC],
                       rd=dbt(X1p, tc * TC, (tc + 1) * TC), wr=[f_xp])
                    O("vector", lambda e, bk=bk, m=m: e.scalar_tensor_tensor(
                        FA[:, 512:1024], bk[:, :], b2_sb[:, m:m + 1], FA[:, 0:512],
                        ALU.add, ALU.add), rd=[bkb, pers["b2_sb"], f_xp], wr=[f_o])
                    ST(X1[m * 128:(m + 1) * 128, tc * TC:(tc + 1) * TC], FA[:, 512:1024],
                       rd=[f_o], wr=dbt(X1, tc * TC, (tc + 1) * TC))

        if upto >= 9:
            layer_norm_phase(X1, X1, XN1_f, XN1_b, None, False)
            fxt = AFA.take(0, 4096)
            for dc in range(8):
                LD(FA[:, 0:4096], XN1_f[dc * 128:(dc + 1) * 128, :],
                   rd=dbt(XN1_f, 0, T), wr=[fxt])
                O("vector", lambda e, dc=dc: e.tensor_reduce(
                    gvec[:, dc:dc + 1], FA[:, 0:4096], AX.X, ALU.add),
                  rd=[fxt], wr=[pers["gvec"]])
            O("vector", lambda e: e.tensor_scalar(gvec[:], gvec[:], 1.0 / T, None,
                                                  ALU.mult),
              rd=[pers["gvec"]], wr=[pers["gvec"]])
            O("vector", lambda e: e.tensor_copy(gvec_b[:], gvec[:]),
              rd=[pers["gvec"]], wr=[pers["gvec_b"]])
            bgw = AB1.take(0, 4096)
            bg1 = AB1.take(4096, 4100)
            bg2 = AB1.take(4100, 4102)
            fwl = AFA.take(0, 512)
            for c in range(8):
                LD(FA[:, 0:512], gcw1[c * 128:(c + 1) * 128, :], wr=[fwl])
                O("vector", lambda e, c=c: e.tensor_copy(b1v(c * 512, 128, 512),
                                                         FA[:, 0:512]),
                  rd=[fwl], wr=[bgw])
            for m in range(4):
                for k in range(8):
                    O("tensor", lambda e, m=m, k=k: e.matmul(
                        P4[:, m:m + 1], b1v(k * 512 + m * 128, 128, 128),
                        gvec_b[:, k:k + 1], start=(k == 0), stop=(k == 7)),
                      rd=[bgw, pers["gvec_b"]], wr=[pb(P4)], inc=(k == 7))
            for m in range(4):
                O("scalar", lambda e, m=m: e.activation(
                    b1v(4096 + m, 128, 1), P4[:, m:m + 1], AF.Gelu,
                    bias=gcb1_sb[:, m:m + 1]), rd=[pb(P4), pers["gcb1_sb"]], wr=[bg1])
            bgw2 = AB1.take(8192, 9216)
            for c in range(4):
                LD(FA[:, 0:256], gcw2[c * 128:(c + 1) * 128, :], wr=[fwl])
                O("vector", lambda e, c=c: e.tensor_copy(b1v(8192 + c * 256, 128, 256),
                                                         FA[:, 0:256]),
                  rd=[fwl], wr=[bgw2])
            for m in range(2):
                for k in range(4):
                    O("tensor", lambda e, m=m, k=k: e.matmul(
                        P4[:, 8 + m:9 + m], b1v(8192 + k * 256 + m * 128, 128, 128),
                        b1v(4096 + k, 128, 1), start=(k == 0), stop=(k == 3)),
                      rd=[bgw2, bg1], wr=[pb(P4)], inc=(k == 3))
            for m in range(2):
                O("scalar", lambda e, m=m: e.activation(
                    b1v(4100 + m, 128, 1), P4[:, 8 + m:9 + m], AF.Gelu,
                    bias=gcb2_sb[:, m:m + 1]), rd=[pb(P4), pers["gcb2_sb"]], wr=[bg2])
            for k in range(2):
                O("tensor", lambda e, k=k: e.matmul(
                    P4[0:1, 12:13], gc3_sb[:, k:k + 1], b1v(4100 + k, 128, 1),
                    start=(k == 0), stop=(k == 1)),
                  rd=[pers["gc3_sb"], bg2], wr=[pb(P4)], inc=(k == 1))
            O("scalar", lambda e: e.activation(sc_sb[0:1, 0:1], P4[0:1, 12:13],
                                               AF.Sigmoid, bias=sc_sb[0:1, 5:6]),
              rd=[pb(P4), pers["sc_sb"]], wr=[pers["sc_sb"]])
            O("vector", lambda e: e.tensor_tensor(sc_sb[0:1, 2:3], sc_sb[0:1, 1:2],
                                                  sc_sb[0:1, 0:1], ALU.mult),
              rd=[pers["sc_sb"]], wr=[pers["sc_sb"]])
            O("vector", lambda e: e.tensor_scalar(sc_sb[0:1, 2:3], sc_sb[0:1, 2:3],
                                                  1.4, 0.1, ALU.mult, ALU.add),
              rd=[pers["sc_sb"]], wr=[pers["sc_sb"]])
            O("vector", lambda e: e.tensor_tensor(sc_sb[0:1, 2:3], sc_sb[0:1, 2:3],
                                                  sc_sb[0:1, 1:2], ALU.subtract),
              rd=[pers["sc_sb"]], wr=[pers["sc_sb"]])
            O("vector", lambda e: e.tensor_scalar(sc_sb[0:1, 3:4], sc_sb[0:1, 1:2],
                                                  0.6, None, ALU.mult),
              rd=[pers["sc_sb"]], wr=[pers["sc_sb"]])

        for zt, hs in ((ZBB, ZBB_HS), (ZPB, ZPB_HS)):
            tot = NH * hs
            CH = 128 * 512
            nch = (tot + CH - 1) // CH
            for i in range(nch):
                sz = min(CH, tot - i * CH)
                rows = sz // 512
                if rows:
                    ST(bass.AP(zt.tensor, i * CH, [[512, rows], [1, 512]]),
                       bass.AP(zbig, 0, [[512, 1], [0, rows], [1, 512]]),
                       rd=[pers["zbig"]], wr=[db(zt)])
                if sz % 512:
                    ST(bass.AP(zt.tensor, i * CH + rows * 512, [[1, 1], [1, sz % 512]]),
                       zbig[0:1, 0:sz % 512], rd=[pers["zbig"]], wr=[db(zt)])

        if upto >= 10:
            bcum = AB1.take(0, 4096)
            bone4 = AB1.take(4096, 8192)
            bcit = AB1.take(8192, 8320)
            bwt = AB1.take(8320, 8384)
            brib = AB1.take(8448, 12544)
            bci = AB1.take(12544, 16640)
            bwe = AB2.take(0, 16384)
            f_sig = AFB.take(0, 1024)
            f_sd = AFB.take(1024, 2048)
            f_et = AFB.take(2048, 3072)
            f_rw = AFB.take(4096, 4608)
            O("gpsimd", lambda e: e.memset(b1v(4096, 128, 4096), 1.0), wr=[bone4])
            O("gpsimd", lambda e: e.memset(FB[:, 2048:3072], 0.0), wr=[f_et])
            fri = AFA.take(0, 4096)
            LD(FA[0:1, 0:4096], RIROW, rd=[db(RIROW)], wr=[fri])
            for q8 in range(8):
                O("tensor", lambda e, q8=q8: e.matmul(
                    PY[:, 0:512], ones1[:], FA[0:1, q8 * 512:(q8 + 1) * 512],
                    start=True, stop=True),
                  rd=[fri, pers["ones1"]], wr=[pb(PY)])
                O("vector", lambda e, q8=q8: e.tensor_copy(
                    b1v(8448 + q8 * 512, 128, 512), PY[:, 0:512]),
                  rd=[pb(PY)], wr=[brib])
            f_x1 = AFA.take(0, 4096)
            for dc in range(8):
                po = (dc % 2) * 64
                LD(FA[:, 0:4096], XN1_f[dc * 128:(dc + 1) * 128, :],
                   rd=dbt(XN1_f, 0, T), wr=[f_x1])
                O("vector", lambda e: e.tensor_tensor_scan(
                    b1v(0, 128, 4096), b1v(4096, 128, 4096), FA[:, 0:4096],
                    0.0, ALU.mult, ALU.add), rd=[bone4, f_x1], wr=[bcum])
                O("vector", lambda e: e.drain(), inc=False)
                O("vector", lambda e: e.tensor_tensor(
                    b1v(12544, 128, 4096), b1v(0, 128, 4096), b1v(8448, 128, 4096),
                    ALU.mult), rd=[bcum, brib], wr=[bci])
                if KSUB < 41:
                    continue
                for tt in range(32 if KSUB >= 42 else 0):
                    s8 = tt % 8
                    slot = TR(lambda e, tt=tt, s8=s8: e.transpose(
                        PBm[:, s8 * 128:(s8 + 1) * 128], b1v(12544 + tt * 128, 128, 128),
                        idb[:]), rd=[bci, pers["idb"]], wr=[pb_half[s8 // 4]])
                    slot[0] = O("vector", lambda e, s8=s8: e.tensor_copy(
                        b1v(8192, 128, 128), PBm[:, s8 * 128:(s8 + 1) * 128]),
                      rd=[pb_half[s8 // 4]], wr=[bcit])
                    ST(CIT.rearrange("(a p) d -> a p d", p=128)[tt, :, dc * 128:(dc + 1) * 128],
                       b1v(8192, 128, 128), rd=[bcit], wr=dbt(CIT, tt * 128, tt * 128 + 128))
                for th in range(4 if KSUB >= 43 else 0):
                    for q in range(2):
                        col = th * 1024 + q * 512
                        O("tensor", lambda e, col=col, po=po: e.matmul(
                            P0[po:po + 63, :], wd_sb[:], b1v(12544 + col, 128, 512),
                            start=True, stop=True),
                          rd=[bci, pers["wd_sb"]], wr=[pb(P0)])
                        O("tensor", lambda e, col=col, po=po: e.matmul(
                            P1[po:po + 63, :], wg_sb[:], b1v(12544 + col, 128, 512),
                            start=True, stop=True),
                          rd=[bci, pers["wg_sb"]], wr=[pb(P1)])
                        O("scalar", lambda e, q=q, po=po: e.activation(
                            FB[po:po + 63, q * 512:(q + 1) * 512], P1[po:po + 63, :],
                            AF.Sigmoid, bias=bpe_g[po:po + 63, 0:1]),
                          rd=[pb(P1), pers["bpe_g"]], wr=[f_sig])
                        O("vector", lambda e, q=q, po=po: e.scalar_tensor_tensor(
                            FB[po:po + 63, 1024 + q * 512:1024 + (q + 1) * 512],
                            P0[po:po + 63, :], bpe_d[po:po + 63, 0:1],
                            FB[po:po + 63, q * 512:(q + 1) * 512], ALU.add, ALU.mult),
                          rd=[pb(P0), pers["bpe_d"], f_sig], wr=[f_sd])
                    O("scalar", lambda e, th=th, po=po, dc=dc: e.activation(
                        FB[po:po + 63, 2048:3072], FB[po:po + 63, 1024:2048], AF.Exp,
                        bias=stt_sb[po:po + 63, dc:dc + 1]),
                      rd=[f_sd, pers["stt_sb"]], wr=[f_et])
                    for q in range(2):
                        O("tensor", lambda e, q=q, dc=dc, po=po: e.matmul(
                            P3[0:2, :], ob124[:, (dc % 2) * 2:(dc % 2) * 2 + 2],
                            FB[:, 2048 + q * 512:2048 + (q + 1) * 512],
                            start=True, stop=True),
                          rd=[f_et, pers["ob124"]], wr=[pb(P3)])
                        O("vector", lambda e: e.tensor_copy(FB[0:2, 4096:4608], P3[0:2, :]),
                          rd=[pb(P3)], wr=[f_rw])
                        O("vector", lambda e: e.reciprocal(FB[0:2, 4096:4608],
                                                           FB[0:2, 4096:4608]),
                          rd=[f_rw], wr=[f_rw])
                        obt = ob2t_lo if dc % 2 == 0 else ob2t_hi
                        obtb = pers["ob2t_lo"] if dc % 2 == 0 else pers["ob2t_hi"]
                        O("tensor", lambda e, obt=obt: e.matmul(
                            P2[:, :], obt[:], FB[0:2, 4096:4608], start=True, stop=True),
                          rd=[f_rw, obtb], wr=[pb(P2)])
                        if KSUB >= 44:
                            O("vector", lambda e, q=q, th=th, dc=dc, po=po: e.tensor_tensor(
                                bass.AP(B2, (dc // 2) * 4096 + th * 1024 + q * 512,
                                        [[32768, 128], [1, 512]])[po:po + 63, :],
                                FB[po:po + 63, 2048 + q * 512:2048 + (q + 1) * 512],
                                P2[po:po + 63, :], ALU.mult),
                              rd=[f_et, pb(P2)], wr=[bwe])
                if KSUB < 44:
                    continue
                O("vector", lambda e, dc=dc, po=po: e.tensor_tensor(
                    bass.AP(B2, (dc // 2) * 4096, [[32768, 128], [1, 30]])[po:po + 63, :],
                    bass.AP(B2, (dc // 2) * 4096, [[32768, 128], [1, 30]])[po:po + 63, :],
                    wmask[po:po + 63, :], ALU.mult),
                  rd=[pers["wmask"], bwe], wr=[bwe])
            for dc in ([0, 2, 4, 6, 1, 3, 5, 7] if KSUB >= 45 else []):
                po = (dc % 2) * 64
                for tt in range(32):
                    s8 = tt % 8
                    slot = TR(lambda e, dc=dc, tt=tt, s8=s8, po=po: e.transpose(
                        PBm[0:128, s8 * 128:s8 * 128 + 63],
                        bass.AP(B2, (dc // 2) * 4096 + tt * 128, [[32768, 128], [1, 128]])
                        [po:po + 63, :], idb[po:po + 63, po:po + 63]),
                      rd=[bwe, pers["idb"]], wr=[pb_half[s8 // 4]], base=po)
                    slot[0] = O("vector", lambda e, s8=s8: e.tensor_copy(
                        b1v(8320, 128, 63), PBm[:, s8 * 128:s8 * 128 + 63]),
                      rd=[pb_half[s8 // 4]], wr=[bwt])
                    for hh in range(2 if KSUB >= 47 else 0):
                        h = 2 * dc + hh
                        ST(bass.AP(ZBB.tensor, h * ZBB_HS + 64 + tt * ZBB_BS - 30,
                                   [[161, 128], [1, 31]]),
                           b1v(8320 + hh * 32, 128, 31), rd=[bwt], wr=[db(ZBB)])
                        if KSUB >= 48:
                            ST(bass.AP(ZPB.tensor, h * ZPB_HS + 64 + tt * ZPB_BS + 98,
                                       [[258, 128], [1, 31]]),
                               b1v(8320 + hh * 32, 128, 31), rd=[bwt], wr=[db(ZPB)])

        if upto >= 11:
            bcv = AB1.take(0, 32768)
            bct = [AB2.take(16384 + s * 1024, 16384 + (s + 1) * 1024) for s in range(2)]
            bzb = AB2.take(18432, 18560)
            bzp = AB2.take(18560, 18688)
            bwbb = AB2.take(18688, 18816)
            bwpb = AB2.take(18816, 18944)
            bwfc = AB2.take(18944, 27136)
            for b in range(32):
                sl = b % 2
                LD(b2v(16384 + sl * 1024, 128, 1024),
                   CIT[b * 128:(b + 1) * 128, :], rd=dbt(CIT, b * 128, b * 128 + 128), wr=[bct[sl]])
                for h in range(NH):
                    pv, pvb = ((P0, pb(P0)), (P1, pb(P1)))[h % 2]
                    pvo = (h % 2) * 64
                    LD(b2v(18432, 128, 128),
                       bass.AP(ZBB.tensor, h * ZBB_HS + 64 + b * ZBB_BS,
                               [[160, 128], [1, 128]]), rd=[db(ZBB)], wr=[bzb])
                    s8 = (b * NH + h) % 4
                    slot = TR(lambda e, s8=s8: e.transpose(
                        PBm[:, s8 * 256:s8 * 256 + 128], b2v(18432, 128, 128), idb[:]),
                      rd=[bzb, pers["idb"]], wr=[pb_half[s8 // 2]])
                    slot[0] = O("vector", lambda e, s8=s8: e.tensor_copy(
                        b2v(18688, 128, 128), PBm[:, s8 * 256:s8 * 256 + 128]),
                      rd=[pb_half[s8 // 2]], wr=[bwbb])
                    if b > 0:
                        LD(b2v(18560, 128, 128),
                           bass.AP(ZPB.tensor, h * ZPB_HS + 64 + b * ZPB_BS,
                                   [[257, 128], [1, 128]]), rd=[db(ZPB)], wr=[bzp])
                        slot2 = TR(lambda e, s8=s8: e.transpose(
                            PBm[:, s8 * 256 + 128:s8 * 256 + 256], b2v(18560, 128, 128),
                            idb[:]), rd=[bzp, pers["idb"]], wr=[pb_half[s8 // 2]])
                        slot2[0] = O("vector", lambda e, s8=s8: e.tensor_copy(
                            b2v(18816, 128, 128), PBm[:, s8 * 256 + 128:s8 * 256 + 256]),
                          rd=[pb_half[s8 // 2]], wr=[bwpb])
                    O("tensor", lambda e, h=h, sl=sl, pv=pv, pvo=pvo, b=b: e.matmul(
                        pv[pvo:pvo + 64, 0:128], b2v(16384 + sl * 1024 + h * 64, 128, 64),
                        b2v(18688, 128, 128), start=True, stop=(b == 0)),
                      rd=[bct[sl], bwbb], wr=[pvb], inc=(b == 0))
                    if b > 0:
                        O("tensor", lambda e, h=h, sl=sl, pv=pv, pvo=pvo: e.matmul(
                            pv[pvo:pvo + 64, 0:128], b2v(16384 + (1 - sl) * 1024 + h * 64, 128, 64),
                            b2v(18816, 128, 128), start=False, stop=True),
                          rd=[bct[1 - sl], bwpb], wr=[pvb])
                    O("vector", lambda e, h=h, b=b, pv=pv, pvo=pvo: e.tensor_copy(
                        b1v((h // 2) * 4096 + b * 128, 128, 128)[pvo:pvo + 64, :],
                        pv[pvo:pvo + 64, 0:128]),
                      rd=[pvb], wr=[bcv])
            fwl = AFA.take(0, 1024)
            for c in range(8):
                LD(FA[:, 0:1024], wfct[c * 128:(c + 1) * 128, :], wr=[fwl])
                O("vector", lambda e, c=c: e.tensor_copy(
                    b2v(18944 + c * 1024, 128, 1024), FA[:, 0:1024]),
                  rd=[fwl], wr=[bwfc])
            f_v = AFA.take(0, 4096)
            f_xn1 = AFB.take(0, 512)
            f_x1t = AFB.take(512, 1024)
            f_sq = AFB.take(1024, 1536)
            f_vx = AFB.take(1536, 2048)
            rows = AFB.take(2048, 4096)
            for tc in range(NTC):
                for m in range(8):
                    bk, bkb = ((P2, pb(P2)), (P3, pb(P3)))[m % 2]
                    for k in range(8):
                        O("tensor", lambda e, m=m, k=k, bk=bk, tc=tc: e.matmul(
                            bk[:, :], b2v(18944 + k * 1024 + m * 128, 128, 128),
                            b1v(k * 4096 + tc * 512, 128, 512),
                            start=(k == 0), stop=(k == 7)),
                          rd=[bwfc, bcv], wr=[bkb], inc=(k == 7))
                    LD(FB[:, 0:512], XN1_f[m * 128:(m + 1) * 128, tc * TC:(tc + 1) * TC],
                       rd=dbt(XN1_f, tc * TC, (tc + 1) * TC), wr=[f_xn1])
                    O("vector", lambda e, bk=bk, m=m: e.tensor_tensor(
                        FA[:, m * 512:(m + 1) * 512], bk[:, :], FB[:, 0:512], ALU.add),
                      rd=[bkb, f_xn1], wr=[f_v])
                for m in range(8):
                    O("scalar", lambda e, m=m: e.activation(
                        FB[:, 1024:1536], FA[:, m * 512:(m + 1) * 512], AF.Square),
                      rd=[f_v], wr=[f_sq])
                    O("tensor", lambda e, m=m: e.matmul(
                        P0[0:1, :], onesv[:], FA[:, m * 512:(m + 1) * 512],
                        start=(m == 0), stop=(m == 7)),
                      rd=[f_v, pers["onesv"]], wr=[pb(P0)], inc=(m == 7))
                    O("tensor", lambda e, m=m: e.matmul(
                        P1[0:1, :], onesv[:], FB[:, 1024:1536],
                        start=(m == 0), stop=(m == 7)),
                      rd=[f_sq, pers["onesv"]], wr=[pb(P1)], inc=(m == 7))
                mu = FB[0:1, 2048:2560]
                var = FB[0:1, 2560:3072]
                sd = FB[0:1, 3072:3584]
                rr = FB[0:1, 3584:4096]
                O("vector", lambda e: e.tensor_copy(mu, P0[0:1, :]), rd=[pb(P0)], wr=[rows])
                O("vector", lambda e: e.tensor_tensor(var, mu, mu, ALU.mult),
                  rd=[rows], wr=[rows])
                O("vector", lambda e: e.tensor_tensor(var, P1[0:1, :], var, ALU.subtract),
                  rd=[pb(P1), rows], wr=[rows])
                O("scalar", lambda e: e.activation(sd, var, AF.Sqrt, bias=epst[0:1, 1:2]),
                  rd=[rows, pers["epst"]], wr=[rows])
                O("vector", lambda e: e.reciprocal(rr, sd), rd=[rows], wr=[rows])
                O("tensor", lambda e: e.matmul(PY[:, 0:512], ones1[:], mu,
                                               start=True, stop=True),
                  rd=[rows, pers["ones1"]], wr=[pb(PY)])
                O("tensor", lambda e: e.matmul(PY[:, 512:1024], ones1[:], rr,
                                               start=True, stop=True),
                  rd=[rows, pers["ones1"]], wr=[pb(PY)])
                for m in range(8):
                    LD(FB[:, 512:1024], X1[m * 128:(m + 1) * 128, tc * TC:(tc + 1) * TC],
                       rd=dbt(X1, tc * TC, (tc + 1) * TC), wr=[f_x1t])
                    O("vector", lambda e, m=m: e.tensor_tensor(
                        FA[:, m * 512:(m + 1) * 512], FA[:, m * 512:(m + 1) * 512],
                        PY[:, 0:512], ALU.subtract), rd=[pb(PY), f_v], wr=[f_v])
                    O("vector", lambda e, m=m: e.scalar_tensor_tensor(
                        FA[:, m * 512:(m + 1) * 512], FA[:, m * 512:(m + 1) * 512],
                        g_col[:, m:m + 1], PY[:, 512:1024], ALU.mult, ALU.mult),
                      rd=[pb(PY), pers["g_col"], f_v], wr=[f_v])
                    O("vector", lambda e, m=m: e.scalar_tensor_tensor(
                        FB[:, 1536:2048], FA[:, m * 512:(m + 1) * 512],
                        bt_col[:, m:m + 1], FB[:, 512:1024], ALU.add, ALU.add),
                      rd=[f_v, pers["bt_col"], f_x1t], wr=[f_vx])
                    ST(VLX[m * 128:(m + 1) * 128, tc * TC:(tc + 1) * TC],
                       FB[:, 1536:2048], rd=[f_vx], wr=dbt(VLX, tc * TC, (tc + 1) * TC))

        if upto >= 12:
            bcn1 = AB1.take(0, 8192)
            bcn2 = AB1.take(8192, 12288)
            bcn3 = AB1.take(12288, 13312)
            bob = AB1.take(13312, 17408)
            bh1c = AB1.take(17408, 21504)
            bh2c = AB1.take(21504, 23552)
            bh3c = AB1.take(23552, 24576)
            fwl = AFA.take(0, 1024)
            for c in range(8):
                LD(FA[:, 0:1024], cnw1[c * 128:(c + 1) * 128, :], wr=[fwl])
                O("vector", lambda e, c=c: e.tensor_copy(b1v(c * 1024, 128, 1024),
                                                         FA[:, 0:1024]),
                  rd=[fwl], wr=[bcn1])
            for c in range(8):
                LD(FA[:, 0:512], cnw2[c * 128:(c + 1) * 128, :], wr=[fwl])
                O("vector", lambda e, c=c: e.tensor_copy(b1v(8192 + c * 512, 128, 512),
                                                         FA[:, 0:512]),
                  rd=[fwl], wr=[bcn2])
            for c in range(4):
                LD(FA[:, 0:256], cnw3[c * 128:(c + 1) * 128, :], wr=[fwl])
                O("vector", lambda e, c=c: e.tensor_copy(b1v(12288 + c * 256, 128, 256),
                                                         FA[:, 0:256]),
                  rd=[fwl], wr=[bcn3])
            f_oc = AFA.take(1024, 3072)
            f_t1 = AFA.take(3072, 3584)
            f_vlx = AFA.take(3584, 4096)
            f_tf = AFB.take(0, 512)
            twist_src = [4, 5, 6, 7, 0, 1, 2, 3]
            twist_sgn = [1.0, 1.0, -1.0, -1.0, -1.0, -1.0, 1.0, 1.0]
            for cyc in range(3):
                cur_f = (XN1_f, OUTCA, OUTCB)[cyc]
                cur_b = (XN1_b, OUTBA, OUTBB)[cyc]
                nxt_f = (OUTCA, OUTCB, None)[cyc]
                nxt_b = (OUTBA, OUTBB, None)[cyc]
                for tc in range(NTC):
                    for h in range(2):
                        LD(b1v(13312 + h * 2048, 128, 4, 512, 512, 1),
                           cur_b.rearrange("(a p) t -> p a t", p=128)
                           [:, h * 4:(h + 1) * 4, tc * TC:(tc + 1) * TC],
                           rd=dbt(cur_b, tc * TC, (tc + 1) * TC), wr=[bob])
                    for m in range(8):
                        bk, bkb = [(P0, pb(P0)), (P1, pb(P1)), (P2, pb(P2)), (P3, pb(P3))][m % 4]
                        for k in range(8):
                            O("tensor", lambda e, m=m, k=k, bk=bk: e.matmul(
                                bk[:, :], b1v(k * 1024 + m * 128, 128, 128),
                                b1v(13312 + k * 512, 128, 512),
                                start=(k == 0), stop=(k == 7)),
                              rd=[bcn1, bob], wr=[bkb], inc=(k == 7))
                        O("scalar", lambda e, bk=bk, m=m: e.activation(
                            b1v(17408 + m * 512, 128, 512), bk[:, :], AF.Gelu,
                            bias=cnb1_sb[:, m:m + 1]),
                          rd=[bkb, pers["cnb1_sb"]], wr=[bh1c])
                    for m in range(4):
                        bk, bkb = ((P2, pb(P2)), (P3, pb(P3)))[m % 2]
                        for k in range(8):
                            O("tensor", lambda e, m=m, k=k, bk=bk: e.matmul(
                                bk[:, :], b1v(8192 + k * 512 + m * 128, 128, 128),
                                b1v(17408 + k * 512, 128, 512),
                                start=(k == 0), stop=(k == 7)),
                              rd=[bcn2, bh1c], wr=[bkb], inc=(k == 7))
                        O("scalar", lambda e, bk=bk, m=m: e.activation(
                            b1v(21504 + m * 512, 128, 512), bk[:, :], AF.Gelu,
                            bias=cnb2_sb[:, m:m + 1]),
                          rd=[bkb, pers["cnb2_sb"]], wr=[bh2c])
                    for m in range(2):
                        bk, bkb = ((P0, pb(P0)), (P1, pb(P1)))[m % 2]
                        for k in range(4):
                            O("tensor", lambda e, m=m, k=k, bk=bk: e.matmul(
                                bk[:, :], b1v(12288 + k * 256 + m * 128, 128, 128),
                                b1v(21504 + k * 512, 128, 512),
                                start=(k == 0), stop=(k == 3)),
                              rd=[bcn3, bh2c], wr=[bkb], inc=(k == 3))
                        O("scalar", lambda e, bk=bk, m=m: e.activation(
                            b1v(23552 + m * 512, 128, 512), bk[:, :], AF.Gelu,
                            bias=cnb3_sb[:, m:m + 1]),
                          rd=[bkb, pers["cnb3_sb"]], wr=[bh3c])
                    for k in range(2):
                        O("tensor", lambda e, k=k: e.matmul(
                            P2[0:1, :], cn4_sb[:, k:k + 1], b1v(23552 + k * 512, 128, 512),
                            start=(k == 0), stop=(k == 1)),
                          rd=[pers["cn4_sb"], bh3c], wr=[pb(P2)], inc=(k == 1))
                    O("scalar", lambda e: e.activation(FB[0:1, 0:512], P2[0:1, :],
                                                       AF.Sigmoid, bias=sc_sb[0:1, 4:5]),
                      rd=[pb(P2), pers["sc_sb"]], wr=[f_tf])
                    O("vector", lambda e: e.tensor_scalar(
                        FB[0:1, 0:512], FB[0:1, 0:512], sc_sb[0:1, 3:4],
                        sc_sb[0:1, 2:3], ALU.mult, ALU.add),
                      rd=[pers["sc_sb"], f_tf], wr=[f_tf])
                    O("tensor", lambda e: e.matmul(P4[:, :], ones1[:], FB[0:1, 0:512],
                                                   start=True, stop=True),
                      rd=[f_tf, pers["ones1"]], wr=[pb(P4)])
                    for dc in range(8):
                        LD(FA[:, 1024:1536],
                           cur_f[twist_src[dc] * 128:(twist_src[dc] + 1) * 128,
                                 tc * TC:(tc + 1) * TC], rd=dbt(cur_f, tc * TC, (tc + 1) * TC), wr=[f_oc])
                        LD(FA[:, 1536:2048],
                           cur_f[dc * 128:(dc + 1) * 128, tc * TC:(tc + 1) * TC],
                           rd=dbt(cur_f, tc * TC, (tc + 1) * TC), wr=[f_oc])
                        O("vector", lambda e, dc=dc: e.scalar_tensor_tensor(
                            FA[:, 3072:3584], FA[:, 1024:1536], twist_sgn[dc],
                            P4[:, :], ALU.mult, ALU.mult),
                          rd=[f_oc, pb(P4)], wr=[f_t1])
                        O("vector", lambda e: e.tensor_tensor(
                            FA[:, 3072:3584], FA[:, 3072:3584], FA[:, 1536:2048],
                            ALU.add), rd=[f_oc, f_t1], wr=[f_t1])
                        if cyc < 2:
                            ST(nxt_f[dc * 128:(dc + 1) * 128, tc * TC:(tc + 1) * TC],
                               FA[:, 3072:3584], rd=[f_t1], wr=dbt(nxt_f, tc * TC, (tc + 1) * TC))
                            O("scalar", lambda e: e.activation(
                                b1v(13312, 128, 512), FA[:, 3072:3584], AF.Copy),
                              rd=[f_t1], wr=[bob])
                            ST(nxt_b[dc * 128:(dc + 1) * 128, tc * TC:(tc + 1) * TC],
                               b1v(13312, 128, 512), rd=[bob], wr=dbt(nxt_b, tc * TC, (tc + 1) * TC))
                        else:
                            LD(FA[:, 3584:4096],
                               VLX[dc * 128:(dc + 1) * 128, tc * TC:(tc + 1) * TC],
                               rd=dbt(VLX, tc * TC, (tc + 1) * TC), wr=[f_vlx])
                            O("vector", lambda e: e.tensor_tensor(
                                FA[:, 3072:3584], FA[:, 3072:3584], FA[:, 3584:4096],
                                ALU.add), rd=[f_vlx, f_t1], wr=[f_t1])
                            ST(out_ext[dc * 128:(dc + 1) * 128, tc * TC:(tc + 1) * TC],
                               FA[:, 3072:3584], rd=[f_t1], wr=dbt(out_ext, tc * TC, (tc + 1) * TC))

        for sm_ in DSEMS:
            if P.cnt[sm_] > 0:
                P.pending["sync"][(sm_, P.seg[sm_])] = P.cnt[sm_]
        O("sync", lambda e: e.nop())

        waited_vals = {}
        for e_ in ENGS:
            for (w_, fn_, sm_, n_, sg_, v_) in P.ops[e_]:
                for ((psm, pseg), pv) in w_:
                    if psm in CSEMS:
                        waited_vals.setdefault((psm, pseg), set()).add(pv)
        import bisect as _bisect
        wsorted = {k: sorted(v) for k, v in waited_vals.items()}

        def remap(sm_, sg_, v_):
            lst = wsorted.get((sm_, sg_), [])
            return _bisect.bisect_right(lst, v_)

        for e_ in ENGS:
            newops = []
            for (w_, fn_, sm_, n_, sg_, v_) in P.ops[e_]:
                w2 = [((psm, pseg), remap(psm, pseg, pv) if psm in CSEMS else pv)
                      for ((psm, pseg), pv) in w_]
                if sm_ in CSEMS:
                    n_ = 1 if v_ in waited_vals.get((sm_, sg_), ()) else 0
                    v_ = remap(sm_, sg_, v_)
                newops.append((w2, fn_, sm_, n_, sg_, v_))
            P.ops[e_] = newops

        def mk_section(ename):
            def body(eng):
                for (waits, fn, sm, n, seg, v_) in P.ops[ename]:
                    for ((psm, pseg), v) in waits:
                        eng.wait_ge(sems[psm][pseg], v)
                    instr = fn(eng)
                    if n:
                        instr.then_inc(sems[sm][seg], n)
            return body

        for ename in ENGS:
            getattr(block, ename)(mk_section(ename))

    return nc


_NC_CACHE = {}


def _wpe_blockdiag(wpe, c0):
    wpe = np.asarray(wpe)
    out = np.zeros((128, 63), np.float32)
    out[0:64, 0:31] = wpe[:, c0:c0 + 31]
    out[64:128, 32:63] = wpe[:, c0:c0 + 31]
    return out


def _stack32(v31):
    out = np.zeros((128, 1), np.float32)
    for j in range(4):
        out[j * 32:j * 32 + 31, 0] = v31
    return out


def _stt_layout(stt):
    stt = np.asarray(stt).reshape(NH, KW)
    out = np.zeros((128, 8), np.float32)
    for dc in range(8):
        for hh in range(2):
            base = (dc % 2) * 64 + hh * 32
            out[base:base + 31, dc] = stt[2 * dc + hh]
    return out


def _col128(v):
    v = np.asarray(v).reshape(-1, 128)
    return np.ascontiguousarray(v.T)


def _chunk128(a, inner):
    a = np.asarray(a).reshape(-1, 128, inner)
    return np.ascontiguousarray(a.transpose(1, 0, 2).reshape(128, -1))


def kernel(**inputs):
    n = 8
    x = np.asarray(inputs["L0_attn_x"])

    def t(a):
        return np.ascontiguousarray(np.asarray(a).T)

    shared = {
        "w_ip": t(inputs["L0_attn_in_proj_w"]),
        "convw": _chunk128(inputs["L0_attn_conv_w"], 4),
        "convb": _col128(inputs["L0_attn_conv_b"]),
        "w_xp": _chunk128(t(inputs["L0_attn_x_proj_w"]), 96),
        "w_dt": t(inputs["L0_attn_dt_proj_w"]),
        "dtb": _col128(inputs["L0_attn_dt_proj_b"]),
        "alog": _chunk128(inputs["L0_attn_A_log"], NST),
        "dvec": _col128(inputs["L0_attn_D"]),
        "w_op": t(inputs["L0_attn_out_proj_w"]),
        "normw": _col128(inputs["L0_attn_norm_w"]),
        "w1t": t(inputs["L0_ffn_w1"]),
        "b1": _col128(inputs["L0_ffn_b1"]),
        "w2t": t(inputs["L0_ffn_w2"]),
        "b2": _col128(inputs["L0_ffn_b2"]),
        "wd2pe": _wpe_blockdiag(inputs["L1_attn_W_pe"], 0),
        "wg2pe": _wpe_blockdiag(inputs["L1_attn_W_pe"], 31),
        "bpe63d": _stack32(np.asarray(inputs["L1_attn_b_pe"])[0:31]),
        "bpe63g": _stack32(np.asarray(inputs["L1_attn_b_pe"])[31:62]),
        "stt63": _stt_layout(inputs["L1_attn_stt_energy"]),
        "wfct": t(inputs["L1_attn_W_fc"]),
        "gamma": _col128(inputs["L1_attn_gamma"]),
        "beta": _col128(inputs["L1_attn_beta"]),
        "cnw1": np.ascontiguousarray(inputs["L1_ffn_cn_w1"]),
        "cnb1": _col128(inputs["L1_ffn_cn_b1"]),
        "cnw2": np.ascontiguousarray(inputs["L1_ffn_cn_w2"]),
        "cnb2": _col128(inputs["L1_ffn_cn_b2"]),
        "cnw3": np.ascontiguousarray(inputs["L1_ffn_cn_w3"]),
        "cnb3": _col128(inputs["L1_ffn_cn_b3"]),
        "cnw4": _col128(inputs["L1_ffn_cn_w4"]),
        "cnb4": np.asarray(inputs["L1_ffn_cn_b4"]).reshape(1, 1),
        "gcw1": np.ascontiguousarray(inputs["L1_ffn_gc_w1"]),
        "gcb1": _col128(inputs["L1_ffn_gc_b1"]),
        "gcw2": np.ascontiguousarray(inputs["L1_ffn_gc_w2"]),
        "gcb2": _col128(inputs["L1_ffn_gc_b2"]),
        "gcw3": _col128(inputs["L1_ffn_gc_w3"]),
        "gcb3": np.asarray(inputs["L1_ffn_gc_b3"]).reshape(1, 1),
        "arange_": np.asarray(inputs["L1_ffn_adaptive_range"]).reshape(1, 1),
    }
    shared = {k: np.ascontiguousarray(v, dtype=np.float32) for k, v in shared.items()}
    in_maps = []
    for c in range(n):
        m = dict(shared)
        m["x"] = np.ascontiguousarray(x[c].T)
        in_maps.append(m)

    nc = build()
    res = run_bass_kernel_spmd(nc, in_maps, core_ids=list(range(n)))
    out = np.stack([np.asarray(r["out"]).T for r in res.results])
    return out.astype(np.float32)


# version 04
# speedup vs baseline: 1.0353x; 1.0353x over previous
import os
import os
import contextlib
import numpy as np
import concourse.bass as bass
import concourse.mybir as mybir
from concourse.bass_utils import run_bass_kernel_spmd

F32 = mybir.dt.float32
BF16 = mybir.dt.bfloat16
AF = mybir.ActivationFunctionType
ALU = mybir.AluOpType
AX = mybir.AxisListType

T = 4096
D = 1024
DI = 2048
H4 = 4096
NST = 16
NH = 16
HD = 64
KW = 31
TC = 512
NTC = T // TC
TB = 1024
NTB = T // TB

ENGS = ("sync", "vector", "scalar", "tensor", "gpsimd")
SEM_SEG_LIMIT = 30000


class Buf:
    __slots__ = ("w", "rs", "dsem", "is_psum")

    def __init__(self):
        self.w = None
        self.rs = []
        self.dsem = None
        self.is_psum = False


class Arena:
    def __init__(self):
        self.regions = []

    def take(self, start, end):
        b = Buf()
        for item in self.regions[:]:
            s0, e0, ob = item
            if s0 < end and start < e0:
                if ob.w is not None:
                    b.rs.append(ob.w)
                b.rs.extend(ob.rs)
                self.regions.remove(item)
                if s0 < start:
                    self.regions.append((s0, start, ob))
                if e0 > end:
                    self.regions.append((end, e0, ob))
        self.regions.append((start, end, b))
        return b


CSEMS = ("vector", "scalar", "tensor", "gpsimd_c")
N_SYNC_SEMS = 40
N_GP_SEMS = 30
DSEMS = tuple(f"d{i}" for i in range(N_SYNC_SEMS)) + tuple(f"g{i}" for i in range(N_GP_SEMS))
SEMS = CSEMS + DSEMS


def _sem_engine(sem):
    if sem == "gpsimd_c":
        return "gpsimd"
    return sem if sem in CSEMS else None


class Prog:
    def __init__(self):
        self.ops = {e: [] for e in ENGS}
        self.cnt = {sm: 0 for sm in SEMS}
        self.seg = {sm: 0 for sm in SEMS}
        self.segfinal = {sm: {} for sm in SEMS}
        self.pending = {e: {} for e in ENGS}
        self.waited = {e: {} for e in ENGS}
        self.dma_rr = {"sync": 0, "gpsimd": 0, "scalar": 0, "vector": 0, "tensor": 0}

    def next_dma_sem(self, q):
        if q == "gpsimd":
            sm = f"g{self.dma_rr[q] % N_GP_SEMS}"
        else:
            sm = f"d{self.dma_rr['sync'] % N_SYNC_SEMS}"
            self.dma_rr['sync'] += 1
            return sm
        self.dma_rr[q] += 1
        return sm

    def _wait(self, e, key):
        sem, seg, val = key
        if val <= 0:
            return
        if _sem_engine(sem) == e and e != "gpsimd":
            if not (seg == self.seg[sem] and val >= self.cnt[sem] - 1):
                return
        if sem not in CSEMS:
            val = self.cnt[sem] if seg == self.seg[sem] else self.segfinal[sem][seg]
        assert seg < self.seg[sem] or (seg == self.seg[sem] and val <= self.cnt[sem])
        if self.waited[e].get((sem, seg), 0) >= val:
            return
        p = self.pending[e]
        p[(sem, seg)] = max(p.get((sem, seg), 0), val)

    def emit(self, e, fn, *, dma=False, inc=True, dmasem=None):
        w = sorted(self.pending[e].items())
        for (ps, v) in w:
            self.waited[e][ps] = max(self.waited[e].get(ps, 0), v)
        self.pending[e] = {}
        if dma:
            sm = dmasem
            assert sm is not None
        else:
            sm = {"sync": None, "gpsimd": "gpsimd_c"}.get(e, e)
        n = (16 if dma else 1) if inc else 0
        if sm is None:
            n = 0
        if n and self.cnt[sm] + n > SEM_SEG_LIMIT:
            assert sm in CSEMS, f"DMA sem overflow on {sm}"
            self.segfinal[sm][self.seg[sm]] = self.cnt[sm]
            self.seg[sm] += 1
            self.cnt[sm] = 0
            assert self.seg[sm] < 2, "increase semaphore segments"
        if n:
            self.cnt[sm] += n
        self.ops[e].append((w, fn, sm, n, self.seg[sm] if sm else 0,
                            self.cnt[sm] if sm else 0))
        if sm is None:
            return ("vector", 0, 0)
        return (sm, self.seg[sm], self.cnt[sm])


def op(P, e, fn, rd=(), wr=(), dma=False, inc=True):
    inc = True
    for b in rd:
        if b.w is not None:
            P._wait(e, b.w)
    for b in wr:
        if b.w is not None:
            P._wait(e, b.w)
        for r in b.rs:
            P._wait(e, r)
    dmasem = None
    if dma:
        b0 = wr[0]
        if b0.dsem is None:
            b0.dsem = {}
        if e not in b0.dsem:
            b0.dsem[e] = P.next_dma_sem(e)
        dmasem = b0.dsem[e]
    key = P.emit(e, fn, dma=dma, inc=inc, dmasem=dmasem)
    for b in rd:
        b.rs.append(key)
    for b in wr:
        b.w = key
        b.rs = []
    return key


def build(debug=False, upto=99):
    nc = bass.Bass("TRN2", target_bir_lowering=False)
    P = Prog()

    def ein(name, shape):
        return nc.dram_tensor(name, shape, F32, kind="ExternalInput").ap()

    x_in = ein("x", [D, T])
    w_ip = ein("w_ip", [D, 2 * DI])
    convw = ein("convw", [128, 64])
    convb = ein("convb", [128, 16])
    w_xp = ein("w_xp", [128, 1536])
    w_dt = ein("w_dt", [64, DI])
    dtb = ein("dtb", [128, 16])
    alog = ein("alog", [128, 256])
    dvec = ein("dvec", [128, 16])
    w_op = ein("w_op", [DI, D])
    normw = ein("normw", [128, 8])
    w1t = ein("w1t", [D, H4])
    b1 = ein("b1", [128, 32])
    w2t = ein("w2t", [H4, D])
    b2 = ein("b2", [128, 8])
    wd2pe = ein("wd2pe", [128, 63])
    wg2pe = ein("wg2pe", [128, 63])
    bpe63d = ein("bpe63d", [128, 1])
    bpe63g = ein("bpe63g", [128, 1])
    stt63 = ein("stt63", [128, 8])
    wfct = ein("wfct", [D, D])
    gamma = ein("gamma", [128, 8])
    beta = ein("beta", [128, 8])
    cnw1 = ein("cnw1", [D, D])
    cnb1 = ein("cnb1", [128, 8])
    cnw2 = ein("cnw2", [D, 512])
    cnb2 = ein("cnb2", [128, 4])
    cnw3 = ein("cnw3", [512, 256])
    cnb3 = ein("cnb3", [128, 2])
    cnw4 = ein("cnw4", [128, 2])
    cnb4 = ein("cnb4", [1, 1])
    gcw1 = ein("gcw1", [D, 512])
    gcb1 = ein("gcb1", [128, 4])
    gcw2 = ein("gcw2", [512, 256])
    gcb2 = ein("gcb2", [128, 2])
    gcw3 = ein("gcw3", [128, 2])
    gcb3 = ein("gcb3", [1, 1])
    arange_ = ein("arange_", [1, 1])

    out_ext = nc.dram_tensor("out", [D, T], F32, kind="ExternalOutput").ap()

    def scr(name, shape, dt=F32):
        if debug:
            return nc.dram_tensor(name, shape, dt, kind="ExternalOutput").ap()
        return nc.dram_tensor(name, shape, dt).ap()

    XN_f = scr("XN_f", [D, T])
    XN_b = scr("XN_b", [D, T], BF16)
    RN_b = scr("RN_b", [D, T], BF16)
    U_f = scr("U_f", [DI, T])
    U_b = scr("U_b", [DI, T], BF16)
    G_f = scr("G_f", [DI, T])
    DLT = scr("DLT", [DI, T])
    DU_b = scr("DU_b", [DI, T], BF16)
    BROW = scr("BROW", [NST, T], BF16)
    CROW = scr("CROW", [NST, T], BF16)
    RIROW = scr("RIROW", [1, T])
    YG_b = scr("YG_b", [DI, T], BF16)
    X1p = scr("X1p", [D, T])
    X1 = scr("X1", [D, T])
    BW1 = scr("BW1", [D, H4], BF16)
    BW2 = scr("BW2", [H4, D], BF16)
    H1 = scr("H1", [H4, T], BF16)
    XN1_f = scr("XN1_f", [D, T])
    XN1_b = scr("XN1_b", [D, T], BF16)
    CIT = scr("CIT", [T, D], BF16)
    VLX = scr("VLX", [D, T])
    OUTCA = scr("OUTCA", [D, T])
    OUTBA = scr("OUTBA", [D, T], BF16)
    OUTCB = scr("OUTCB", [D, T])
    OUTBB = scr("OUTBB", [D, T], BF16)
    ZBB_BS = 128 * 160
    ZBB_HS = 64 + 32 * ZBB_BS
    ZBB = scr("ZBB", [NH, ZBB_HS], BF16)
    ZPB_BS = 128 * 257
    ZPB_HS = 64 + 32 * ZPB_BS
    ZPB = scr("ZPB", [NH, ZPB_HS], BF16)

    dbuf = {}
    dparts = {}
    for t_ in (XN_f, XN_b, RN_b, U_f, U_b, G_f, DLT, DU_b, BROW, CROW, RIROW,
               YG_b, X1p, X1, BW1, BW2, H1, XN1_f, XN1_b, CIT, VLX,
               OUTCA, OUTBA, OUTCB, OUTBB, ZBB, ZPB, out_ext):
        dbuf[id(t_)] = Buf()
    for t_ in (XN_f, XN_b, RN_b, U_f, U_b, G_f, DLT, DU_b, BROW, CROW,
               YG_b, X1p, X1, H1, XN1_f, XN1_b, CIT, VLX,
               OUTCA, OUTBA, OUTCB, OUTBB, out_ext):
        dparts[id(t_)] = [Buf() for _ in range(8)]

    def db(t_):
        return dbuf[id(t_)]

    def dbt(t_, lo, hi):
        ps = dparts.get(id(t_))
        if ps is None:
            return [dbuf[id(t_)]]
        p0 = lo * 8 // T
        p1 = (hi - 1) * 8 // T
        return ps[p0:p1 + 1]

    ctx = contextlib.ExitStack()
    with ctx:
        def sbuf(name, shape, dt=F32):
            return ctx.enter_context(nc.sbuf_tensor(name, shape, dt))
        def psum(name, shape, dt=F32):
            return ctx.enter_context(nc.psum_tensor(name, shape, dt))

        B1 = sbuf("B1", [128, 32768], BF16)
        B2 = sbuf("B2", [128, 32768], BF16)
        FA = sbuf("FA", [128, 6656])
        FB = sbuf("FB", [128, 6144])
        AB1 = Arena()
        AB2 = Arena()
        AFA = Arena()
        AFB = Arena()

        def b1v(o, p_, d1, s1=None, d2=None, s2=None):
            if d2 is None:
                return bass.AP(B1, o, [[32768, p_], [s1 or 1, d1]])
            return bass.AP(B1, o, [[32768, p_], [s1, d1], [s2, d2]])

        def b2v(o, p_, d1, s1=None, d2=None, s2=None):
            if d2 is None:
                return bass.AP(B2, o, [[32768, p_], [s1 or 1, d1]])
            return bass.AP(B2, o, [[32768, p_], [s1, d1], [s2, d2]])

        pers = {}

        def mk(name, shape, dt=F32):
            t_ = sbuf(name, shape, dt)
            pers[name] = Buf()
            return t_

        onesv = mk("onesv", [128, 1])
        ones1 = mk("ones1", [1, 128])
        ones1b = mk("ones1b", [1, 128], BF16)
        onesDIb = mk("onesDIb", [128, 1], BF16)
        onesH4b = mk("onesH4b", [128, 1], BF16)
        idb = mk("idb", [128, 128], BF16)
        a_sb = mk("a_sb", [128, 16, 16])
        d_sb = mk("d_sb", [128, 16])
        cb_sb = mk("cb_sb", [128, 16])
        cw_sb = mk("cw_sb", [128, 16, 4])
        nw_sb = mk("nw_sb", [128, 8])
        g_col = mk("g_col", [128, 8])
        bt_col = mk("bt_col", [128, 8])
        b1_sb = mk("b1_sb", [128, 32])
        b2_sb = mk("b2_sb", [128, 8])
        dtb_sb = mk("dtb_sb", [128, 16])
        wd_sb = mk("wd_sb", [128, 63], BF16)
        wg_sb = mk("wg_sb", [128, 63], BF16)
        bpe_d = mk("bpe_d", [128, 1])
        bpe_g = mk("bpe_g", [128, 1])
        stt_sb = mk("stt_sb", [128, 8])
        ob124 = mk("ob124", [128, 4])
        ob2t_lo = mk("ob2t_lo", [2, 128])
        ob2t_hi = mk("ob2t_hi", [2, 128])
        ritok = mk("ritok", [128, 32])
        wmask = mk("wmask", [128, 30], BF16)
        cn4_sb = mk("cn4_sb", [128, 2], BF16)
        gc3_sb = mk("gc3_sb", [128, 2], BF16)
        cnb1_sb = mk("cnb1_sb", [128, 8])
        cnb2_sb = mk("cnb2_sb", [128, 4])
        cnb3_sb = mk("cnb3_sb", [128, 2])
        gcb1_sb = mk("gcb1_sb", [128, 4])
        gcb2_sb = mk("gcb2_sb", [128, 2])
        sc_sb = mk("sc_sb", [1, 16])
        gvec = mk("gvec", [128, 8])
        gvec_b = mk("gvec_b", [128, 8], BF16)
        tails = mk("tails", [128, 16, 3])
        carry = mk("carry", [128, 16, 16])
        abuf = sbuf("abuf", [128, 2, TB])
        pers["abuf0"] = Buf()
        pers["abuf1"] = Buf()
        dub_sb = mk("dub_sb", [128, TB], BF16)
        pers["rowbuf"] = Buf()
        epst = mk("epst", [128, 4])
        zbig = mk("zbig", [128, 512], BF16)

        PY = psum("PY", [128, 1024])
        P0 = psum("P0", [128, 512])
        P1 = psum("P1", [128, 512])
        P2 = psum("P2", [128, 512])
        P3 = psum("P3", [128, 512])
        P4 = psum("P4", [128, 512])
        PBm = psum("PBm", [128, 1024], BF16)
        pbuf = {id(t_): Buf() for t_ in (PY, P0, P1, P2, P3, P4)}
        pb_half = [Buf(), Buf()]

        def pb(t_):
            return pbuf[id(t_)]

        sems = {}
        for sm in CSEMS:
            sems[sm] = [ctx.enter_context(nc.semaphore(f"sem_{sm}_{i}")) for i in range(2)]
        for sm in DSEMS:
            sems[sm] = [ctx.enter_context(nc.semaphore(f"sem_{sm}"))]
        block = ctx.enter_context(nc.Block())

        for b_ in list(pbuf.values()) + pb_half:
            b_.is_psum = True

        psum_rd = {"e": None, "key": None}

        tr_hist = []
        tr_base = [None]

        def TR(fn, rd=(), wr=(), base=0):
            if tr_hist:
                if base != tr_base[0] and tr_hist[-1][0] is not None:
                    P._wait("tensor", tr_hist[-1][0])
                elif len(tr_hist) >= 2 and tr_hist[-2][0] is not None:
                    P._wait("tensor", tr_hist[-2][0])
            tr_base[0] = base
            k = O("tensor", fn, rd=rd, wr=wr)
            slot = [None]
            tr_hist.append(slot)
            return slot

        def O(e, fn, rd=(), wr=(), dma=False, inc=True):
            touches_psum = any(getattr(b_, "is_psum", False) for b_ in rd)
            if touches_psum and e in ("vector", "scalar"):
                if psum_rd["e"] is not None and psum_rd["e"] != e:
                    P._wait(e, psum_rd["key"])
            k = op(P, e, fn, rd=rd, wr=wr, dma=dma, inc=inc)
            if touches_psum and e in ("vector", "scalar"):
                psum_rd["e"] = e
                psum_rd["key"] = k
            return k

        def LD(dst, src, rd=(), wr=()):
            return O("sync", lambda e, d=dst, s=src: e.dma_start(d, s),
                     rd=rd, wr=wr, dma=True)

        def ST(dst, src, rd=(), wr=(), q="gpsimd"):
            return O(q, lambda e, d=dst, s=src: e.dma_start(d, s),
                     rd=rd, wr=wr, dma=True)

        smalls = [
            (a_sb[:].rearrange("p a b -> p (a b)"), alog, "a_sb"),
            (d_sb[:], dvec, "d_sb"),
            (cb_sb[:], convb, "cb_sb"),
            (cw_sb[:].rearrange("p a b -> p (a b)"), convw, "cw_sb"),
            (nw_sb[:], normw, "nw_sb"),
            (g_col[:], gamma, "g_col"),
            (bt_col[:], beta, "bt_col"),
            (b1_sb[:], b1, "b1_sb"),
            (b2_sb[:], b2, "b2_sb"),
            (dtb_sb[:], dtb, "dtb_sb"),
            (stt_sb[:], stt63, "stt_sb"),
            (bpe_d[:], bpe63d, "bpe_d"),
            (bpe_g[:], bpe63g, "bpe_g"),
            (cnb1_sb[:], cnb1, "cnb1_sb"),
            (cnb2_sb[:], cnb2, "cnb2_sb"),
            (cnb3_sb[:], cnb3, "cnb3_sb"),
            (gcb1_sb[:], gcb1, "gcb1_sb"),
            (gcb2_sb[:], gcb2, "gcb2_sb"),
            (sc_sb[0:1, 4:5], cnb4, "sc_sb"),
            (sc_sb[0:1, 5:6], gcb3, "sc_sb"),
            (sc_sb[0:1, 1:2], arange_, "sc_sb"),
        ]
        for dst, src, nm in smalls:
            LD(dst, src, wr=[pers[nm]])

        for t_, v, nm in ((onesv, 1.0 / 1024.0, "onesv"), (ones1, 1.0, "ones1"),
                          (ones1b, 1.0, "ones1b"), (onesDIb, 1.0 / 1024.0, "onesDIb"),
                          (onesH4b, 1.0 / 4096.0, "onesH4b")):
            O("gpsimd", lambda e, t_=t_, v=v: e.memset(t_[:], v), wr=[pers[nm]])
        O("gpsimd", lambda e: e.memset(idb[:], 0.0), wr=[pers["idb"]])
        O("gpsimd", lambda e: e.affine_select(out=idb[:], in_=idb[:],
                                              compare_op=ALU.not_equal, fill=1.0, base=0,
                                              pattern=[[-1, 128]], channel_multiplier=1),
          wr=[pers["idb"]])
        O("gpsimd", lambda e: e.memset(wmask[:], 1.0), wr=[pers["wmask"]])
        O("gpsimd", lambda e: e.affine_select(
            out=wmask[0:31, :], in_=wmask[0:31, :],
            compare_op=ALU.is_ge, fill=0.0, base=-30,
            pattern=[[1, 30]], channel_multiplier=1), wr=[pers["wmask"]])
        for j in range(1, 4):
            O("gpsimd", lambda e, j=j: e.dma_start(wmask[j * 32:j * 32 + 31, :],
                                                   wmask[0:31, :]),
              rd=[pers["wmask"]], wr=[pers["wmask"]], dma=True)
        O("gpsimd", lambda e: e.memset(ob124[:], 1.0), wr=[pers["ob124"]])
        O("gpsimd", lambda e: e.affine_select(
            out=ob124[:], in_=ob124[:], compare_op=ALU.is_ge, fill=0.0,
            base=0, pattern=[[-32, 4]], channel_multiplier=1), wr=[pers["ob124"]])
        O("gpsimd", lambda e: e.affine_select(
            out=ob124[:], in_=ob124[:], compare_op=ALU.is_ge, fill=0.0,
            base=30, pattern=[[32, 4]], channel_multiplier=-1), wr=[pers["ob124"]])
        O("gpsimd", lambda e: e.memset(ob2t_lo[:], 1.0), wr=[pers["ob2t_lo"]])
        O("gpsimd", lambda e: e.affine_select(
            out=ob2t_lo[:], in_=ob2t_lo[:], compare_op=ALU.is_ge, fill=0.0,
            base=0, pattern=[[1, 128]], channel_multiplier=-32), wr=[pers["ob2t_lo"]])
        O("gpsimd", lambda e: e.affine_select(
            out=ob2t_lo[:], in_=ob2t_lo[:], compare_op=ALU.is_ge, fill=0.0,
            base=30, pattern=[[-1, 128]], channel_multiplier=32), wr=[pers["ob2t_lo"]])
        O("gpsimd", lambda e: e.memset(ob2t_hi[:], 1.0), wr=[pers["ob2t_hi"]])
        O("gpsimd", lambda e: e.affine_select(
            out=ob2t_hi[:], in_=ob2t_hi[:], compare_op=ALU.is_ge, fill=0.0,
            base=-64, pattern=[[1, 128]], channel_multiplier=-32), wr=[pers["ob2t_hi"]])
        O("gpsimd", lambda e: e.affine_select(
            out=ob2t_hi[:], in_=ob2t_hi[:], compare_op=ALU.is_ge, fill=0.0,
            base=94, pattern=[[-1, 128]], channel_multiplier=32), wr=[pers["ob2t_hi"]])
        O("gpsimd", lambda e: e.iota(ritok[:], [[128, 32]], base=1, channel_multiplier=1,
                                     allow_small_or_imprecise_dtypes=True), wr=[pers["ritok"]])
        O("gpsimd", lambda e: e.memset(zbig[:], 0.0), wr=[pers["zbig"]])
        O("gpsimd", lambda e: e.memset(tails[:], 0.0), wr=[pers["tails"]])
        for j, v in ((0, 1e-5), (1, 1e-6), (2, -1.0), (3, 1.0)):
            O("gpsimd", lambda e, j=j, v=v: e.memset(epst[:, j:j + 1], v), wr=[pers["epst"]])

        fari = AFA.take(0, 4096)
        O("gpsimd", lambda e: e.iota(FA[0:1, 0:T], [[1, T]], base=1, channel_multiplier=0,
                                     allow_small_or_imprecise_dtypes=True), wr=[fari])
        O("vector", lambda e: e.reciprocal(FA[0:1, 0:T], FA[0:1, 0:T]), rd=[fari], wr=[fari])
        O("vector", lambda e: e.reciprocal(ritok[:], ritok[:]),
          rd=[pers["ritok"]], wr=[pers["ritok"]])
        ST(RIROW, FA[0:1, 0:T], rd=[fari], wr=[db(RIROW)])

        O("scalar", lambda e: e.activation(a_sb[:].rearrange("p a b -> p (a b)"),
                                           a_sb[:].rearrange("p a b -> p (a b)"), AF.Exp),
          rd=[pers["a_sb"]], wr=[pers["a_sb"]])
        O("vector", lambda e: e.tensor_scalar(a_sb[:].rearrange("p a b -> p (a b)"),
                                              a_sb[:].rearrange("p a b -> p (a b)"),
                                              -1.0, None, ALU.mult),
          rd=[pers["a_sb"]], wr=[pers["a_sb"]])

        fpe = AFA.take(4096, 4300)
        LD(FA[:, 4096:4159], wd2pe, wr=[fpe])
        O("vector", lambda e: e.tensor_copy(wd_sb[:], FA[:, 4096:4159]),
          rd=[fpe], wr=[pers["wd_sb"]])
        LD(FA[:, 4096:4159], wg2pe, wr=[fpe])
        O("vector", lambda e: e.tensor_copy(wg_sb[:], FA[:, 4096:4159]),
          rd=[fpe], wr=[pers["wg_sb"]])
        LD(FA[:, 4160:4162], cnw4, wr=[fpe])
        LD(FA[:, 4162:4164], gcw3, wr=[fpe])
        O("vector", lambda e: e.tensor_copy(cn4_sb[:], FA[:, 4160:4162]),
          rd=[fpe], wr=[pers["cn4_sb"]])
        O("vector", lambda e: e.tensor_copy(gc3_sb[:], FA[:, 4162:4164]),
          rd=[fpe], wr=[pers["gc3_sb"]])

        def binarize(wt, bwout, nchunk, onescol, onesbuf, Hh):
            fw = AFA.take(0, 4096)
            bab = AB1.take(0, nchunk * Hh)
            bsg = AB2.take(0, 4096)
            bar = AB2.take(8192, 8192 + Hh)
            far = AFB.take(0, 4096)
            for c in range(nchunk):
                LD(FA[:, 0:Hh], wt[c * 128:(c + 1) * 128, :], wr=[fw])
                O("scalar", lambda e, c=c, Hh=Hh: e.activation(
                    b1v(c * Hh, 128, Hh), FA[:, 0:Hh], AF.Abs), rd=[fw], wr=[bab])
            nq = Hh // 512
            for q in range(nq):
                pq = (P0, P1)[q % 2]
                for c in range(nchunk):
                    O("tensor", lambda e, c=c, q=q, pq=pq, Hh=Hh: e.matmul(
                        pq[0:1, :], onescol[:], b1v(c * Hh + q * 512, 128, 512),
                        start=(c == 0), stop=(c == nchunk - 1)),
                      rd=[bab, onesbuf], wr=[pb(pq)], inc=(c == nchunk - 1))
                O("vector", lambda e, q=q, pq=pq: e.tensor_copy(
                    FB[0:1, q * 512:(q + 1) * 512], pq[0:1, :]),
                  rd=[pb(pq)], wr=[far])
            O("vector", lambda e, Hh=Hh: e.tensor_copy(
                b2v(8192, 1, Hh), FB[0:1, 0:Hh]), rd=[far], wr=[bar])
            for c in range(nchunk):
                LD(FA[:, 0:Hh], wt[c * 128:(c + 1) * 128, :], wr=[fw])
                O("scalar", lambda e, Hh=Hh: e.activation(
                    b2v(0, 128, Hh), FA[:, 0:Hh], AF.Sign), rd=[fw], wr=[bsg])
                for q in range(Hh // 512):
                    ph = q % 2
                    O("tensor", lambda e, q=q, ph=ph, Hh=Hh: e.matmul(
                        PY[:, ph * 512:(ph + 1) * 512], ones1b[:],
                        b2v(8192 + q * 512, 1, 512), start=True, stop=True),
                      rd=[bar, pers["ones1b"]], wr=[pb(PY)])
                    O("vector", lambda e, q=q, ph=ph: e.tensor_tensor(
                        b2v(q * 512, 128, 512), b2v(q * 512, 128, 512),
                        PY[:, ph * 512:(ph + 1) * 512], ALU.mult),
                      rd=[pb(PY), bsg], wr=[bsg])
                ST(bwout[c * 128:(c + 1) * 128, :], b2v(0, 128, Hh),
                   rd=[bsg], wr=[db(bwout)])

        if upto >= 1:
            binarize(w1t, BW1, 8, onesDIb, pers["onesDIb"], H4)
            binarize(w2t, BW2, 32, onesH4b, pers["onesH4b"], D)

        def layer_norm_phase(src_ap, src_buf, dst_f, dst_b, rn_dst, first):
            fxt = AFA.take(0, 4096)
            fsq = AFB.take(0, 1024)
            fxn = AFB.take(1024, 2048)
            bxb = AB1.take(0, 1024)
            brn = AB2.take(0, 1024)
            rows = AFB.take(2048, 5120)
            for tc in range(NTC):
                for dc in range(8):
                    LD(FA[:, dc * 512:(dc + 1) * 512],
                       src_ap[dc * 128:(dc + 1) * 128, tc * TC:(tc + 1) * TC],
                       rd=(dbt(src_buf, tc * TC, (tc + 1) * TC) if src_buf is not None else ()),
                       wr=[fxt])
                for dc in range(8):
                    sl = dc % 2
                    O("scalar", lambda e, dc=dc, sl=sl: e.activation(
                        FB[:, sl * 512:(sl + 1) * 512], FA[:, dc * 512:(dc + 1) * 512],
                        AF.Square), rd=[fxt], wr=[fsq])
                    O("tensor", lambda e, dc=dc: e.matmul(
                        P0[0:1, :], onesv[:], FA[:, dc * 512:(dc + 1) * 512],
                        start=(dc == 0), stop=(dc == 7)),
                      rd=[fxt, pers["onesv"]], wr=[pb(P0)], inc=(dc == 7))
                    O("tensor", lambda e, dc=dc, sl=sl: e.matmul(
                        P1[0:1, :], onesv[:], FB[:, sl * 512:(sl + 1) * 512],
                        start=(dc == 0), stop=(dc == 7)),
                      rd=[fsq, pers["onesv"]], wr=[pb(P1)], inc=(dc == 7))
                mu = FB[0:1, 2048:2560]
                var = FB[0:1, 2560:3072]
                sd = FB[0:1, 3072:3584]
                rr = FB[0:1, 3584:4096]
                r2 = FB[0:1, 4096:4608]
                t5 = FB[0:1, 4608:5120]
                O("vector", lambda e: e.tensor_copy(mu, P0[0:1, :]), rd=[pb(P0)], wr=[rows])
                O("vector", lambda e: e.tensor_tensor(var, mu, mu, ALU.mult),
                  rd=[rows], wr=[rows])
                O("vector", lambda e: e.tensor_tensor(var, P1[0:1, :], var, ALU.subtract),
                  rd=[pb(P1), rows], wr=[rows])
                O("scalar", lambda e: e.activation(sd, var, AF.Sqrt, bias=epst[0:1, 0:1]),
                  rd=[rows, pers["epst"]], wr=[rows])
                O("vector", lambda e: e.reciprocal(rr, sd), rd=[rows], wr=[rows])
                if first:
                    O("vector", lambda e: e.tensor_tensor(t5, var, rr, ALU.mult),
                      rd=[rows], wr=[rows])
                    O("vector", lambda e: e.tensor_tensor(t5, t5, rr, ALU.mult),
                      rd=[rows], wr=[rows])
                    O("scalar", lambda e: e.activation(t5, t5, AF.Sqrt, bias=epst[0:1, 0:1]),
                      rd=[rows, pers["epst"]], wr=[rows])
                    O("vector", lambda e: e.reciprocal(r2, t5), rd=[rows], wr=[rows])
                O("tensor", lambda e: e.matmul(P2[:, :], ones1[:], mu, start=True, stop=True),
                  rd=[rows, pers["ones1"]], wr=[pb(P2)])
                O("tensor", lambda e: e.matmul(P3[:, :], ones1[:], rr, start=True, stop=True),
                  rd=[rows, pers["ones1"]], wr=[pb(P3)])
                if first:
                    O("tensor", lambda e: e.matmul(P4[:, :], ones1[:], r2,
                                                   start=True, stop=True),
                      rd=[rows, pers["ones1"]], wr=[pb(P4)])
                for dc in range(8):
                    sl = dc % 2
                    xn = FB[:, 1024 + sl * 512:1024 + (sl + 1) * 512]
                    xb = b1v(sl * 512, 128, 512)
                    rn = b2v(sl * 512, 128, 512)
                    O("vector", lambda e, dc=dc, xn=xn: e.tensor_tensor(
                        xn, FA[:, dc * 512:(dc + 1) * 512], P2[:, :], ALU.subtract),
                      rd=[fxt, pb(P2)], wr=[fxn])
                    O("vector", lambda e, xn=xn: e.tensor_tensor(xn, xn, P3[:, :], ALU.mult),
                      rd=[pb(P3), fxn], wr=[fxn])
                    O("scalar", lambda e, xn=xn, xb=xb: e.activation(xb, xn, AF.Copy),
                      rd=[fxn], wr=[bxb])
                    if first:
                        O("vector", lambda e, xn=xn, rn=rn: e.tensor_tensor(
                            rn, xn, P4[:, :], ALU.mult), rd=[fxn, pb(P4)], wr=[brn])
                    ST(dst_f[dc * 128:(dc + 1) * 128, tc * TC:(tc + 1) * TC], xn,
                       rd=[fxn], wr=dbt(dst_f, tc * TC, (tc + 1) * TC))
                    ST(dst_b[dc * 128:(dc + 1) * 128, tc * TC:(tc + 1) * TC], xb,
                       rd=[bxb], wr=dbt(dst_b, tc * TC, (tc + 1) * TC))
                    if first:
                        ST(rn_dst[dc * 128:(dc + 1) * 128, tc * TC:(tc + 1) * TC], rn,
                           rd=[brn], wr=dbt(rn_dst, tc * TC, (tc + 1) * TC))

        if upto >= 2:
            layer_norm_phase(x_in, None, XN_f, XN_b, RN_b, True)

        if upto >= 3:
            bw1p = AB1.take(0, 32768)
            fwl = AFA.take(0, 4096)
            for c in range(8):
                LD(FA[:, 0:4096], w_ip[c * 128:(c + 1) * 128, :], wr=[fwl])
                O("vector", lambda e, c=c: e.tensor_scalar(
                    b1v(c * 4096, 128, 4096), FA[:, 0:4096], nw_sb[:, c:c + 1],
                    None, ALU.mult), rd=[fwl, pers["nw_sb"]], wr=[bw1p])
            brn_t = [AB2.take(s * 4096, (s + 1) * 4096) for s in range(2)]
            bub = AB2.take(8192, 8704)
            fxp = AFA.take(4096, 4616)
            facc = AFA.take(4616, 5128)
            fuo = AFB.take(0, 512)
            fgo = AFB.take(512, 1024)
            banks = [(P0, pb(P0)), (P1, pb(P1)), (P2, pb(P2)), (P3, pb(P3))]
            for tc in range(NTC):
                sl = tc % 2
                for h in range(2):
                    LD(b2v(sl * 4096 + h * 2048, 128, 4, 512, 512, 1),
                       RN_b.rearrange("(a p) t -> p a t", p=128)[:, h * 4:(h + 1) * 4,
                                                                tc * TC:(tc + 1) * TC],
                       rd=dbt(RN_b, tc * TC, (tc + 1) * TC), wr=[brn_t[sl]])
                for m in range(32):
                    bk, bkb = banks[m % 4]
                    for k in range(8):
                        O("tensor", lambda e, m=m, k=k, bk=bk, sl=sl: e.matmul(
                            bk[:, :], b1v(k * 4096 + m * 128, 128, 128),
                            b2v(sl * 4096 + k * 512, 128, 512),
                            start=(k == 0), stop=(k == 7)),
                          rd=[bw1p, brn_t[sl]], wr=[bkb], inc=(k == 7))
                    if m < 16:
                        mc = m
                        O("vector", lambda e, mc=mc: e.tensor_copy(
                            FA[:, 4096:4099], tails[:, mc, :]),
                          rd=[pers["tails"]], wr=[fxp])
                        O("scalar", lambda e, bk=bk: e.activation(
                            FA[:, 4099:4611], bk[:, :], AF.Copy), rd=[bkb], wr=[fxp])
                        O("vector", lambda e, mc=mc: e.tensor_copy(
                            tails[:, mc, :], FA[:, 4608:4611]),
                          rd=[fxp], wr=[pers["tails"]])
                        O("vector", lambda e, mc=mc: e.tensor_scalar(
                            FA[:, 4616:5128], FA[:, 4096:4608], cw_sb[:, mc, 0:1],
                            None, ALU.mult), rd=[fxp, pers["cw_sb"]], wr=[facc])
                        for kk in range(1, 4):
                            O("vector", lambda e, mc=mc, kk=kk: e.scalar_tensor_tensor(
                                FA[:, 4616:5128], FA[:, 4096 + kk:4608 + kk],
                                cw_sb[:, mc, kk:kk + 1], FA[:, 4616:5128],
                                ALU.mult, ALU.add), rd=[fxp, pers["cw_sb"], facc], wr=[facc])
                        O("scalar", lambda e, mc=mc: e.activation(
                            FB[:, 0:512], FA[:, 4616:5128], AF.Silu,
                            bias=cb_sb[:, mc:mc + 1]), rd=[facc, pers["cb_sb"]], wr=[fuo])
                        O("scalar", lambda e, mc=mc: e.activation(
                            b2v(8192, 128, 512), FA[:, 4616:5128], AF.Silu,
                            bias=cb_sb[:, mc:mc + 1]), rd=[facc, pers["cb_sb"]], wr=[bub])
                        ST(U_f[mc * 128:(mc + 1) * 128, tc * TC:(tc + 1) * TC],
                           FB[:, 0:512], rd=[fuo], wr=dbt(U_f, tc * TC, (tc + 1) * TC))
                        ST(U_b[mc * 128:(mc + 1) * 128, tc * TC:(tc + 1) * TC],
                           b2v(8192, 128, 512), rd=[bub], wr=dbt(U_b, tc * TC, (tc + 1) * TC))
                    else:
                        mc = m - 16
                        O("scalar", lambda e, bk=bk: e.activation(
                            FB[:, 512:1024], bk[:, :], AF.Silu), rd=[bkb], wr=[fgo])
                        ST(G_f[mc * 128:(mc + 1) * 128, tc * TC:(tc + 1) * TC],
                           FB[:, 512:1024], rd=[fgo], wr=dbt(G_f, tc * TC, (tc + 1) * TC))

        KSUB = int(os.environ.get("KSUB", "99"))
        KTC = int(os.environ.get("KTC", "8"))
        if upto >= 4:
            bxp = AB1.take(0, 1536)
            bdt = AB1.take(2048, 4096)
            bdtb = AB1.take(4096, 4608)
            bbce = AB1.take(4608, 5120)
            bubt = [AB2.take(s * 8192, (s + 1) * 8192) for s in range(2)]
            bduo = AB2.take(16384, 16896)
            fwl = AFA.take(0, 2048)
            f_ab = AFB.take(0, 512)
            f_ln = AFB.take(512, 1024)
            f_dl = AFB.take(1024, 1536)
            f_uf = AFB.take(1536, 2048)
            LD(FA[:, 0:1536], w_xp, wr=[fwl])
            O("vector", lambda e: e.tensor_copy(b1v(0, 128, 1536), FA[:, 0:1536]),
              rd=[fwl], wr=[bxp])
            LD(FA[0:64, 0:2048], w_dt, wr=[fwl])
            O("vector", lambda e: e.tensor_copy(b1v(2048, 64, 2048), FA[0:64, 0:2048]),
              rd=[fwl], wr=[bdt])
            for tc in range(KTC):
                sl = tc % 2
                for h in range(4):
                    LD(b2v(sl * 8192 + h * 2048, 128, 4, 512, 512, 1),
                       U_b.rearrange("(a p) t -> p a t", p=128)[:, h * 4:(h + 1) * 4,
                                                               tc * TC:(tc + 1) * TC],
                       rd=dbt(U_b, tc * TC, (tc + 1) * TC), wr=[bubt[sl]])
                for k in range(16):
                    O("tensor", lambda e, k=k, sl=sl: e.matmul(
                        P4[0:96, :], b1v(k * 96, 128, 96),
                        b2v(sl * 8192 + k * 512, 128, 512),
                        start=(k == 0), stop=(k == 15)),
                      rd=[bxp, bubt[sl]], wr=[pb(P4)], inc=(k == 15))
                O("scalar", lambda e: e.activation(b1v(4096, 64, 512), P4[0:64, :], AF.Copy),
                  rd=[pb(P4)], wr=[bdtb])
                O("vector", lambda e: e.tensor_copy(b1v(4608, 96, 512)[64:96, :],
                                                    P4[64:96, :]),
                  rd=[pb(P4)], wr=[bbce])
                ST(BROW[:, tc * TC:(tc + 1) * TC], b1v(4608, 96, 512)[64:80, :],
                   rd=[bbce], wr=dbt(BROW, tc * TC, (tc + 1) * TC))
                ST(CROW[:, tc * TC:(tc + 1) * TC], b1v(4608, 96, 512)[80:96, :],
                   rd=[bbce], wr=dbt(CROW, tc * TC, (tc + 1) * TC))
                for mc in range(16 if KSUB >= 1 else 0):
                    pk, pkb = ((P0, pb(P0)), (P1, pb(P1)))[mc % 2]
                    O("tensor", lambda e, mc=mc, pk=pk: e.matmul(
                        pk[:, :], b1v(2048 + mc * 128, 64, 128), b1v(4096, 64, 512),
                        start=True, stop=True), rd=[bdt, bdtb], wr=[pkb])
                    if KSUB < 20:
                        continue
                    if KSUB >= 21:
                        O("scalar", lambda e, pk=pk, mc=mc: e.activation(
                            FB[:, 0:512], pk[:, :], AF.Abs, bias=dtb_sb[:, mc:mc + 1]),
                          rd=[pkb, pers["dtb_sb"]], wr=[f_ab])
                    if KSUB >= 22:
                        O("scalar", lambda e: e.activation(FB[:, 512:1024], FB[:, 0:512],
                                                           AF.Exp, scale=epst[:, 2:3]),
                          rd=[f_ab, pers["epst"]], wr=[f_ln])
                    if KSUB >= 23:
                        O("scalar", lambda e: e.activation(FB[:, 512:1024], FB[:, 512:1024],
                                                           AF.Ln, bias=epst[:, 3:4]),
                          rd=[f_ln, pers["epst"]], wr=[f_ln])
                    if KSUB >= 24:
                        O("vector", lambda e, pk=pk, mc=mc: e.tensor_scalar(
                            FB[:, 1024:1536], pk[:, :], dtb_sb[:, mc:mc + 1], 0.0,
                            ALU.add, ALU.max), rd=[pkb, pers["dtb_sb"]], wr=[f_dl])
                    if KSUB >= 25:
                        O("vector", lambda e: e.tensor_tensor(
                            FB[:, 1024:1536], FB[:, 1024:1536], FB[:, 512:1024], ALU.add),
                          rd=[f_ln, f_dl], wr=[f_dl])
                    if KSUB < 30:
                        continue
                    LD(FB[:, 1536:2048],
                       U_f[mc * 128:(mc + 1) * 128, tc * TC:(tc + 1) * TC],
                       rd=dbt(U_f, tc * TC, (tc + 1) * TC), wr=[f_uf])
                    O("vector", lambda e: e.tensor_tensor(
                        b2v(16384, 128, 512), FB[:, 1024:1536], FB[:, 1536:2048],
                        ALU.mult), rd=[f_dl, f_uf], wr=[bduo])
                    ST(DLT[mc * 128:(mc + 1) * 128, tc * TC:(tc + 1) * TC],
                       FB[:, 1024:1536], rd=[f_dl], wr=dbt(DLT, tc * TC, (tc + 1) * TC))
                    ST(DU_b[mc * 128:(mc + 1) * 128, tc * TC:(tc + 1) * TC],
                       b2v(16384, 128, 512), rd=[bduo], wr=dbt(DU_b, tc * TC, (tc + 1) * TC))

        if upto >= 5:
            bB = AB1.take(0, 16384)
            bX = AB1.take(16384, 32768)
            bC = AB2.take(0, 16384)
            bH = AB2.take(16384, 32768)
            f_d = AFA.take(0, 1024)
            f_u = AFA.take(1024, 2048)
            f_g = AFA.take(2048, 3072)
            f_y = AFA.take(3072, 4096)
            for tb in range(NTB):
                for n in range(16):
                    ST(b1v(n * TB, 128, TB),
                       bass.AP(BROW.tensor, n * T + tb * TB, [[0, 128], [1, TB]]),
                       rd=dbt(BROW, tb * TB, (tb + 1) * TB), wr=[bB], q="scalar")
                    ST(b2v(n * TB, 128, TB),
                       bass.AP(CROW.tensor, n * T + tb * TB, [[0, 128], [1, TB]]),
                       rd=dbt(CROW, tb * TB, (tb + 1) * TB), wr=[bC], q="scalar")
                for dc in range(16):
                    LD(FA[:, 0:1024], DLT[dc * 128:(dc + 1) * 128, tb * TB:(tb + 1) * TB],
                       rd=dbt(DLT, tb * TB, (tb + 1) * TB), wr=[f_d])
                    LD(dub_sb[:], DU_b[dc * 128:(dc + 1) * 128, tb * TB:(tb + 1) * TB],
                       rd=dbt(DU_b, tb * TB, (tb + 1) * TB), wr=[pers["dub_sb"]])
                    LD(FA[:, 1024:2048], U_f[dc * 128:(dc + 1) * 128, tb * TB:(tb + 1) * TB],
                       rd=dbt(U_f, tb * TB, (tb + 1) * TB), wr=[f_u])
                    LD(FA[:, 2048:3072], G_f[dc * 128:(dc + 1) * 128, tb * TB:(tb + 1) * TB],
                       rd=dbt(G_f, tb * TB, (tb + 1) * TB), wr=[f_g])
                    O("vector", lambda e: e.tensor_tensor(
                        b1v(16384, 128, 16, TB, TB, 1),
                        bass.AP(dub_sb, 0, [[TB, 128], [0, 16], [1, TB]]),
                        b1v(0, 128, 16, TB, TB, 1), ALU.mult),
                      rd=[pers["dub_sb"], bB], wr=[bX])
                    for n in range(16):
                        sl = n % 2
                        ab = pers["abuf0"] if sl == 0 else pers["abuf1"]
                        O("scalar", lambda e, sl=sl, dc=dc, n=n: e.activation(
                            abuf[:, sl, :], FA[:, 0:1024], AF.Exp,
                            scale=a_sb[:, dc, n:n + 1]),
                          rd=[f_d, pers["a_sb"]], wr=[ab])
                        if tb == 0:
                            O("vector", lambda e, sl=sl, n=n: e.tensor_tensor_scan(
                                b2v(16384 + n * TB, 128, TB), abuf[:, sl, :],
                                b1v(16384 + n * TB, 128, TB), 0.0, ALU.mult, ALU.add),
                              rd=[ab, bX], wr=[bH])
                        else:
                            O("vector", lambda e, sl=sl, n=n, dc=dc: e.tensor_tensor_scan(
                                b2v(16384 + n * TB, 128, TB), abuf[:, sl, :],
                                b1v(16384 + n * TB, 128, TB),
                                carry[:, dc, n:n + 1], ALU.mult, ALU.add),
                              rd=[ab, bX, pers["carry"]], wr=[bH])
                    O("vector", lambda e: e.drain(), inc=False)
                    O("vector", lambda e, dc=dc: e.tensor_copy(
                        carry[:, dc, :],
                        bass.AP(B2, 16384 + TB - 1, [[32768, 128], [TB, 16]])),
                      rd=[bH], wr=[pers["carry"]])
                    O("vector", lambda e: e.tensor_tensor(
                        b1v(16384, 128, 16, TB, TB, 1), b2v(16384, 128, 16, TB, TB, 1),
                        b2v(0, 128, 16, TB, TB, 1), ALU.mult),
                      rd=[bH, bC], wr=[bX])
                    for n in range(16):
                        for q in range(2):
                            O("tensor", lambda e, n=n, q=q: e.matmul(
                                PY[:, q * 512:(q + 1) * 512], idb[:],
                                b1v(16384 + n * TB + q * 512, 128, 512),
                                start=(n == 0), stop=(n == 15)),
                              rd=[bX, pers["idb"]], wr=[pb(PY)],
                              inc=(n == 15 and q == 1))
                    for q in range(2):
                        O("vector", lambda e, q=q, dc=dc: e.scalar_tensor_tensor(
                            FA[:, 3072 + q * 512:3072 + (q + 1) * 512],
                            FA[:, 1024 + q * 512:1024 + (q + 1) * 512],
                            d_sb[:, dc:dc + 1], PY[:, q * 512:(q + 1) * 512],
                            ALU.mult, ALU.add),
                          rd=[f_u, pers["d_sb"], pb(PY)], wr=[f_y])
                        O("vector", lambda e, q=q: e.tensor_tensor(
                            b2v(16384 + q * 512, 128, 512),
                            FA[:, 3072 + q * 512:3072 + (q + 1) * 512],
                            FA[:, 2048 + q * 512:2048 + (q + 1) * 512], ALU.mult),
                          rd=[f_y, f_g], wr=[bH])
                    ST(YG_b[dc * 128:(dc + 1) * 128, tb * TB:(tb + 1) * TB],
                       b2v(16384, 128, 1024), rd=[bH], wr=dbt(YG_b, tb * TB, (tb + 1) * TB))

        if upto >= 6:
            bwo = AB1.take(0, 16384)
            fwl = AFA.take(0, 1024)
            for c in range(16):
                LD(FA[:, 0:1024], w_op[c * 128:(c + 1) * 128, :], wr=[fwl])
                O("vector", lambda e, c=c: e.tensor_copy(b1v(c * 1024, 128, 1024),
                                                         FA[:, 0:1024]),
                  rd=[fwl], wr=[bwo])
            byg = [AB2.take(s * 8192, (s + 1) * 8192) for s in range(2)]
            f_xn = AFA.take(0, 1024)
            f_x = AFA.take(1024, 2048)
            f_o = AFA.take(2048, 2560)
            for tc in range(NTC):
                sl = tc % 2
                for h in range(4):
                    LD(b2v(sl * 8192 + h * 2048, 128, 4, 512, 512, 1),
                       YG_b.rearrange("(a p) t -> p a t", p=128)[:, h * 4:(h + 1) * 4,
                                                                tc * TC:(tc + 1) * TC],
                       rd=dbt(YG_b, tc * TC, (tc + 1) * TC), wr=[byg[sl]])
                for m in range(8):
                    bk, bkb = [(P0, pb(P0)), (P1, pb(P1)), (P2, pb(P2)), (P3, pb(P3))][m % 4]
                    for k in range(16):
                        O("tensor", lambda e, m=m, k=k, bk=bk, sl=sl: e.matmul(
                            bk[:, :], b1v(k * 1024 + m * 128, 128, 128),
                            b2v(sl * 8192 + k * 512, 128, 512),
                            start=(k == 0), stop=(k == 15)),
                          rd=[bwo, byg[sl]], wr=[bkb], inc=(k == 15))
                    LD(FA[:, 0:512], XN_f[m * 128:(m + 1) * 128, tc * TC:(tc + 1) * TC],
                       rd=dbt(XN_f, tc * TC, (tc + 1) * TC), wr=[f_xn])
                    LD(FA[:, 1024:1536], x_in[m * 128:(m + 1) * 128, tc * TC:(tc + 1) * TC],
                       wr=[f_x])
                    O("vector", lambda e, bk=bk: e.tensor_tensor(
                        FA[:, 2048:2560], bk[:, :], FA[:, 0:512], ALU.add),
                      rd=[bkb, f_xn], wr=[f_o])
                    O("vector", lambda e: e.tensor_tensor(
                        FA[:, 2048:2560], FA[:, 2048:2560], FA[:, 1024:1536], ALU.add),
                      rd=[f_x, f_o], wr=[f_o])
                    ST(X1p[m * 128:(m + 1) * 128, tc * TC:(tc + 1) * TC], FA[:, 2048:2560],
                       rd=[f_o], wr=dbt(X1p, tc * TC, (tc + 1) * TC))

        if upto >= 7:
            bw = AB1.take(0, 32768)
            for c in range(8):
                LD(b1v(c * 4096, 128, 4096), BW1[c * 128:(c + 1) * 128, :],
                   rd=[db(BW1)], wr=[bw])
            bxn = [AB2.take(s * 4096, (s + 1) * 4096) for s in range(2)]
            bh1o = AB2.take(8192, 8704)
            banks = [(P0, pb(P0)), (P1, pb(P1)), (P2, pb(P2)), (P3, pb(P3))]
            for tc in range(NTC):
                sl = tc % 2
                for h in range(2):
                    LD(b2v(sl * 4096 + h * 2048, 128, 4, 512, 512, 1),
                       XN_b.rearrange("(a p) t -> p a t", p=128)[:, h * 4:(h + 1) * 4,
                                                                tc * TC:(tc + 1) * TC],
                       rd=dbt(XN_b, tc * TC, (tc + 1) * TC), wr=[bxn[sl]])
                for m in range(32):
                    bk, bkb = banks[m % 4]
                    for k in range(8):
                        O("tensor", lambda e, m=m, k=k, bk=bk, sl=sl: e.matmul(
                            bk[:, :], b1v(k * 4096 + m * 128, 128, 128),
                            b2v(sl * 4096 + k * 512, 128, 512),
                            start=(k == 0), stop=(k == 7)),
                          rd=[bw, bxn[sl]], wr=[bkb], inc=(k == 7))
                    O("scalar", lambda e, bk=bk, m=m: e.activation(
                        b2v(8192, 128, 512), bk[:, :], AF.Gelu, bias=b1_sb[:, m:m + 1]),
                      rd=[bkb, pers["b1_sb"]], wr=[bh1o])
                    ST(H1[m * 128:(m + 1) * 128, tc * TC:(tc + 1) * TC],
                       b2v(8192, 128, 512), rd=[bh1o], wr=dbt(H1, tc * TC, (tc + 1) * TC))

        if upto >= 8:
            bw = AB1.take(0, 32768)
            for c in range(32):
                LD(b1v(c * 1024, 128, 1024), BW2[c * 128:(c + 1) * 128, :],
                   rd=[db(BW2)], wr=[bw])
            bh1 = AB2.take(0, 16384)
            f_xp = AFA.take(0, 512)
            f_o = AFA.take(512, 1024)
            for tc in range(NTC):
                for h in range(8):
                    LD(b2v(h * 2048, 128, 4, 512, 512, 1),
                       H1.rearrange("(a p) t -> p a t", p=128)[:, h * 4:(h + 1) * 4,
                                                              tc * TC:(tc + 1) * TC],
                       rd=dbt(H1, tc * TC, (tc + 1) * TC), wr=[bh1])
                for m in range(8):
                    bk, bkb = [(P0, pb(P0)), (P1, pb(P1)), (P2, pb(P2)), (P3, pb(P3))][m % 4]
                    for k in range(32):
                        O("tensor", lambda e, m=m, k=k, bk=bk: e.matmul(
                            bk[:, :], b1v(k * 1024 + m * 128, 128, 128),
                            b2v(k * 512, 128, 512),
                            start=(k == 0), stop=(k == 31)),
                          rd=[bw, bh1], wr=[bkb], inc=(k == 31))
                    LD(FA[:, 0:512], X1p[m * 128:(m + 1) * 128, tc * TC:(tc + 1) * TC],
                       rd=dbt(X1p, tc * TC, (tc + 1) * TC), wr=[f_xp])
                    O("vector", lambda e, bk=bk, m=m: e.scalar_tensor_tensor(
                        FA[:, 512:1024], bk[:, :], b2_sb[:, m:m + 1], FA[:, 0:512],
                        ALU.add, ALU.add), rd=[bkb, pers["b2_sb"], f_xp], wr=[f_o])
                    ST(X1[m * 128:(m + 1) * 128, tc * TC:(tc + 1) * TC], FA[:, 512:1024],
                       rd=[f_o], wr=dbt(X1, tc * TC, (tc + 1) * TC))

        if upto >= 9:
            layer_norm_phase(X1, X1, XN1_f, XN1_b, None, False)
            fxt = AFA.take(0, 4096)
            for dc in range(8):
                LD(FA[:, 0:4096], XN1_f[dc * 128:(dc + 1) * 128, :],
                   rd=dbt(XN1_f, 0, T), wr=[fxt])
                O("vector", lambda e, dc=dc: e.tensor_reduce(
                    gvec[:, dc:dc + 1], FA[:, 0:4096], AX.X, ALU.add),
                  rd=[fxt], wr=[pers["gvec"]])
            O("vector", lambda e: e.tensor_scalar(gvec[:], gvec[:], 1.0 / T, None,
                                                  ALU.mult),
              rd=[pers["gvec"]], wr=[pers["gvec"]])
            O("vector", lambda e: e.tensor_copy(gvec_b[:], gvec[:]),
              rd=[pers["gvec"]], wr=[pers["gvec_b"]])
            bgw = AB1.take(0, 4096)
            bg1 = AB1.take(4096, 4100)
            bg2 = AB1.take(4100, 4102)
            fwl = AFA.take(0, 512)
            for c in range(8):
                LD(FA[:, 0:512], gcw1[c * 128:(c + 1) * 128, :], wr=[fwl])
                O("vector", lambda e, c=c: e.tensor_copy(b1v(c * 512, 128, 512),
                                                         FA[:, 0:512]),
                  rd=[fwl], wr=[bgw])
            for m in range(4):
                for k in range(8):
                    O("tensor", lambda e, m=m, k=k: e.matmul(
                        P4[:, m:m + 1], b1v(k * 512 + m * 128, 128, 128),
                        gvec_b[:, k:k + 1], start=(k == 0), stop=(k == 7)),
                      rd=[bgw, pers["gvec_b"]], wr=[pb(P4)], inc=(k == 7))
            for m in range(4):
                O("scalar", lambda e, m=m: e.activation(
                    b1v(4096 + m, 128, 1), P4[:, m:m + 1], AF.Gelu,
                    bias=gcb1_sb[:, m:m + 1]), rd=[pb(P4), pers["gcb1_sb"]], wr=[bg1])
            bgw2 = AB1.take(8192, 9216)
            for c in range(4):
                LD(FA[:, 0:256], gcw2[c * 128:(c + 1) * 128, :], wr=[fwl])
                O("vector", lambda e, c=c: e.tensor_copy(b1v(8192 + c * 256, 128, 256),
                                                         FA[:, 0:256]),
                  rd=[fwl], wr=[bgw2])
            for m in range(2):
                for k in range(4):
                    O("tensor", lambda e, m=m, k=k: e.matmul(
                        P4[:, 8 + m:9 + m], b1v(8192 + k * 256 + m * 128, 128, 128),
                        b1v(4096 + k, 128, 1), start=(k == 0), stop=(k == 3)),
                      rd=[bgw2, bg1], wr=[pb(P4)], inc=(k == 3))
            for m in range(2):
                O("scalar", lambda e, m=m: e.activation(
                    b1v(4100 + m, 128, 1), P4[:, 8 + m:9 + m], AF.Gelu,
                    bias=gcb2_sb[:, m:m + 1]), rd=[pb(P4), pers["gcb2_sb"]], wr=[bg2])
            for k in range(2):
                O("tensor", lambda e, k=k: e.matmul(
                    P4[0:1, 12:13], gc3_sb[:, k:k + 1], b1v(4100 + k, 128, 1),
                    start=(k == 0), stop=(k == 1)),
                  rd=[pers["gc3_sb"], bg2], wr=[pb(P4)], inc=(k == 1))
            O("scalar", lambda e: e.activation(sc_sb[0:1, 0:1], P4[0:1, 12:13],
                                               AF.Sigmoid, bias=sc_sb[0:1, 5:6]),
              rd=[pb(P4), pers["sc_sb"]], wr=[pers["sc_sb"]])
            O("vector", lambda e: e.tensor_tensor(sc_sb[0:1, 2:3], sc_sb[0:1, 1:2],
                                                  sc_sb[0:1, 0:1], ALU.mult),
              rd=[pers["sc_sb"]], wr=[pers["sc_sb"]])
            O("vector", lambda e: e.tensor_scalar(sc_sb[0:1, 2:3], sc_sb[0:1, 2:3],
                                                  1.4, 0.1, ALU.mult, ALU.add),
              rd=[pers["sc_sb"]], wr=[pers["sc_sb"]])
            O("vector", lambda e: e.tensor_tensor(sc_sb[0:1, 2:3], sc_sb[0:1, 2:3],
                                                  sc_sb[0:1, 1:2], ALU.subtract),
              rd=[pers["sc_sb"]], wr=[pers["sc_sb"]])
            O("vector", lambda e: e.tensor_scalar(sc_sb[0:1, 3:4], sc_sb[0:1, 1:2],
                                                  0.6, None, ALU.mult),
              rd=[pers["sc_sb"]], wr=[pers["sc_sb"]])

        for zt, hs in ((ZBB, ZBB_HS), (ZPB, ZPB_HS)):
            tot = NH * hs
            CH = 128 * 512
            nch = (tot + CH - 1) // CH
            for i in range(nch):
                sz = min(CH, tot - i * CH)
                rows = sz // 512
                if rows:
                    ST(bass.AP(zt.tensor, i * CH, [[512, rows], [1, 512]]),
                       bass.AP(zbig, 0, [[512, 1], [0, rows], [1, 512]]),
                       rd=[pers["zbig"]], wr=[db(zt)])
                if sz % 512:
                    ST(bass.AP(zt.tensor, i * CH + rows * 512, [[1, 1], [1, sz % 512]]),
                       zbig[0:1, 0:sz % 512], rd=[pers["zbig"]], wr=[db(zt)])

        if upto >= 10:
            bcum = AB1.take(0, 4096)
            bone4 = AB1.take(4096, 8192)
            bcit = AB1.take(8192, 8320)
            bwt = AB1.take(8320, 8384)
            brib = AB1.take(8448, 12544)
            bci = AB1.take(12544, 16640)
            bwe = AB2.take(0, 16384)
            f_sig = AFB.take(0, 1024)
            f_sd = AFB.take(1024, 2048)
            f_et = AFB.take(2048, 3072)
            f_rw = AFB.take(4096, 4608)
            O("gpsimd", lambda e: e.memset(b1v(4096, 128, 4096), 1.0), wr=[bone4])
            O("gpsimd", lambda e: e.memset(FB[:, 2048:3072], 0.0), wr=[f_et])
            fri = AFA.take(0, 4096)
            LD(FA[0:1, 0:4096], RIROW, rd=[db(RIROW)], wr=[fri])
            for q8 in range(8):
                O("tensor", lambda e, q8=q8: e.matmul(
                    PY[:, 0:512], ones1[:], FA[0:1, q8 * 512:(q8 + 1) * 512],
                    start=True, stop=True),
                  rd=[fri, pers["ones1"]], wr=[pb(PY)])
                O("vector", lambda e, q8=q8: e.tensor_copy(
                    b1v(8448 + q8 * 512, 128, 512), PY[:, 0:512]),
                  rd=[pb(PY)], wr=[brib])
            f_x1 = AFA.take(0, 4096)
            for dc in range(8):
                po = (dc % 2) * 64
                LD(FA[:, 0:4096], XN1_f[dc * 128:(dc + 1) * 128, :],
                   rd=dbt(XN1_f, 0, T), wr=[f_x1])
                O("vector", lambda e: e.tensor_tensor_scan(
                    b1v(0, 128, 4096), b1v(4096, 128, 4096), FA[:, 0:4096],
                    0.0, ALU.mult, ALU.add), rd=[bone4, f_x1], wr=[bcum])
                O("vector", lambda e: e.drain(), inc=False)
                O("vector", lambda e: e.tensor_tensor(
                    b1v(12544, 128, 4096), b1v(0, 128, 4096), b1v(8448, 128, 4096),
                    ALU.mult), rd=[bcum, brib], wr=[bci])
                if KSUB < 41:
                    continue
                for tt in range(32 if KSUB >= 42 else 0):
                    s8 = tt % 8
                    slot = TR(lambda e, tt=tt, s8=s8: e.transpose(
                        PBm[:, s8 * 128:(s8 + 1) * 128], b1v(12544 + tt * 128, 128, 128),
                        idb[:]), rd=[bci, pers["idb"]], wr=[pb_half[s8 // 4]])
                    slot[0] = O("vector", lambda e, s8=s8: e.tensor_copy(
                        b1v(8192, 128, 128), PBm[:, s8 * 128:(s8 + 1) * 128]),
                      rd=[pb_half[s8 // 4]], wr=[bcit])
                    ST(CIT.rearrange("(a p) d -> a p d", p=128)[tt, :, dc * 128:(dc + 1) * 128],
                       b1v(8192, 128, 128), rd=[bcit], wr=dbt(CIT, tt * 128, tt * 128 + 128))
                for th in range(4 if KSUB >= 43 else 0):
                    for q in range(2):
                        col = th * 1024 + q * 512
                        O("tensor", lambda e, col=col, po=po: e.matmul(
                            P0[po:po + 63, :], wd_sb[:], b1v(12544 + col, 128, 512),
                            start=True, stop=True),
                          rd=[bci, pers["wd_sb"]], wr=[pb(P0)])
                        O("tensor", lambda e, col=col, po=po: e.matmul(
                            P1[po:po + 63, :], wg_sb[:], b1v(12544 + col, 128, 512),
                            start=True, stop=True),
                          rd=[bci, pers["wg_sb"]], wr=[pb(P1)])
                        O("scalar", lambda e, q=q, po=po: e.activation(
                            FB[po:po + 63, q * 512:(q + 1) * 512], P1[po:po + 63, :],
                            AF.Sigmoid, bias=bpe_g[po:po + 63, 0:1]),
                          rd=[pb(P1), pers["bpe_g"]], wr=[f_sig])
                        O("vector", lambda e, q=q, po=po: e.scalar_tensor_tensor(
                            FB[po:po + 63, 1024 + q * 512:1024 + (q + 1) * 512],
                            P0[po:po + 63, :], bpe_d[po:po + 63, 0:1],
                            FB[po:po + 63, q * 512:(q + 1) * 512], ALU.add, ALU.mult),
                          rd=[pb(P0), pers["bpe_d"], f_sig], wr=[f_sd])
                    O("scalar", lambda e, th=th, po=po, dc=dc: e.activation(
                        FB[po:po + 63, 2048:3072], FB[po:po + 63, 1024:2048], AF.Exp,
                        bias=stt_sb[po:po + 63, dc:dc + 1]),
                      rd=[f_sd, pers["stt_sb"]], wr=[f_et])
                    for q in range(2):
                        O("tensor", lambda e, q=q, dc=dc, po=po: e.matmul(
                            P3[0:2, :], ob124[:, (dc % 2) * 2:(dc % 2) * 2 + 2],
                            FB[:, 2048 + q * 512:2048 + (q + 1) * 512],
                            start=True, stop=True),
                          rd=[f_et, pers["ob124"]], wr=[pb(P3)])
                        O("vector", lambda e: e.tensor_copy(FB[0:2, 4096:4608], P3[0:2, :]),
                          rd=[pb(P3)], wr=[f_rw])
                        O("vector", lambda e: e.reciprocal(FB[0:2, 4096:4608],
                                                           FB[0:2, 4096:4608]),
                          rd=[f_rw], wr=[f_rw])
                        obt = ob2t_lo if dc % 2 == 0 else ob2t_hi
                        obtb = pers["ob2t_lo"] if dc % 2 == 0 else pers["ob2t_hi"]
                        O("tensor", lambda e, obt=obt: e.matmul(
                            P2[:, :], obt[:], FB[0:2, 4096:4608], start=True, stop=True),
                          rd=[f_rw, obtb], wr=[pb(P2)])
                        if KSUB >= 44:
                            O("vector", lambda e, q=q, th=th, dc=dc, po=po: e.tensor_tensor(
                                bass.AP(B2, (dc // 2) * 4096 + th * 1024 + q * 512,
                                        [[32768, 128], [1, 512]])[po:po + 63, :],
                                FB[po:po + 63, 2048 + q * 512:2048 + (q + 1) * 512],
                                P2[po:po + 63, :], ALU.mult),
                              rd=[f_et, pb(P2)], wr=[bwe])
                if KSUB < 44:
                    continue
                O("vector", lambda e, dc=dc, po=po: e.tensor_tensor(
                    bass.AP(B2, (dc // 2) * 4096, [[32768, 128], [1, 30]])[po:po + 63, :],
                    bass.AP(B2, (dc // 2) * 4096, [[32768, 128], [1, 30]])[po:po + 63, :],
                    wmask[po:po + 63, :], ALU.mult),
                  rd=[pers["wmask"], bwe], wr=[bwe])
            for dc in ([0, 2, 4, 6, 1, 3, 5, 7] if KSUB >= 45 else []):
                po = (dc % 2) * 64
                for tt in range(32):
                    s8 = tt % 8
                    slot = TR(lambda e, dc=dc, tt=tt, s8=s8, po=po: e.transpose(
                        PBm[0:128, s8 * 128:s8 * 128 + 63],
                        bass.AP(B2, (dc // 2) * 4096 + tt * 128, [[32768, 128], [1, 128]])
                        [po:po + 63, :], idb[po:po + 63, po:po + 63]),
                      rd=[bwe, pers["idb"]], wr=[pb_half[s8 // 4]], base=po)
                    slot[0] = O("vector", lambda e, s8=s8: e.tensor_copy(
                        b1v(8320, 128, 63), PBm[:, s8 * 128:s8 * 128 + 63]),
                      rd=[pb_half[s8 // 4]], wr=[bwt])
                    for hh in range(2 if KSUB >= 47 else 0):
                        h = 2 * dc + hh
                        ST(bass.AP(ZBB.tensor, h * ZBB_HS + 64 + tt * ZBB_BS - 30,
                                   [[161, 128], [1, 31]]),
                           b1v(8320 + hh * 32, 128, 31), rd=[bwt], wr=[db(ZBB)])
                        if KSUB >= 48:
                            ST(bass.AP(ZPB.tensor, h * ZPB_HS + 64 + tt * ZPB_BS + 98,
                                       [[258, 128], [1, 31]]),
                               b1v(8320 + hh * 32, 128, 31), rd=[bwt], wr=[db(ZPB)])

        if upto >= 11:
            bcv = AB1.take(0, 32768)
            bct = [AB2.take(16384 + s_ * 1024, 16384 + (s_ + 1) * 1024) for s_ in range(2)]
            NZS = 4
            bzb = [AB2.take(18432 + s_ * 128, 18432 + (s_ + 1) * 128) for s_ in range(NZS)]
            bzp = [AB2.take(18944 + s_ * 128, 18944 + (s_ + 1) * 128) for s_ in range(NZS)]
            bwbb = AB2.take(19456, 19584)
            bwpb = AB2.take(19584, 19712)
            bwfc = AB2.take(19712, 27904)

            def zload(i):
                b, h = divmod(i, NH)
                zs = i % NZS
                LD(b2v(18432 + zs * 128, 128, 128),
                   bass.AP(ZBB.tensor, h * ZBB_HS + 64 + b * ZBB_BS,
                           [[160, 128], [1, 128]]), rd=[db(ZBB)], wr=[bzb[zs]])
                if b > 0:
                    LD(b2v(18944 + zs * 128, 128, 128),
                       bass.AP(ZPB.tensor, h * ZPB_HS + 64 + b * ZPB_BS,
                               [[257, 128], [1, 128]]), rd=[db(ZPB)], wr=[bzp[zs]])

            for i0 in range(NZS - 1):
                zload(i0)
            for b in range(32):
                sl = b % 2
                LD(b2v(16384 + sl * 1024, 128, 1024),
                   CIT[b * 128:(b + 1) * 128, :], rd=dbt(CIT, b * 128, b * 128 + 128), wr=[bct[sl]])
                for h in range(NH):
                    i = b * NH + h
                    if i + NZS - 1 < 32 * NH:
                        zload(i + NZS - 1)
                    zs = i % NZS
                    pv, pvb = ((P0, pb(P0)), (P1, pb(P1)))[h % 2]
                    pvo = (h % 2) * 64
                    s8 = i % 4
                    slot = TR(lambda e, s8=s8, zs=zs: e.transpose(
                        PBm[:, s8 * 256:s8 * 256 + 128], b2v(18432 + zs * 128, 128, 128),
                        idb[:]), rd=[bzb[zs], pers["idb"]], wr=[pb_half[s8 // 2]])
                    slot[0] = O("vector", lambda e, s8=s8: e.tensor_copy(
                        b2v(19456, 128, 128), PBm[:, s8 * 256:s8 * 256 + 128]),
                      rd=[pb_half[s8 // 2]], wr=[bwbb])
                    if b > 0:
                        slot2 = TR(lambda e, s8=s8, zs=zs: e.transpose(
                            PBm[:, s8 * 256 + 128:s8 * 256 + 256],
                            b2v(18944 + zs * 128, 128, 128),
                            idb[:]), rd=[bzp[zs], pers["idb"]], wr=[pb_half[s8 // 2]])
                        slot2[0] = O("vector", lambda e, s8=s8: e.tensor_copy(
                            b2v(19584, 128, 128), PBm[:, s8 * 256 + 128:s8 * 256 + 256]),
                          rd=[pb_half[s8 // 2]], wr=[bwpb])
                    O("tensor", lambda e, h=h, sl=sl, pv=pv, pvo=pvo, b=b: e.matmul(
                        pv[pvo:pvo + 64, 0:128], b2v(16384 + sl * 1024 + h * 64, 128, 64),
                        b2v(19456, 128, 128), start=True, stop=(b == 0)),
                      rd=[bct[sl], bwbb], wr=[pvb], inc=(b == 0))
                    if b > 0:
                        O("tensor", lambda e, h=h, sl=sl, pv=pv, pvo=pvo: e.matmul(
                            pv[pvo:pvo + 64, 0:128], b2v(16384 + (1 - sl) * 1024 + h * 64, 128, 64),
                            b2v(19584, 128, 128), start=False, stop=True),
                          rd=[bct[1 - sl], bwpb], wr=[pvb])
                    O("vector", lambda e, h=h, b=b, pv=pv, pvo=pvo: e.tensor_copy(
                        b1v((h // 2) * 4096 + b * 128, 128, 128)[pvo:pvo + 64, :],
                        pv[pvo:pvo + 64, 0:128]),
                      rd=[pvb], wr=[bcv])
            fwl = AFA.take(0, 1024)
            for c in range(8):
                LD(FA[:, 0:1024], wfct[c * 128:(c + 1) * 128, :], wr=[fwl])
                O("vector", lambda e, c=c: e.tensor_copy(
                    b2v(19712 + c * 1024, 128, 1024), FA[:, 0:1024]),
                  rd=[fwl], wr=[bwfc])
            f_v = AFA.take(0, 4096)
            f_xn1 = AFB.take(0, 512)
            f_x1t = AFB.take(512, 1024)
            f_sq = AFB.take(1024, 1536)
            f_vx = AFB.take(1536, 2048)
            rows = AFB.take(2048, 4096)
            for tc in range(NTC):
                for m in range(8):
                    bk, bkb = ((P2, pb(P2)), (P3, pb(P3)))[m % 2]
                    for k in range(8):
                        O("tensor", lambda e, m=m, k=k, bk=bk, tc=tc: e.matmul(
                            bk[:, :], b2v(19712 + k * 1024 + m * 128, 128, 128),
                            b1v(k * 4096 + tc * 512, 128, 512),
                            start=(k == 0), stop=(k == 7)),
                          rd=[bwfc, bcv], wr=[bkb], inc=(k == 7))
                    LD(FB[:, 0:512], XN1_f[m * 128:(m + 1) * 128, tc * TC:(tc + 1) * TC],
                       rd=dbt(XN1_f, tc * TC, (tc + 1) * TC), wr=[f_xn1])
                    O("vector", lambda e, bk=bk, m=m: e.tensor_tensor(
                        FA[:, m * 512:(m + 1) * 512], bk[:, :], FB[:, 0:512], ALU.add),
                      rd=[bkb, f_xn1], wr=[f_v])
                for m in range(8):
                    O("scalar", lambda e, m=m: e.activation(
                        FB[:, 1024:1536], FA[:, m * 512:(m + 1) * 512], AF.Square),
                      rd=[f_v], wr=[f_sq])
                    O("tensor", lambda e, m=m: e.matmul(
                        P0[0:1, :], onesv[:], FA[:, m * 512:(m + 1) * 512],
                        start=(m == 0), stop=(m == 7)),
                      rd=[f_v, pers["onesv"]], wr=[pb(P0)], inc=(m == 7))
                    O("tensor", lambda e, m=m: e.matmul(
                        P1[0:1, :], onesv[:], FB[:, 1024:1536],
                        start=(m == 0), stop=(m == 7)),
                      rd=[f_sq, pers["onesv"]], wr=[pb(P1)], inc=(m == 7))
                mu = FB[0:1, 2048:2560]
                var = FB[0:1, 2560:3072]
                sd = FB[0:1, 3072:3584]
                rr = FB[0:1, 3584:4096]
                O("vector", lambda e: e.tensor_copy(mu, P0[0:1, :]), rd=[pb(P0)], wr=[rows])
                O("vector", lambda e: e.tensor_tensor(var, mu, mu, ALU.mult),
                  rd=[rows], wr=[rows])
                O("vector", lambda e: e.tensor_tensor(var, P1[0:1, :], var, ALU.subtract),
                  rd=[pb(P1), rows], wr=[rows])
                O("scalar", lambda e: e.activation(sd, var, AF.Sqrt, bias=epst[0:1, 1:2]),
                  rd=[rows, pers["epst"]], wr=[rows])
                O("vector", lambda e: e.reciprocal(rr, sd), rd=[rows], wr=[rows])
                O("tensor", lambda e: e.matmul(PY[:, 0:512], ones1[:], mu,
                                               start=True, stop=True),
                  rd=[rows, pers["ones1"]], wr=[pb(PY)])
                O("tensor", lambda e: e.matmul(PY[:, 512:1024], ones1[:], rr,
                                               start=True, stop=True),
                  rd=[rows, pers["ones1"]], wr=[pb(PY)])
                for m in range(8):
                    LD(FB[:, 512:1024], X1[m * 128:(m + 1) * 128, tc * TC:(tc + 1) * TC],
                       rd=dbt(X1, tc * TC, (tc + 1) * TC), wr=[f_x1t])
                    O("vector", lambda e, m=m: e.tensor_tensor(
                        FA[:, m * 512:(m + 1) * 512], FA[:, m * 512:(m + 1) * 512],
                        PY[:, 0:512], ALU.subtract), rd=[pb(PY), f_v], wr=[f_v])
                    O("vector", lambda e, m=m: e.scalar_tensor_tensor(
                        FA[:, m * 512:(m + 1) * 512], FA[:, m * 512:(m + 1) * 512],
                        g_col[:, m:m + 1], PY[:, 512:1024], ALU.mult, ALU.mult),
                      rd=[pb(PY), pers["g_col"], f_v], wr=[f_v])
                    O("vector", lambda e, m=m: e.scalar_tensor_tensor(
                        FB[:, 1536:2048], FA[:, m * 512:(m + 1) * 512],
                        bt_col[:, m:m + 1], FB[:, 512:1024], ALU.add, ALU.add),
                      rd=[f_v, pers["bt_col"], f_x1t], wr=[f_vx])
                    ST(VLX[m * 128:(m + 1) * 128, tc * TC:(tc + 1) * TC],
                       FB[:, 1536:2048], rd=[f_vx], wr=dbt(VLX, tc * TC, (tc + 1) * TC))

        if upto >= 12:
            bcn1 = AB1.take(0, 8192)
            bcn2 = AB1.take(8192, 12288)
            bcn3 = AB1.take(12288, 13312)
            bob = AB1.take(13312, 17408)
            bob2 = [bob, bob]
            bh1c = AB1.take(17408, 21504)
            bh2c = AB1.take(21504, 23552)
            bh3c = AB1.take(23552, 24576)
            fwl = AFA.take(0, 1024)
            for c in range(8):
                LD(FA[:, 0:1024], cnw1[c * 128:(c + 1) * 128, :], wr=[fwl])
                O("vector", lambda e, c=c: e.tensor_copy(b1v(c * 1024, 128, 1024),
                                                         FA[:, 0:1024]),
                  rd=[fwl], wr=[bcn1])
            for c in range(8):
                LD(FA[:, 0:512], cnw2[c * 128:(c + 1) * 128, :], wr=[fwl])
                O("vector", lambda e, c=c: e.tensor_copy(b1v(8192 + c * 512, 128, 512),
                                                         FA[:, 0:512]),
                  rd=[fwl], wr=[bcn2])
            for c in range(4):
                LD(FA[:, 0:256], cnw3[c * 128:(c + 1) * 128, :], wr=[fwl])
                O("vector", lambda e, c=c: e.tensor_copy(b1v(12288 + c * 256, 128, 256),
                                                         FA[:, 0:256]),
                  rd=[fwl], wr=[bcn3])
            f_oc2 = [AFA.take(1024, 3072), AFA.take(3072, 5120)]
            f_t2 = [AFA.take(5120, 5632), AFA.take(5632, 6144)]
            f_tf = AFB.take(0, 512)
            twist_src = [4, 5, 6, 7, 0, 1, 2, 3]
            twist_sgn = [1.0, 1.0, -1.0, -1.0, -1.0, -1.0, 1.0, 1.0]
            for cyc in range(3):
                cur_f = (XN1_f, OUTCA, OUTCB)[cyc]
                cur_b = (XN1_b, OUTBA, OUTBB)[cyc]
                nxt_f = (OUTCA, OUTCB, None)[cyc]
                nxt_b = (OUTBA, OUTBB, None)[cyc]
                for tc in range(NTC):
                    for h in range(2):
                        LD(b1v(13312 + h * 2048, 128, 4, 512, 512, 1),
                           cur_b.rearrange("(a p) t -> p a t", p=128)
                           [:, h * 4:(h + 1) * 4, tc * TC:(tc + 1) * TC],
                           rd=dbt(cur_b, tc * TC, (tc + 1) * TC), wr=[bob])
                    for m in range(8):
                        bk, bkb = [(P0, pb(P0)), (P1, pb(P1)), (P2, pb(P2)), (P3, pb(P3))][m % 4]
                        for k in range(8):
                            O("tensor", lambda e, m=m, k=k, bk=bk: e.matmul(
                                bk[:, :], b1v(k * 1024 + m * 128, 128, 128),
                                b1v(13312 + k * 512, 128, 512),
                                start=(k == 0), stop=(k == 7)),
                              rd=[bcn1, bob], wr=[bkb], inc=(k == 7))
                        O("scalar", lambda e, bk=bk, m=m: e.activation(
                            b1v(17408 + m * 512, 128, 512), bk[:, :], AF.Gelu,
                            bias=cnb1_sb[:, m:m + 1]),
                          rd=[bkb, pers["cnb1_sb"]], wr=[bh1c])
                    for m in range(4):
                        bk, bkb = ((P2, pb(P2)), (P3, pb(P3)))[m % 2]
                        for k in range(8):
                            O("tensor", lambda e, m=m, k=k, bk=bk: e.matmul(
                                bk[:, :], b1v(8192 + k * 512 + m * 128, 128, 128),
                                b1v(17408 + k * 512, 128, 512),
                                start=(k == 0), stop=(k == 7)),
                              rd=[bcn2, bh1c], wr=[bkb], inc=(k == 7))
                        O("scalar", lambda e, bk=bk, m=m: e.activation(
                            b1v(21504 + m * 512, 128, 512), bk[:, :], AF.Gelu,
                            bias=cnb2_sb[:, m:m + 1]),
                          rd=[bkb, pers["cnb2_sb"]], wr=[bh2c])
                    for m in range(2):
                        bk, bkb = ((P0, pb(P0)), (P1, pb(P1)))[m % 2]
                        for k in range(4):
                            O("tensor", lambda e, m=m, k=k, bk=bk: e.matmul(
                                bk[:, :], b1v(12288 + k * 256 + m * 128, 128, 128),
                                b1v(21504 + k * 512, 128, 512),
                                start=(k == 0), stop=(k == 3)),
                              rd=[bcn3, bh2c], wr=[bkb], inc=(k == 3))
                        O("scalar", lambda e, bk=bk, m=m: e.activation(
                            b1v(23552 + m * 512, 128, 512), bk[:, :], AF.Gelu,
                            bias=cnb3_sb[:, m:m + 1]),
                          rd=[bkb, pers["cnb3_sb"]], wr=[bh3c])
                    for k in range(2):
                        O("tensor", lambda e, k=k: e.matmul(
                            P2[0:1, :], cn4_sb[:, k:k + 1], b1v(23552 + k * 512, 128, 512),
                            start=(k == 0), stop=(k == 1)),
                          rd=[pers["cn4_sb"], bh3c], wr=[pb(P2)], inc=(k == 1))
                    O("scalar", lambda e: e.activation(FB[0:1, 0:512], P2[0:1, :],
                                                       AF.Sigmoid, bias=sc_sb[0:1, 4:5]),
                      rd=[pb(P2), pers["sc_sb"]], wr=[f_tf])
                    O("vector", lambda e: e.tensor_scalar(
                        FB[0:1, 0:512], FB[0:1, 0:512], sc_sb[0:1, 3:4],
                        sc_sb[0:1, 2:3], ALU.mult, ALU.add),
                      rd=[pers["sc_sb"], f_tf], wr=[f_tf])
                    O("tensor", lambda e: e.matmul(P4[:, :], ones1[:], FB[0:1, 0:512],
                                                   start=True, stop=True),
                      rd=[f_tf, pers["ones1"]], wr=[pb(P4)])
                    def upd_load(dc):
                        p = dc % 2
                        LD(FA[:, 1024 + p * 2048:1536 + p * 2048],
                           cur_f[twist_src[dc] * 128:(twist_src[dc] + 1) * 128,
                                 tc * TC:(tc + 1) * TC],
                           rd=dbt(cur_f, tc * TC, (tc + 1) * TC), wr=[f_oc2[p]])
                        LD(FA[:, 1536 + p * 2048:2048 + p * 2048],
                           cur_f[dc * 128:(dc + 1) * 128, tc * TC:(tc + 1) * TC],
                           rd=dbt(cur_f, tc * TC, (tc + 1) * TC), wr=[f_oc2[p]])
                        if cyc == 2:
                            LD(FA[:, 2048 + p * 2048:2560 + p * 2048],
                               VLX[dc * 128:(dc + 1) * 128, tc * TC:(tc + 1) * TC],
                               rd=dbt(VLX, tc * TC, (tc + 1) * TC), wr=[f_oc2[p]])

                    upd_load(0)
                    for dc in range(8):
                        p = dc % 2
                        if dc < 7:
                            upd_load(dc + 1)
                        O("vector", lambda e, dc=dc, p=p: e.scalar_tensor_tensor(
                            FA[:, 5120 + p * 512:5632 + p * 512],
                            FA[:, 1024 + p * 2048:1536 + p * 2048], twist_sgn[dc],
                            P4[:, :], ALU.mult, ALU.mult),
                          rd=[f_oc2[p], pb(P4)], wr=[f_t2[p]])
                        O("vector", lambda e, p=p: e.tensor_tensor(
                            FA[:, 5120 + p * 512:5632 + p * 512],
                            FA[:, 5120 + p * 512:5632 + p * 512],
                            FA[:, 1536 + p * 2048:2048 + p * 2048],
                            ALU.add), rd=[f_oc2[p], f_t2[p]], wr=[f_t2[p]])
                        if cyc < 2:
                            ST(nxt_f[dc * 128:(dc + 1) * 128, tc * TC:(tc + 1) * TC],
                               FA[:, 5120 + p * 512:5632 + p * 512],
                               rd=[f_t2[p]], wr=dbt(nxt_f, tc * TC, (tc + 1) * TC))
                            O("scalar", lambda e, p=p: e.activation(
                                b1v(13312 + p * 512, 128, 512),
                                FA[:, 5120 + p * 512:5632 + p * 512], AF.Copy),
                              rd=[f_t2[p]], wr=[bob2[p]])
                            ST(nxt_b[dc * 128:(dc + 1) * 128, tc * TC:(tc + 1) * TC],
                               b1v(13312 + p * 512, 128, 512), rd=[bob2[p]],
                               wr=dbt(nxt_b, tc * TC, (tc + 1) * TC))
                        else:
                            O("vector", lambda e, p=p: e.tensor_tensor(
                                FA[:, 5120 + p * 512:5632 + p * 512],
                                FA[:, 5120 + p * 512:5632 + p * 512],
                                FA[:, 2048 + p * 2048:2560 + p * 2048],
                                ALU.add), rd=[f_oc2[p], f_t2[p]], wr=[f_t2[p]])
                            ST(out_ext[dc * 128:(dc + 1) * 128, tc * TC:(tc + 1) * TC],
                               FA[:, 5120 + p * 512:5632 + p * 512], rd=[f_t2[p]],
                               wr=dbt(out_ext, tc * TC, (tc + 1) * TC))

        for sm_ in DSEMS:
            if P.cnt[sm_] > 0:
                P.pending["sync"][(sm_, P.seg[sm_])] = P.cnt[sm_]
        O("sync", lambda e: e.nop())

        waited_vals = {}
        for e_ in ENGS:
            for (w_, fn_, sm_, n_, sg_, v_) in P.ops[e_]:
                for ((psm, pseg), pv) in w_:
                    if psm in CSEMS:
                        waited_vals.setdefault((psm, pseg), set()).add(pv)
        import bisect as _bisect
        wsorted = {k: sorted(v) for k, v in waited_vals.items()}

        def remap(sm_, sg_, v_):
            lst = wsorted.get((sm_, sg_), [])
            return _bisect.bisect_right(lst, v_)

        for e_ in ENGS:
            newops = []
            for (w_, fn_, sm_, n_, sg_, v_) in P.ops[e_]:
                w2 = [((psm, pseg), remap(psm, pseg, pv) if psm in CSEMS else pv)
                      for ((psm, pseg), pv) in w_]
                if sm_ in CSEMS:
                    n_ = 1 if v_ in waited_vals.get((sm_, sg_), ()) else 0
                    v_ = remap(sm_, sg_, v_)
                newops.append((w2, fn_, sm_, n_, sg_, v_))
            P.ops[e_] = newops

        def mk_section(ename):
            def body(eng):
                for (waits, fn, sm, n, seg, v_) in P.ops[ename]:
                    for ((psm, pseg), v) in waits:
                        eng.wait_ge(sems[psm][pseg], v)
                    instr = fn(eng)
                    if n:
                        instr.then_inc(sems[sm][seg], n)
            return body

        for ename in ENGS:
            getattr(block, ename)(mk_section(ename))

    return nc


_NC_CACHE = {}


def _wpe_blockdiag(wpe, c0):
    wpe = np.asarray(wpe)
    out = np.zeros((128, 63), np.float32)
    out[0:64, 0:31] = wpe[:, c0:c0 + 31]
    out[64:128, 32:63] = wpe[:, c0:c0 + 31]
    return out


def _stack32(v31):
    out = np.zeros((128, 1), np.float32)
    for j in range(4):
        out[j * 32:j * 32 + 31, 0] = v31
    return out


def _stt_layout(stt):
    stt = np.asarray(stt).reshape(NH, KW)
    out = np.zeros((128, 8), np.float32)
    for dc in range(8):
        for hh in range(2):
            base = (dc % 2) * 64 + hh * 32
            out[base:base + 31, dc] = stt[2 * dc + hh]
    return out


def _col128(v):
    v = np.asarray(v).reshape(-1, 128)
    return np.ascontiguousarray(v.T)


def _chunk128(a, inner):
    a = np.asarray(a).reshape(-1, 128, inner)
    return np.ascontiguousarray(a.transpose(1, 0, 2).reshape(128, -1))


def kernel(**inputs):
    n = 8
    x = np.asarray(inputs["L0_attn_x"])

    def t(a):
        return np.ascontiguousarray(np.asarray(a).T)

    shared = {
        "w_ip": t(inputs["L0_attn_in_proj_w"]),
        "convw": _chunk128(inputs["L0_attn_conv_w"], 4),
        "convb": _col128(inputs["L0_attn_conv_b"]),
        "w_xp": _chunk128(t(inputs["L0_attn_x_proj_w"]), 96),
        "w_dt": t(inputs["L0_attn_dt_proj_w"]),
        "dtb": _col128(inputs["L0_attn_dt_proj_b"]),
        "alog": _chunk128(inputs["L0_attn_A_log"], NST),
        "dvec": _col128(inputs["L0_attn_D"]),
        "w_op": t(inputs["L0_attn_out_proj_w"]),
        "normw": _col128(inputs["L0_attn_norm_w"]),
        "w1t": t(inputs["L0_ffn_w1"]),
        "b1": _col128(inputs["L0_ffn_b1"]),
        "w2t": t(inputs["L0_ffn_w2"]),
        "b2": _col128(inputs["L0_ffn_b2"]),
        "wd2pe": _wpe_blockdiag(inputs["L1_attn_W_pe"], 0),
        "wg2pe": _wpe_blockdiag(inputs["L1_attn_W_pe"], 31),
        "bpe63d": _stack32(np.asarray(inputs["L1_attn_b_pe"])[0:31]),
        "bpe63g": _stack32(np.asarray(inputs["L1_attn_b_pe"])[31:62]),
        "stt63": _stt_layout(inputs["L1_attn_stt_energy"]),
        "wfct": t(inputs["L1_attn_W_fc"]),
        "gamma": _col128(inputs["L1_attn_gamma"]),
        "beta": _col128(inputs["L1_attn_beta"]),
        "cnw1": np.ascontiguousarray(inputs["L1_ffn_cn_w1"]),
        "cnb1": _col128(inputs["L1_ffn_cn_b1"]),
        "cnw2": np.ascontiguousarray(inputs["L1_ffn_cn_w2"]),
        "cnb2": _col128(inputs["L1_ffn_cn_b2"]),
        "cnw3": np.ascontiguousarray(inputs["L1_ffn_cn_w3"]),
        "cnb3": _col128(inputs["L1_ffn_cn_b3"]),
        "cnw4": _col128(inputs["L1_ffn_cn_w4"]),
        "cnb4": np.asarray(inputs["L1_ffn_cn_b4"]).reshape(1, 1),
        "gcw1": np.ascontiguousarray(inputs["L1_ffn_gc_w1"]),
        "gcb1": _col128(inputs["L1_ffn_gc_b1"]),
        "gcw2": np.ascontiguousarray(inputs["L1_ffn_gc_w2"]),
        "gcb2": _col128(inputs["L1_ffn_gc_b2"]),
        "gcw3": _col128(inputs["L1_ffn_gc_w3"]),
        "gcb3": np.asarray(inputs["L1_ffn_gc_b3"]).reshape(1, 1),
        "arange_": np.asarray(inputs["L1_ffn_adaptive_range"]).reshape(1, 1),
    }
    shared = {k: np.ascontiguousarray(v, dtype=np.float32) for k, v in shared.items()}
    in_maps = []
    for c in range(n):
        m = dict(shared)
        m["x"] = np.ascontiguousarray(x[c].T)
        in_maps.append(m)

    nc = build()
    res = run_bass_kernel_spmd(nc, in_maps, core_ids=list(range(n)))
    out = np.stack([np.asarray(r["out"]).T for r in res.results])
    return out.astype(np.float32)


# version 05 (final)
# speedup vs baseline: 1.0485x; 1.0127x over previous
import os
import os
import contextlib
import numpy as np
import concourse.bass as bass
import concourse.mybir as mybir
from concourse.bass_utils import run_bass_kernel_spmd

F32 = mybir.dt.float32
BF16 = mybir.dt.bfloat16
AF = mybir.ActivationFunctionType
ALU = mybir.AluOpType
AX = mybir.AxisListType

T = 4096
D = 1024
DI = 2048
H4 = 4096
NST = 16
NH = 16
HD = 64
KW = 31
TC = 512
NTC = T // TC
TB = 1024
NTB = T // TB

ENGS = ("sync", "vector", "scalar", "tensor", "gpsimd")
SEM_SEG_LIMIT = 30000


class Buf:
    __slots__ = ("w", "rs", "dsem", "is_psum")

    def __init__(self):
        self.w = None
        self.rs = []
        self.dsem = None
        self.is_psum = False


class Arena:
    def __init__(self):
        self.regions = []

    def take(self, start, end):
        b = Buf()
        for item in self.regions[:]:
            s0, e0, ob = item
            if s0 < end and start < e0:
                if ob.w is not None:
                    b.rs.append(ob.w)
                b.rs.extend(ob.rs)
                self.regions.remove(item)
                if s0 < start:
                    self.regions.append((s0, start, ob))
                if e0 > end:
                    self.regions.append((end, e0, ob))
        self.regions.append((start, end, b))
        return b


CSEMS = ("vector", "scalar", "tensor", "gpsimd_c")
N_SYNC_SEMS = 40
N_GP_SEMS = 30
DSEMS = tuple(f"d{i}" for i in range(N_SYNC_SEMS)) + tuple(f"g{i}" for i in range(N_GP_SEMS))
SEMS = CSEMS + DSEMS


def _sem_engine(sem):
    if sem == "gpsimd_c":
        return "gpsimd"
    return sem if sem in CSEMS else None


class Prog:
    def __init__(self):
        self.ops = {e: [] for e in ENGS}
        self.cnt = {sm: 0 for sm in SEMS}
        self.seg = {sm: 0 for sm in SEMS}
        self.segfinal = {sm: {} for sm in SEMS}
        self.pending = {e: {} for e in ENGS}
        self.waited = {e: {} for e in ENGS}
        self.dma_rr = {"sync": 0, "gpsimd": 0, "scalar": 0, "vector": 0, "tensor": 0}

    def next_dma_sem(self, q):
        if q == "gpsimd":
            sm = f"g{self.dma_rr[q] % N_GP_SEMS}"
        else:
            sm = f"d{self.dma_rr['sync'] % N_SYNC_SEMS}"
            self.dma_rr['sync'] += 1
            return sm
        self.dma_rr[q] += 1
        return sm

    def _wait(self, e, key):
        sem, seg, val = key
        if val <= 0:
            return
        if _sem_engine(sem) == e and e != "gpsimd":
            if not (seg == self.seg[sem] and val >= self.cnt[sem] - 1):
                return
        if sem not in CSEMS:
            val = self.cnt[sem] if seg == self.seg[sem] else self.segfinal[sem][seg]
        assert seg < self.seg[sem] or (seg == self.seg[sem] and val <= self.cnt[sem])
        if self.waited[e].get((sem, seg), 0) >= val:
            return
        p = self.pending[e]
        p[(sem, seg)] = max(p.get((sem, seg), 0), val)

    def emit(self, e, fn, *, dma=False, inc=True, dmasem=None):
        w = sorted(self.pending[e].items())
        for (ps, v) in w:
            self.waited[e][ps] = max(self.waited[e].get(ps, 0), v)
        self.pending[e] = {}
        if dma:
            sm = dmasem
            assert sm is not None
        else:
            sm = {"sync": None, "gpsimd": "gpsimd_c"}.get(e, e)
        n = (16 if dma else 1) if inc else 0
        if sm is None:
            n = 0
        if n and self.cnt[sm] + n > SEM_SEG_LIMIT:
            assert sm in CSEMS, f"DMA sem overflow on {sm}"
            self.segfinal[sm][self.seg[sm]] = self.cnt[sm]
            self.seg[sm] += 1
            self.cnt[sm] = 0
            assert self.seg[sm] < 2, "increase semaphore segments"
        if n:
            self.cnt[sm] += n
        self.ops[e].append((w, fn, sm, n, self.seg[sm] if sm else 0,
                            self.cnt[sm] if sm else 0))
        if sm is None:
            return ("vector", 0, 0)
        return (sm, self.seg[sm], self.cnt[sm])


def op(P, e, fn, rd=(), wr=(), dma=False, inc=True):
    inc = True
    for b in rd:
        if b.w is not None:
            P._wait(e, b.w)
    for b in wr:
        if b.w is not None:
            P._wait(e, b.w)
        for r in b.rs:
            P._wait(e, r)
    dmasem = None
    if dma:
        b0 = wr[0]
        if b0.dsem is None:
            b0.dsem = {}
        if e not in b0.dsem:
            b0.dsem[e] = P.next_dma_sem(e)
        dmasem = b0.dsem[e]
    key = P.emit(e, fn, dma=dma, inc=inc, dmasem=dmasem)
    for b in rd:
        b.rs.append(key)
    for b in wr:
        b.w = key
        b.rs = []
    return key


def build(debug=False, upto=99):
    nc = bass.Bass("TRN2", target_bir_lowering=False)
    P = Prog()

    def ein(name, shape):
        return nc.dram_tensor(name, shape, F32, kind="ExternalInput").ap()

    x_in = ein("x", [D, T])
    w_ip = ein("w_ip", [D, 2 * DI])
    convw = ein("convw", [128, 64])
    convb = ein("convb", [128, 16])
    w_xp = ein("w_xp", [128, 1536])
    w_dt = ein("w_dt", [64, DI])
    dtb = ein("dtb", [128, 16])
    alog = ein("alog", [128, 256])
    dvec = ein("dvec", [128, 16])
    w_op = ein("w_op", [DI, D])
    normw = ein("normw", [128, 8])
    w1t = ein("w1t", [D, H4])
    b1 = ein("b1", [128, 32])
    w2t = ein("w2t", [H4, D])
    b2 = ein("b2", [128, 8])
    wd2pe = ein("wd2pe", [128, 63])
    wg2pe = ein("wg2pe", [128, 63])
    bpe63d = ein("bpe63d", [128, 1])
    bpe63g = ein("bpe63g", [128, 1])
    stt63 = ein("stt63", [128, 8])
    wfct = ein("wfct", [D, D])
    gamma = ein("gamma", [128, 8])
    beta = ein("beta", [128, 8])
    cnw1 = ein("cnw1", [D, D])
    cnb1 = ein("cnb1", [128, 8])
    cnw2 = ein("cnw2", [D, 512])
    cnb2 = ein("cnb2", [128, 4])
    cnw3 = ein("cnw3", [512, 256])
    cnb3 = ein("cnb3", [128, 2])
    cnw4 = ein("cnw4", [128, 2])
    cnb4 = ein("cnb4", [1, 1])
    gcw1 = ein("gcw1", [D, 512])
    gcb1 = ein("gcb1", [128, 4])
    gcw2 = ein("gcw2", [512, 256])
    gcb2 = ein("gcb2", [128, 2])
    gcw3 = ein("gcw3", [128, 2])
    gcb3 = ein("gcb3", [1, 1])
    arange_ = ein("arange_", [1, 1])

    out_ext = nc.dram_tensor("out", [D, T], F32, kind="ExternalOutput").ap()

    def scr(name, shape, dt=F32):
        if debug:
            return nc.dram_tensor(name, shape, dt, kind="ExternalOutput").ap()
        return nc.dram_tensor(name, shape, dt).ap()

    XN_f = scr("XN_f", [D, T])
    XN_b = scr("XN_b", [D, T], BF16)
    RN_b = scr("RN_b", [D, T], BF16)
    U_f = scr("U_f", [DI, T])
    U_b = scr("U_b", [DI, T], BF16)
    G_f = scr("G_f", [DI, T])
    DLT = scr("DLT", [DI, T])
    DU_b = scr("DU_b", [DI, T], BF16)
    BROW = scr("BROW", [NST, T], BF16)
    CROW = scr("CROW", [NST, T], BF16)
    RIROW = scr("RIROW", [1, T])
    YG_b = scr("YG_b", [DI, T], BF16)
    X1p = scr("X1p", [D, T])
    X1 = scr("X1", [D, T])
    BW1 = scr("BW1", [D, H4], BF16)
    BW2 = scr("BW2", [H4, D], BF16)
    H1 = scr("H1", [H4, T], BF16)
    XN1_f = scr("XN1_f", [D, T])
    XN1_b = scr("XN1_b", [D, T], BF16)
    CIT = scr("CIT", [T, D], BF16)
    VLX = scr("VLX", [D, T])
    OUTCA = scr("OUTCA", [D, T])
    OUTBA = scr("OUTBA", [D, T], BF16)
    OUTCB = scr("OUTCB", [D, T])
    OUTBB = scr("OUTBB", [D, T], BF16)
    ZBB_BS = 128 * 160
    ZBB_HS = 64 + 32 * ZBB_BS
    ZBB = scr("ZBB", [NH, ZBB_HS], BF16)
    ZPB_BS = 128 * 257
    ZPB_HS = 64 + 32 * ZPB_BS
    ZPB = scr("ZPB", [NH, ZPB_HS], BF16)

    dbuf = {}
    dparts = {}
    for t_ in (XN_f, XN_b, RN_b, U_f, U_b, G_f, DLT, DU_b, BROW, CROW, RIROW,
               YG_b, X1p, X1, BW1, BW2, H1, XN1_f, XN1_b, CIT, VLX,
               OUTCA, OUTBA, OUTCB, OUTBB, ZBB, ZPB, out_ext):
        dbuf[id(t_)] = Buf()
    for t_ in (XN_f, XN_b, RN_b, U_f, U_b, G_f, DLT, DU_b, BROW, CROW,
               YG_b, X1p, X1, H1, XN1_f, XN1_b, CIT, VLX,
               OUTCA, OUTBA, OUTCB, OUTBB, out_ext):
        dparts[id(t_)] = [Buf() for _ in range(8)]

    def db(t_):
        return dbuf[id(t_)]

    def dbt(t_, lo, hi):
        ps = dparts.get(id(t_))
        if ps is None:
            return [dbuf[id(t_)]]
        p0 = lo * 8 // T
        p1 = (hi - 1) * 8 // T
        return ps[p0:p1 + 1]

    ctx = contextlib.ExitStack()
    with ctx:
        def sbuf(name, shape, dt=F32):
            return ctx.enter_context(nc.sbuf_tensor(name, shape, dt))
        def psum(name, shape, dt=F32):
            return ctx.enter_context(nc.psum_tensor(name, shape, dt))

        B1 = sbuf("B1", [128, 32768], BF16)
        B2 = sbuf("B2", [128, 32768], BF16)
        FA = sbuf("FA", [128, 6656])
        FB = sbuf("FB", [128, 6144])
        AB1 = Arena()
        AB2 = Arena()
        AFA = Arena()
        AFB = Arena()

        def b1v(o, p_, d1, s1=None, d2=None, s2=None):
            if d2 is None:
                return bass.AP(B1, o, [[32768, p_], [s1 or 1, d1]])
            return bass.AP(B1, o, [[32768, p_], [s1, d1], [s2, d2]])

        def b2v(o, p_, d1, s1=None, d2=None, s2=None):
            if d2 is None:
                return bass.AP(B2, o, [[32768, p_], [s1 or 1, d1]])
            return bass.AP(B2, o, [[32768, p_], [s1, d1], [s2, d2]])

        pers = {}

        def mk(name, shape, dt=F32):
            t_ = sbuf(name, shape, dt)
            pers[name] = Buf()
            return t_

        onesv = mk("onesv", [128, 1])
        ones1 = mk("ones1", [1, 128])
        ones1b = mk("ones1b", [1, 128], BF16)
        onesDIb = mk("onesDIb", [128, 1], BF16)
        onesH4b = mk("onesH4b", [128, 1], BF16)
        idb = mk("idb", [128, 128], BF16)
        a_sb = mk("a_sb", [128, 16, 16])
        d_sb = mk("d_sb", [128, 16])
        cb_sb = mk("cb_sb", [128, 16])
        cw_sb = mk("cw_sb", [128, 16, 4])
        nw_sb = mk("nw_sb", [128, 8])
        g_col = mk("g_col", [128, 8])
        bt_col = mk("bt_col", [128, 8])
        b1_sb = mk("b1_sb", [128, 32])
        b2_sb = mk("b2_sb", [128, 8])
        dtb_sb = mk("dtb_sb", [128, 16])
        wd_sb = mk("wd_sb", [128, 63], BF16)
        wg_sb = mk("wg_sb", [128, 63], BF16)
        bpe_d = mk("bpe_d", [128, 1])
        bpe_g = mk("bpe_g", [128, 1])
        stt_sb = mk("stt_sb", [128, 8])
        ob124 = mk("ob124", [128, 4])
        ob2t_lo = mk("ob2t_lo", [2, 128])
        ob2t_hi = mk("ob2t_hi", [2, 128])
        ritok = mk("ritok", [128, 32])
        wmask = mk("wmask", [128, 30], BF16)
        cn4_sb = mk("cn4_sb", [128, 2], BF16)
        gc3_sb = mk("gc3_sb", [128, 2], BF16)
        cnb1_sb = mk("cnb1_sb", [128, 8])
        cnb2_sb = mk("cnb2_sb", [128, 4])
        cnb3_sb = mk("cnb3_sb", [128, 2])
        gcb1_sb = mk("gcb1_sb", [128, 4])
        gcb2_sb = mk("gcb2_sb", [128, 2])
        sc_sb = mk("sc_sb", [1, 16])
        gvec = mk("gvec", [128, 8])
        gvec_b = mk("gvec_b", [128, 8], BF16)
        tails = mk("tails", [128, 16, 3])
        carry = mk("carry", [128, 16, 16])
        abuf = sbuf("abuf", [128, 2, TB])
        pers["abuf0"] = Buf()
        pers["abuf1"] = Buf()
        dub_sb = mk("dub_sb", [128, TB], BF16)
        pers["rowbuf"] = Buf()
        epst = mk("epst", [128, 4])
        zbig = mk("zbig", [128, 512], BF16)

        PY = psum("PY", [128, 1024])
        P0 = psum("P0", [128, 512])
        P1 = psum("P1", [128, 512])
        P2 = psum("P2", [128, 512])
        P3 = psum("P3", [128, 512])
        P4 = psum("P4", [128, 512])
        PBm = psum("PBm", [128, 1024], BF16)
        pbuf = {id(t_): Buf() for t_ in (PY, P0, P1, P2, P3, P4)}
        pb_half = [Buf(), Buf()]

        def pb(t_):
            return pbuf[id(t_)]

        sems = {}
        for sm in CSEMS:
            sems[sm] = [ctx.enter_context(nc.semaphore(f"sem_{sm}_{i}")) for i in range(2)]
        for sm in DSEMS:
            sems[sm] = [ctx.enter_context(nc.semaphore(f"sem_{sm}"))]
        block = ctx.enter_context(nc.Block())

        for b_ in list(pbuf.values()) + pb_half:
            b_.is_psum = True

        psum_rd = {"e": None, "key": None}

        tr_hist = []
        tr_base = [None]

        def TR(fn, rd=(), wr=(), base=0):
            if tr_hist:
                if base != tr_base[0] and tr_hist[-1][0] is not None:
                    P._wait("tensor", tr_hist[-1][0])
                elif len(tr_hist) >= 2 and tr_hist[-2][0] is not None:
                    P._wait("tensor", tr_hist[-2][0])
            tr_base[0] = base
            k = O("tensor", fn, rd=rd, wr=wr)
            slot = [None]
            tr_hist.append(slot)
            return slot

        def O(e, fn, rd=(), wr=(), dma=False, inc=True):
            touches_psum = any(getattr(b_, "is_psum", False) for b_ in rd)
            if touches_psum and e in ("vector", "scalar"):
                if psum_rd["e"] is not None and psum_rd["e"] != e:
                    P._wait(e, psum_rd["key"])
            k = op(P, e, fn, rd=rd, wr=wr, dma=dma, inc=inc)
            if touches_psum and e in ("vector", "scalar"):
                psum_rd["e"] = e
                psum_rd["key"] = k
            return k

        def LD(dst, src, rd=(), wr=()):
            return O("sync", lambda e, d=dst, s=src: e.dma_start(d, s),
                     rd=rd, wr=wr, dma=True)

        def ST(dst, src, rd=(), wr=(), q="gpsimd"):
            return O(q, lambda e, d=dst, s=src: e.dma_start(d, s),
                     rd=rd, wr=wr, dma=True)

        smalls = [
            (a_sb[:].rearrange("p a b -> p (a b)"), alog, "a_sb"),
            (d_sb[:], dvec, "d_sb"),
            (cb_sb[:], convb, "cb_sb"),
            (cw_sb[:].rearrange("p a b -> p (a b)"), convw, "cw_sb"),
            (nw_sb[:], normw, "nw_sb"),
            (g_col[:], gamma, "g_col"),
            (bt_col[:], beta, "bt_col"),
            (b1_sb[:], b1, "b1_sb"),
            (b2_sb[:], b2, "b2_sb"),
            (dtb_sb[:], dtb, "dtb_sb"),
            (stt_sb[:], stt63, "stt_sb"),
            (bpe_d[:], bpe63d, "bpe_d"),
            (bpe_g[:], bpe63g, "bpe_g"),
            (cnb1_sb[:], cnb1, "cnb1_sb"),
            (cnb2_sb[:], cnb2, "cnb2_sb"),
            (cnb3_sb[:], cnb3, "cnb3_sb"),
            (gcb1_sb[:], gcb1, "gcb1_sb"),
            (gcb2_sb[:], gcb2, "gcb2_sb"),
            (sc_sb[0:1, 4:5], cnb4, "sc_sb"),
            (sc_sb[0:1, 5:6], gcb3, "sc_sb"),
            (sc_sb[0:1, 1:2], arange_, "sc_sb"),
        ]
        for dst, src, nm in smalls:
            LD(dst, src, wr=[pers[nm]])

        for t_, v, nm in ((onesv, 1.0 / 1024.0, "onesv"), (ones1, 1.0, "ones1"),
                          (ones1b, 1.0, "ones1b"), (onesDIb, 1.0 / 1024.0, "onesDIb"),
                          (onesH4b, 1.0 / 4096.0, "onesH4b")):
            O("gpsimd", lambda e, t_=t_, v=v: e.memset(t_[:], v), wr=[pers[nm]])
        O("gpsimd", lambda e: e.memset(idb[:], 0.0), wr=[pers["idb"]])
        O("gpsimd", lambda e: e.affine_select(out=idb[:], in_=idb[:],
                                              compare_op=ALU.not_equal, fill=1.0, base=0,
                                              pattern=[[-1, 128]], channel_multiplier=1),
          wr=[pers["idb"]])
        O("gpsimd", lambda e: e.memset(wmask[:], 1.0), wr=[pers["wmask"]])
        O("gpsimd", lambda e: e.affine_select(
            out=wmask[0:31, :], in_=wmask[0:31, :],
            compare_op=ALU.is_ge, fill=0.0, base=-30,
            pattern=[[1, 30]], channel_multiplier=1), wr=[pers["wmask"]])
        for j in range(1, 4):
            O("gpsimd", lambda e, j=j: e.dma_start(wmask[j * 32:j * 32 + 31, :],
                                                   wmask[0:31, :]),
              rd=[pers["wmask"]], wr=[pers["wmask"]], dma=True)
        O("gpsimd", lambda e: e.memset(ob124[:], 1.0), wr=[pers["ob124"]])
        O("gpsimd", lambda e: e.affine_select(
            out=ob124[:], in_=ob124[:], compare_op=ALU.is_ge, fill=0.0,
            base=0, pattern=[[-32, 4]], channel_multiplier=1), wr=[pers["ob124"]])
        O("gpsimd", lambda e: e.affine_select(
            out=ob124[:], in_=ob124[:], compare_op=ALU.is_ge, fill=0.0,
            base=30, pattern=[[32, 4]], channel_multiplier=-1), wr=[pers["ob124"]])
        O("gpsimd", lambda e: e.memset(ob2t_lo[:], 1.0), wr=[pers["ob2t_lo"]])
        O("gpsimd", lambda e: e.affine_select(
            out=ob2t_lo[:], in_=ob2t_lo[:], compare_op=ALU.is_ge, fill=0.0,
            base=0, pattern=[[1, 128]], channel_multiplier=-32), wr=[pers["ob2t_lo"]])
        O("gpsimd", lambda e: e.affine_select(
            out=ob2t_lo[:], in_=ob2t_lo[:], compare_op=ALU.is_ge, fill=0.0,
            base=30, pattern=[[-1, 128]], channel_multiplier=32), wr=[pers["ob2t_lo"]])
        O("gpsimd", lambda e: e.memset(ob2t_hi[:], 1.0), wr=[pers["ob2t_hi"]])
        O("gpsimd", lambda e: e.affine_select(
            out=ob2t_hi[:], in_=ob2t_hi[:], compare_op=ALU.is_ge, fill=0.0,
            base=-64, pattern=[[1, 128]], channel_multiplier=-32), wr=[pers["ob2t_hi"]])
        O("gpsimd", lambda e: e.affine_select(
            out=ob2t_hi[:], in_=ob2t_hi[:], compare_op=ALU.is_ge, fill=0.0,
            base=94, pattern=[[-1, 128]], channel_multiplier=32), wr=[pers["ob2t_hi"]])
        O("gpsimd", lambda e: e.iota(ritok[:], [[128, 32]], base=1, channel_multiplier=1,
                                     allow_small_or_imprecise_dtypes=True), wr=[pers["ritok"]])
        O("gpsimd", lambda e: e.memset(zbig[:], 0.0), wr=[pers["zbig"]])
        O("gpsimd", lambda e: e.memset(tails[:], 0.0), wr=[pers["tails"]])
        for j, v in ((0, 1e-5), (1, 1e-6), (2, -1.0), (3, 1.0)):
            O("gpsimd", lambda e, j=j, v=v: e.memset(epst[:, j:j + 1], v), wr=[pers["epst"]])

        fari = AFA.take(0, 4096)
        O("gpsimd", lambda e: e.iota(FA[0:1, 0:T], [[1, T]], base=1, channel_multiplier=0,
                                     allow_small_or_imprecise_dtypes=True), wr=[fari])
        O("vector", lambda e: e.reciprocal(FA[0:1, 0:T], FA[0:1, 0:T]), rd=[fari], wr=[fari])
        O("vector", lambda e: e.reciprocal(ritok[:], ritok[:]),
          rd=[pers["ritok"]], wr=[pers["ritok"]])
        ST(RIROW, FA[0:1, 0:T], rd=[fari], wr=[db(RIROW)])

        O("scalar", lambda e: e.activation(a_sb[:].rearrange("p a b -> p (a b)"),
                                           a_sb[:].rearrange("p a b -> p (a b)"), AF.Exp),
          rd=[pers["a_sb"]], wr=[pers["a_sb"]])
        O("vector", lambda e: e.tensor_scalar(a_sb[:].rearrange("p a b -> p (a b)"),
                                              a_sb[:].rearrange("p a b -> p (a b)"),
                                              -1.0, None, ALU.mult),
          rd=[pers["a_sb"]], wr=[pers["a_sb"]])

        fpe = AFA.take(4096, 4300)
        LD(FA[:, 4096:4159], wd2pe, wr=[fpe])
        O("vector", lambda e: e.tensor_copy(wd_sb[:], FA[:, 4096:4159]),
          rd=[fpe], wr=[pers["wd_sb"]])
        LD(FA[:, 4096:4159], wg2pe, wr=[fpe])
        O("vector", lambda e: e.tensor_copy(wg_sb[:], FA[:, 4096:4159]),
          rd=[fpe], wr=[pers["wg_sb"]])
        LD(FA[:, 4160:4162], cnw4, wr=[fpe])
        LD(FA[:, 4162:4164], gcw3, wr=[fpe])
        O("vector", lambda e: e.tensor_copy(cn4_sb[:], FA[:, 4160:4162]),
          rd=[fpe], wr=[pers["cn4_sb"]])
        O("vector", lambda e: e.tensor_copy(gc3_sb[:], FA[:, 4162:4164]),
          rd=[fpe], wr=[pers["gc3_sb"]])

        def binarize(wt, bwout, nchunk, onescol, onesbuf, Hh):
            fw = AFA.take(0, 4096)
            bab = AB1.take(0, nchunk * Hh)
            bsg = AB2.take(0, 4096)
            bar = AB2.take(8192, 8192 + Hh)
            far = AFB.take(0, 4096)
            for c in range(nchunk):
                LD(FA[:, 0:Hh], wt[c * 128:(c + 1) * 128, :], wr=[fw])
                O("scalar", lambda e, c=c, Hh=Hh: e.activation(
                    b1v(c * Hh, 128, Hh), FA[:, 0:Hh], AF.Abs), rd=[fw], wr=[bab])
            nq = Hh // 512
            for q in range(nq):
                pq = (P0, P1)[q % 2]
                for c in range(nchunk):
                    O("tensor", lambda e, c=c, q=q, pq=pq, Hh=Hh: e.matmul(
                        pq[0:1, :], onescol[:], b1v(c * Hh + q * 512, 128, 512),
                        start=(c == 0), stop=(c == nchunk - 1)),
                      rd=[bab, onesbuf], wr=[pb(pq)], inc=(c == nchunk - 1))
                O("vector", lambda e, q=q, pq=pq: e.tensor_copy(
                    FB[0:1, q * 512:(q + 1) * 512], pq[0:1, :]),
                  rd=[pb(pq)], wr=[far])
            O("vector", lambda e, Hh=Hh: e.tensor_copy(
                b2v(8192, 1, Hh), FB[0:1, 0:Hh]), rd=[far], wr=[bar])
            for c in range(nchunk):
                LD(FA[:, 0:Hh], wt[c * 128:(c + 1) * 128, :], wr=[fw])
                O("scalar", lambda e, Hh=Hh: e.activation(
                    b2v(0, 128, Hh), FA[:, 0:Hh], AF.Sign), rd=[fw], wr=[bsg])
                for q in range(Hh // 512):
                    ph = q % 2
                    O("tensor", lambda e, q=q, ph=ph, Hh=Hh: e.matmul(
                        PY[:, ph * 512:(ph + 1) * 512], ones1b[:],
                        b2v(8192 + q * 512, 1, 512), start=True, stop=True),
                      rd=[bar, pers["ones1b"]], wr=[pb(PY)])
                    O("vector", lambda e, q=q, ph=ph: e.tensor_tensor(
                        b2v(q * 512, 128, 512), b2v(q * 512, 128, 512),
                        PY[:, ph * 512:(ph + 1) * 512], ALU.mult),
                      rd=[pb(PY), bsg], wr=[bsg])
                ST(bwout[c * 128:(c + 1) * 128, :], b2v(0, 128, Hh),
                   rd=[bsg], wr=[db(bwout)])

        if upto >= 1:
            binarize(w1t, BW1, 8, onesDIb, pers["onesDIb"], H4)
            binarize(w2t, BW2, 32, onesH4b, pers["onesH4b"], D)

        def layer_norm_phase(src_ap, src_buf, dst_f, dst_b, rn_dst, first):
            fxt = AFA.take(0, 4096)
            fsq = AFB.take(0, 1024)
            fxn = AFB.take(1024, 2048)
            bxb = AB1.take(0, 1024)
            brn = AB2.take(0, 1024)
            rows = AFB.take(2048, 5120)
            for tc in range(NTC):
                for dc in range(8):
                    LD(FA[:, dc * 512:(dc + 1) * 512],
                       src_ap[dc * 128:(dc + 1) * 128, tc * TC:(tc + 1) * TC],
                       rd=(dbt(src_buf, tc * TC, (tc + 1) * TC) if src_buf is not None else ()),
                       wr=[fxt])
                for dc in range(8):
                    sl = dc % 2
                    O("scalar", lambda e, dc=dc, sl=sl: e.activation(
                        FB[:, sl * 512:(sl + 1) * 512], FA[:, dc * 512:(dc + 1) * 512],
                        AF.Square), rd=[fxt], wr=[fsq])
                    O("tensor", lambda e, dc=dc: e.matmul(
                        P0[0:1, :], onesv[:], FA[:, dc * 512:(dc + 1) * 512],
                        start=(dc == 0), stop=(dc == 7)),
                      rd=[fxt, pers["onesv"]], wr=[pb(P0)], inc=(dc == 7))
                    O("tensor", lambda e, dc=dc, sl=sl: e.matmul(
                        P1[0:1, :], onesv[:], FB[:, sl * 512:(sl + 1) * 512],
                        start=(dc == 0), stop=(dc == 7)),
                      rd=[fsq, pers["onesv"]], wr=[pb(P1)], inc=(dc == 7))
                mu = FB[0:1, 2048:2560]
                var = FB[0:1, 2560:3072]
                sd = FB[0:1, 3072:3584]
                rr = FB[0:1, 3584:4096]
                r2 = FB[0:1, 4096:4608]
                t5 = FB[0:1, 4608:5120]
                O("vector", lambda e: e.tensor_copy(mu, P0[0:1, :]), rd=[pb(P0)], wr=[rows])
                O("vector", lambda e: e.tensor_tensor(var, mu, mu, ALU.mult),
                  rd=[rows], wr=[rows])
                O("vector", lambda e: e.tensor_tensor(var, P1[0:1, :], var, ALU.subtract),
                  rd=[pb(P1), rows], wr=[rows])
                O("scalar", lambda e: e.activation(sd, var, AF.Sqrt, bias=epst[0:1, 0:1]),
                  rd=[rows, pers["epst"]], wr=[rows])
                O("vector", lambda e: e.reciprocal(rr, sd), rd=[rows], wr=[rows])
                if first:
                    O("vector", lambda e: e.tensor_tensor(t5, var, rr, ALU.mult),
                      rd=[rows], wr=[rows])
                    O("vector", lambda e: e.tensor_tensor(t5, t5, rr, ALU.mult),
                      rd=[rows], wr=[rows])
                    O("scalar", lambda e: e.activation(t5, t5, AF.Sqrt, bias=epst[0:1, 0:1]),
                      rd=[rows, pers["epst"]], wr=[rows])
                    O("vector", lambda e: e.reciprocal(r2, t5), rd=[rows], wr=[rows])
                O("tensor", lambda e: e.matmul(P2[:, :], ones1[:], mu, start=True, stop=True),
                  rd=[rows, pers["ones1"]], wr=[pb(P2)])
                O("tensor", lambda e: e.matmul(P3[:, :], ones1[:], rr, start=True, stop=True),
                  rd=[rows, pers["ones1"]], wr=[pb(P3)])
                if first:
                    O("tensor", lambda e: e.matmul(P4[:, :], ones1[:], r2,
                                                   start=True, stop=True),
                      rd=[rows, pers["ones1"]], wr=[pb(P4)])
                for dc in range(8):
                    sl = dc % 2
                    xn = FB[:, 1024 + sl * 512:1024 + (sl + 1) * 512]
                    xb = b1v(sl * 512, 128, 512)
                    rn = b2v(sl * 512, 128, 512)
                    O("vector", lambda e, dc=dc, xn=xn: e.tensor_tensor(
                        xn, FA[:, dc * 512:(dc + 1) * 512], P2[:, :], ALU.subtract),
                      rd=[fxt, pb(P2)], wr=[fxn])
                    O("vector", lambda e, xn=xn: e.tensor_tensor(xn, xn, P3[:, :], ALU.mult),
                      rd=[pb(P3), fxn], wr=[fxn])
                    O("scalar", lambda e, xn=xn, xb=xb: e.activation(xb, xn, AF.Copy),
                      rd=[fxn], wr=[bxb])
                    if first:
                        O("vector", lambda e, xn=xn, rn=rn: e.tensor_tensor(
                            rn, xn, P4[:, :], ALU.mult), rd=[fxn, pb(P4)], wr=[brn])
                    ST(dst_f[dc * 128:(dc + 1) * 128, tc * TC:(tc + 1) * TC], xn,
                       rd=[fxn], wr=dbt(dst_f, tc * TC, (tc + 1) * TC))
                    ST(dst_b[dc * 128:(dc + 1) * 128, tc * TC:(tc + 1) * TC], xb,
                       rd=[bxb], wr=dbt(dst_b, tc * TC, (tc + 1) * TC))
                    if first:
                        ST(rn_dst[dc * 128:(dc + 1) * 128, tc * TC:(tc + 1) * TC], rn,
                           rd=[brn], wr=dbt(rn_dst, tc * TC, (tc + 1) * TC))

        if upto >= 2:
            layer_norm_phase(x_in, None, XN_f, XN_b, RN_b, True)

        if upto >= 3:
            bw1p = AB1.take(0, 32768)
            fwl = AFA.take(0, 4096)
            for c in range(8):
                LD(FA[:, 0:4096], w_ip[c * 128:(c + 1) * 128, :], wr=[fwl])
                O("vector", lambda e, c=c: e.tensor_scalar(
                    b1v(c * 4096, 128, 4096), FA[:, 0:4096], nw_sb[:, c:c + 1],
                    None, ALU.mult), rd=[fwl, pers["nw_sb"]], wr=[bw1p])
            brn_t = [AB2.take(s * 4096, (s + 1) * 4096) for s in range(2)]
            bub = AB2.take(8192, 8704)
            fxp = AFA.take(4096, 4616)
            facc = AFA.take(4616, 5128)
            fuo = AFB.take(0, 512)
            fgo = AFB.take(512, 1024)
            banks = [(P0, pb(P0)), (P1, pb(P1)), (P2, pb(P2)), (P3, pb(P3))]
            for tc in range(NTC):
                sl = tc % 2
                for h in range(2):
                    LD(b2v(sl * 4096 + h * 2048, 128, 4, 512, 512, 1),
                       RN_b.rearrange("(a p) t -> p a t", p=128)[:, h * 4:(h + 1) * 4,
                                                                tc * TC:(tc + 1) * TC],
                       rd=dbt(RN_b, tc * TC, (tc + 1) * TC), wr=[brn_t[sl]])
                for m in range(32):
                    bk, bkb = banks[m % 4]
                    for k in range(8):
                        O("tensor", lambda e, m=m, k=k, bk=bk, sl=sl: e.matmul(
                            bk[:, :], b1v(k * 4096 + m * 128, 128, 128),
                            b2v(sl * 4096 + k * 512, 128, 512),
                            start=(k == 0), stop=(k == 7)),
                          rd=[bw1p, brn_t[sl]], wr=[bkb], inc=(k == 7))
                    if m < 16:
                        mc = m
                        O("vector", lambda e, mc=mc: e.tensor_copy(
                            FA[:, 4096:4099], tails[:, mc, :]),
                          rd=[pers["tails"]], wr=[fxp])
                        O("scalar", lambda e, bk=bk: e.activation(
                            FA[:, 4099:4611], bk[:, :], AF.Copy), rd=[bkb], wr=[fxp])
                        O("vector", lambda e, mc=mc: e.tensor_copy(
                            tails[:, mc, :], FA[:, 4608:4611]),
                          rd=[fxp], wr=[pers["tails"]])
                        O("vector", lambda e, mc=mc: e.tensor_scalar(
                            FA[:, 4616:5128], FA[:, 4096:4608], cw_sb[:, mc, 0:1],
                            None, ALU.mult), rd=[fxp, pers["cw_sb"]], wr=[facc])
                        for kk in range(1, 4):
                            O("vector", lambda e, mc=mc, kk=kk: e.scalar_tensor_tensor(
                                FA[:, 4616:5128], FA[:, 4096 + kk:4608 + kk],
                                cw_sb[:, mc, kk:kk + 1], FA[:, 4616:5128],
                                ALU.mult, ALU.add), rd=[fxp, pers["cw_sb"], facc], wr=[facc])
                        O("scalar", lambda e, mc=mc: e.activation(
                            FB[:, 0:512], FA[:, 4616:5128], AF.Silu,
                            bias=cb_sb[:, mc:mc + 1]), rd=[facc, pers["cb_sb"]], wr=[fuo])
                        O("scalar", lambda e, mc=mc: e.activation(
                            b2v(8192, 128, 512), FA[:, 4616:5128], AF.Silu,
                            bias=cb_sb[:, mc:mc + 1]), rd=[facc, pers["cb_sb"]], wr=[bub])
                        ST(U_f[mc * 128:(mc + 1) * 128, tc * TC:(tc + 1) * TC],
                           FB[:, 0:512], rd=[fuo], wr=dbt(U_f, tc * TC, (tc + 1) * TC))
                        ST(U_b[mc * 128:(mc + 1) * 128, tc * TC:(tc + 1) * TC],
                           b2v(8192, 128, 512), rd=[bub], wr=dbt(U_b, tc * TC, (tc + 1) * TC))
                    else:
                        mc = m - 16
                        O("scalar", lambda e, bk=bk: e.activation(
                            FB[:, 512:1024], bk[:, :], AF.Silu), rd=[bkb], wr=[fgo])
                        ST(G_f[mc * 128:(mc + 1) * 128, tc * TC:(tc + 1) * TC],
                           FB[:, 512:1024], rd=[fgo], wr=dbt(G_f, tc * TC, (tc + 1) * TC))

        KSUB = int(os.environ.get("KSUB", "99"))
        KTC = int(os.environ.get("KTC", "8"))
        if upto >= 4:
            bxp = AB1.take(0, 1536)
            bdt = AB1.take(2048, 4096)
            bdtb = AB1.take(4096, 4608)
            bbce = AB1.take(4608, 5120)
            bubt = [AB2.take(s * 8192, (s + 1) * 8192) for s in range(2)]
            bduo = AB2.take(16384, 16896)
            fwl = AFA.take(0, 2048)
            f_ab = AFB.take(0, 512)
            f_ln = AFB.take(512, 1024)
            f_dl = AFB.take(1024, 1536)
            f_uf = AFB.take(1536, 2048)
            LD(FA[:, 0:1536], w_xp, wr=[fwl])
            O("vector", lambda e: e.tensor_copy(b1v(0, 128, 1536), FA[:, 0:1536]),
              rd=[fwl], wr=[bxp])
            LD(FA[0:64, 0:2048], w_dt, wr=[fwl])
            O("vector", lambda e: e.tensor_copy(b1v(2048, 64, 2048), FA[0:64, 0:2048]),
              rd=[fwl], wr=[bdt])
            for tc in range(KTC):
                sl = tc % 2
                for h in range(4):
                    LD(b2v(sl * 8192 + h * 2048, 128, 4, 512, 512, 1),
                       U_b.rearrange("(a p) t -> p a t", p=128)[:, h * 4:(h + 1) * 4,
                                                               tc * TC:(tc + 1) * TC],
                       rd=dbt(U_b, tc * TC, (tc + 1) * TC), wr=[bubt[sl]])
                for k in range(16):
                    O("tensor", lambda e, k=k, sl=sl: e.matmul(
                        P4[0:96, :], b1v(k * 96, 128, 96),
                        b2v(sl * 8192 + k * 512, 128, 512),
                        start=(k == 0), stop=(k == 15)),
                      rd=[bxp, bubt[sl]], wr=[pb(P4)], inc=(k == 15))
                O("scalar", lambda e: e.activation(b1v(4096, 64, 512), P4[0:64, :], AF.Copy),
                  rd=[pb(P4)], wr=[bdtb])
                O("vector", lambda e: e.tensor_copy(b1v(4608, 96, 512)[64:96, :],
                                                    P4[64:96, :]),
                  rd=[pb(P4)], wr=[bbce])
                ST(BROW[:, tc * TC:(tc + 1) * TC], b1v(4608, 96, 512)[64:80, :],
                   rd=[bbce], wr=dbt(BROW, tc * TC, (tc + 1) * TC))
                ST(CROW[:, tc * TC:(tc + 1) * TC], b1v(4608, 96, 512)[80:96, :],
                   rd=[bbce], wr=dbt(CROW, tc * TC, (tc + 1) * TC))
                for mc in range(16 if KSUB >= 1 else 0):
                    pk, pkb = ((P0, pb(P0)), (P1, pb(P1)))[mc % 2]
                    O("tensor", lambda e, mc=mc, pk=pk: e.matmul(
                        pk[:, :], b1v(2048 + mc * 128, 64, 128), b1v(4096, 64, 512),
                        start=True, stop=True), rd=[bdt, bdtb], wr=[pkb])
                    if KSUB < 20:
                        continue
                    if KSUB >= 21:
                        O("scalar", lambda e, pk=pk, mc=mc: e.activation(
                            FB[:, 0:512], pk[:, :], AF.Abs, bias=dtb_sb[:, mc:mc + 1]),
                          rd=[pkb, pers["dtb_sb"]], wr=[f_ab])
                    if KSUB >= 22:
                        O("scalar", lambda e: e.activation(FB[:, 512:1024], FB[:, 0:512],
                                                           AF.Exp, scale=epst[:, 2:3]),
                          rd=[f_ab, pers["epst"]], wr=[f_ln])
                    if KSUB >= 23:
                        O("scalar", lambda e: e.activation(FB[:, 512:1024], FB[:, 512:1024],
                                                           AF.Ln, bias=epst[:, 3:4]),
                          rd=[f_ln, pers["epst"]], wr=[f_ln])
                    if KSUB >= 24:
                        O("vector", lambda e, pk=pk, mc=mc: e.tensor_scalar(
                            FB[:, 1024:1536], pk[:, :], dtb_sb[:, mc:mc + 1], 0.0,
                            ALU.add, ALU.max), rd=[pkb, pers["dtb_sb"]], wr=[f_dl])
                    if KSUB >= 25:
                        O("vector", lambda e: e.tensor_tensor(
                            FB[:, 1024:1536], FB[:, 1024:1536], FB[:, 512:1024], ALU.add),
                          rd=[f_ln, f_dl], wr=[f_dl])
                    if KSUB < 30:
                        continue
                    LD(FB[:, 1536:2048],
                       U_f[mc * 128:(mc + 1) * 128, tc * TC:(tc + 1) * TC],
                       rd=dbt(U_f, tc * TC, (tc + 1) * TC), wr=[f_uf])
                    O("vector", lambda e: e.tensor_tensor(
                        b2v(16384, 128, 512), FB[:, 1024:1536], FB[:, 1536:2048],
                        ALU.mult), rd=[f_dl, f_uf], wr=[bduo])
                    ST(DLT[mc * 128:(mc + 1) * 128, tc * TC:(tc + 1) * TC],
                       FB[:, 1024:1536], rd=[f_dl], wr=dbt(DLT, tc * TC, (tc + 1) * TC))
                    ST(DU_b[mc * 128:(mc + 1) * 128, tc * TC:(tc + 1) * TC],
                       b2v(16384, 128, 512), rd=[bduo], wr=dbt(DU_b, tc * TC, (tc + 1) * TC))

        if upto >= 5:
            bB = AB1.take(0, 16384)
            bX = AB1.take(16384, 32768)
            bC = AB2.take(0, 16384)
            bH = AB2.take(16384, 32768)
            f_d = AFA.take(0, 1024)
            f_u = AFA.take(1024, 2048)
            f_g = AFA.take(2048, 3072)
            f_y = AFA.take(3072, 4096)
            for tb in range(NTB):
                for n in range(16):
                    ST(b1v(n * TB, 128, TB),
                       bass.AP(BROW.tensor, n * T + tb * TB, [[0, 128], [1, TB]]),
                       rd=dbt(BROW, tb * TB, (tb + 1) * TB), wr=[bB], q="scalar")
                    ST(b2v(n * TB, 128, TB),
                       bass.AP(CROW.tensor, n * T + tb * TB, [[0, 128], [1, TB]]),
                       rd=dbt(CROW, tb * TB, (tb + 1) * TB), wr=[bC], q="scalar")
                for dc in range(16):
                    LD(FA[:, 0:1024], DLT[dc * 128:(dc + 1) * 128, tb * TB:(tb + 1) * TB],
                       rd=dbt(DLT, tb * TB, (tb + 1) * TB), wr=[f_d])
                    LD(dub_sb[:], DU_b[dc * 128:(dc + 1) * 128, tb * TB:(tb + 1) * TB],
                       rd=dbt(DU_b, tb * TB, (tb + 1) * TB), wr=[pers["dub_sb"]])
                    LD(FA[:, 1024:2048], U_f[dc * 128:(dc + 1) * 128, tb * TB:(tb + 1) * TB],
                       rd=dbt(U_f, tb * TB, (tb + 1) * TB), wr=[f_u])
                    LD(FA[:, 2048:3072], G_f[dc * 128:(dc + 1) * 128, tb * TB:(tb + 1) * TB],
                       rd=dbt(G_f, tb * TB, (tb + 1) * TB), wr=[f_g])
                    O("vector", lambda e: e.tensor_tensor(
                        b1v(16384, 128, 16, TB, TB, 1),
                        bass.AP(dub_sb, 0, [[TB, 128], [0, 16], [1, TB]]),
                        b1v(0, 128, 16, TB, TB, 1), ALU.mult),
                      rd=[pers["dub_sb"], bB], wr=[bX])
                    for n in range(16):
                        sl = n % 2
                        ab = pers["abuf0"] if sl == 0 else pers["abuf1"]
                        O("scalar", lambda e, sl=sl, dc=dc, n=n: e.activation(
                            abuf[:, sl, :], FA[:, 0:1024], AF.Exp,
                            scale=a_sb[:, dc, n:n + 1]),
                          rd=[f_d, pers["a_sb"]], wr=[ab])
                        if tb == 0:
                            O("vector", lambda e, sl=sl, n=n: e.tensor_tensor_scan(
                                b2v(16384 + n * TB, 128, TB), abuf[:, sl, :],
                                b1v(16384 + n * TB, 128, TB), 0.0, ALU.mult, ALU.add),
                              rd=[ab, bX], wr=[bH])
                        else:
                            O("vector", lambda e, sl=sl, n=n, dc=dc: e.tensor_tensor_scan(
                                b2v(16384 + n * TB, 128, TB), abuf[:, sl, :],
                                b1v(16384 + n * TB, 128, TB),
                                carry[:, dc, n:n + 1], ALU.mult, ALU.add),
                              rd=[ab, bX, pers["carry"]], wr=[bH])
                    O("vector", lambda e: e.drain(), inc=False)
                    O("vector", lambda e, dc=dc: e.tensor_copy(
                        carry[:, dc, :],
                        bass.AP(B2, 16384 + TB - 1, [[32768, 128], [TB, 16]])),
                      rd=[bH], wr=[pers["carry"]])
                    O("vector", lambda e: e.tensor_tensor(
                        b1v(16384, 128, 16, TB, TB, 1), b2v(16384, 128, 16, TB, TB, 1),
                        b2v(0, 128, 16, TB, TB, 1), ALU.mult),
                      rd=[bH, bC], wr=[bX])
                    for n in range(16):
                        for q in range(2):
                            O("tensor", lambda e, n=n, q=q: e.matmul(
                                PY[:, q * 512:(q + 1) * 512], idb[:],
                                b1v(16384 + n * TB + q * 512, 128, 512),
                                start=(n == 0), stop=(n == 15)),
                              rd=[bX, pers["idb"]], wr=[pb(PY)],
                              inc=(n == 15 and q == 1))
                    for q in range(2):
                        O("vector", lambda e, q=q, dc=dc: e.scalar_tensor_tensor(
                            FA[:, 3072 + q * 512:3072 + (q + 1) * 512],
                            FA[:, 1024 + q * 512:1024 + (q + 1) * 512],
                            d_sb[:, dc:dc + 1], PY[:, q * 512:(q + 1) * 512],
                            ALU.mult, ALU.add),
                          rd=[f_u, pers["d_sb"], pb(PY)], wr=[f_y])
                        O("vector", lambda e, q=q: e.tensor_tensor(
                            b2v(16384 + q * 512, 128, 512),
                            FA[:, 3072 + q * 512:3072 + (q + 1) * 512],
                            FA[:, 2048 + q * 512:2048 + (q + 1) * 512], ALU.mult),
                          rd=[f_y, f_g], wr=[bH])
                    ST(YG_b[dc * 128:(dc + 1) * 128, tb * TB:(tb + 1) * TB],
                       b2v(16384, 128, 1024), rd=[bH], wr=dbt(YG_b, tb * TB, (tb + 1) * TB))

        if upto >= 6:
            bwo = AB1.take(0, 16384)
            fwl = AFA.take(0, 1024)
            for c in range(16):
                LD(FA[:, 0:1024], w_op[c * 128:(c + 1) * 128, :], wr=[fwl])
                O("vector", lambda e, c=c: e.tensor_copy(b1v(c * 1024, 128, 1024),
                                                         FA[:, 0:1024]),
                  rd=[fwl], wr=[bwo])
            byg = [AB2.take(s * 8192, (s + 1) * 8192) for s in range(2)]
            f_xn = AFA.take(0, 1024)
            f_x = AFA.take(1024, 2048)
            f_o = AFA.take(2048, 2560)
            for tc in range(NTC):
                sl = tc % 2
                for h in range(4):
                    LD(b2v(sl * 8192 + h * 2048, 128, 4, 512, 512, 1),
                       YG_b.rearrange("(a p) t -> p a t", p=128)[:, h * 4:(h + 1) * 4,
                                                                tc * TC:(tc + 1) * TC],
                       rd=dbt(YG_b, tc * TC, (tc + 1) * TC), wr=[byg[sl]])
                for m in range(8):
                    bk, bkb = [(P0, pb(P0)), (P1, pb(P1)), (P2, pb(P2)), (P3, pb(P3))][m % 4]
                    for k in range(16):
                        O("tensor", lambda e, m=m, k=k, bk=bk, sl=sl: e.matmul(
                            bk[:, :], b1v(k * 1024 + m * 128, 128, 128),
                            b2v(sl * 8192 + k * 512, 128, 512),
                            start=(k == 0), stop=(k == 15)),
                          rd=[bwo, byg[sl]], wr=[bkb], inc=(k == 15))
                    LD(FA[:, 0:512], XN_f[m * 128:(m + 1) * 128, tc * TC:(tc + 1) * TC],
                       rd=dbt(XN_f, tc * TC, (tc + 1) * TC), wr=[f_xn])
                    LD(FA[:, 1024:1536], x_in[m * 128:(m + 1) * 128, tc * TC:(tc + 1) * TC],
                       wr=[f_x])
                    O("vector", lambda e, bk=bk: e.tensor_tensor(
                        FA[:, 2048:2560], bk[:, :], FA[:, 0:512], ALU.add),
                      rd=[bkb, f_xn], wr=[f_o])
                    O("vector", lambda e: e.tensor_tensor(
                        FA[:, 2048:2560], FA[:, 2048:2560], FA[:, 1024:1536], ALU.add),
                      rd=[f_x, f_o], wr=[f_o])
                    ST(X1p[m * 128:(m + 1) * 128, tc * TC:(tc + 1) * TC], FA[:, 2048:2560],
                       rd=[f_o], wr=dbt(X1p, tc * TC, (tc + 1) * TC))

        if upto >= 7:
            bw = AB1.take(0, 32768)
            for c in range(8):
                LD(b1v(c * 4096, 128, 4096), BW1[c * 128:(c + 1) * 128, :],
                   rd=[db(BW1)], wr=[bw])
            bxn = [AB2.take(s * 4096, (s + 1) * 4096) for s in range(2)]
            bh1o = AB2.take(8192, 8704)
            banks = [(P0, pb(P0)), (P1, pb(P1)), (P2, pb(P2)), (P3, pb(P3))]
            for tc in range(NTC):
                sl = tc % 2
                for h in range(2):
                    LD(b2v(sl * 4096 + h * 2048, 128, 4, 512, 512, 1),
                       XN_b.rearrange("(a p) t -> p a t", p=128)[:, h * 4:(h + 1) * 4,
                                                                tc * TC:(tc + 1) * TC],
                       rd=dbt(XN_b, tc * TC, (tc + 1) * TC), wr=[bxn[sl]])
                for m in range(32):
                    bk, bkb = banks[m % 4]
                    for k in range(8):
                        O("tensor", lambda e, m=m, k=k, bk=bk, sl=sl: e.matmul(
                            bk[:, :], b1v(k * 4096 + m * 128, 128, 128),
                            b2v(sl * 4096 + k * 512, 128, 512),
                            start=(k == 0), stop=(k == 7)),
                          rd=[bw, bxn[sl]], wr=[bkb], inc=(k == 7))
                    O("scalar", lambda e, bk=bk, m=m: e.activation(
                        b2v(8192, 128, 512), bk[:, :], AF.Gelu, bias=b1_sb[:, m:m + 1]),
                      rd=[bkb, pers["b1_sb"]], wr=[bh1o])
                    ST(H1[m * 128:(m + 1) * 128, tc * TC:(tc + 1) * TC],
                       b2v(8192, 128, 512), rd=[bh1o], wr=dbt(H1, tc * TC, (tc + 1) * TC))

        if upto >= 8:
            bw = AB1.take(0, 32768)
            for c in range(32):
                LD(b1v(c * 1024, 128, 1024), BW2[c * 128:(c + 1) * 128, :],
                   rd=[db(BW2)], wr=[bw])
            bh1 = [AB2.take(0, 16384), AB2.take(16384, 32768)]
            f_xp = AFA.take(0, 512)
            f_o = AFA.take(512, 1024)
            for tc in range(NTC):
                sl = tc % 2
                for h in range(8):
                    LD(b2v(sl * 16384 + h * 2048, 128, 4, 512, 512, 1),
                       H1.rearrange("(a p) t -> p a t", p=128)[:, h * 4:(h + 1) * 4,
                                                              tc * TC:(tc + 1) * TC],
                       rd=dbt(H1, tc * TC, (tc + 1) * TC), wr=[bh1[sl]])
                for m in range(8):
                    bk, bkb = [(P0, pb(P0)), (P1, pb(P1)), (P2, pb(P2)), (P3, pb(P3))][m % 4]
                    for k in range(32):
                        O("tensor", lambda e, m=m, k=k, bk=bk, sl=sl: e.matmul(
                            bk[:, :], b1v(k * 1024 + m * 128, 128, 128),
                            b2v(sl * 16384 + k * 512, 128, 512),
                            start=(k == 0), stop=(k == 31)),
                          rd=[bw, bh1[sl]], wr=[bkb], inc=(k == 31))
                    LD(FA[:, 0:512], X1p[m * 128:(m + 1) * 128, tc * TC:(tc + 1) * TC],
                       rd=dbt(X1p, tc * TC, (tc + 1) * TC), wr=[f_xp])
                    O("vector", lambda e, bk=bk, m=m: e.scalar_tensor_tensor(
                        FA[:, 512:1024], bk[:, :], b2_sb[:, m:m + 1], FA[:, 0:512],
                        ALU.add, ALU.add), rd=[bkb, pers["b2_sb"], f_xp], wr=[f_o])
                    ST(X1[m * 128:(m + 1) * 128, tc * TC:(tc + 1) * TC], FA[:, 512:1024],
                       rd=[f_o], wr=dbt(X1, tc * TC, (tc + 1) * TC))

        if upto >= 9:
            layer_norm_phase(X1, X1, XN1_f, XN1_b, None, False)
            fxt = AFA.take(0, 4096)
            for dc in range(8):
                LD(FA[:, 0:4096], XN1_f[dc * 128:(dc + 1) * 128, :],
                   rd=dbt(XN1_f, 0, T), wr=[fxt])
                O("vector", lambda e, dc=dc: e.tensor_reduce(
                    gvec[:, dc:dc + 1], FA[:, 0:4096], AX.X, ALU.add),
                  rd=[fxt], wr=[pers["gvec"]])
            O("vector", lambda e: e.tensor_scalar(gvec[:], gvec[:], 1.0 / T, None,
                                                  ALU.mult),
              rd=[pers["gvec"]], wr=[pers["gvec"]])
            O("vector", lambda e: e.tensor_copy(gvec_b[:], gvec[:]),
              rd=[pers["gvec"]], wr=[pers["gvec_b"]])
            bgw = AB1.take(0, 4096)
            bg1 = AB1.take(4096, 4100)
            bg2 = AB1.take(4100, 4102)
            fwl = AFA.take(0, 512)
            for c in range(8):
                LD(FA[:, 0:512], gcw1[c * 128:(c + 1) * 128, :], wr=[fwl])
                O("vector", lambda e, c=c: e.tensor_copy(b1v(c * 512, 128, 512),
                                                         FA[:, 0:512]),
                  rd=[fwl], wr=[bgw])
            for m in range(4):
                for k in range(8):
                    O("tensor", lambda e, m=m, k=k: e.matmul(
                        P4[:, m:m + 1], b1v(k * 512 + m * 128, 128, 128),
                        gvec_b[:, k:k + 1], start=(k == 0), stop=(k == 7)),
                      rd=[bgw, pers["gvec_b"]], wr=[pb(P4)], inc=(k == 7))
            for m in range(4):
                O("scalar", lambda e, m=m: e.activation(
                    b1v(4096 + m, 128, 1), P4[:, m:m + 1], AF.Gelu,
                    bias=gcb1_sb[:, m:m + 1]), rd=[pb(P4), pers["gcb1_sb"]], wr=[bg1])
            bgw2 = AB1.take(8192, 9216)
            for c in range(4):
                LD(FA[:, 0:256], gcw2[c * 128:(c + 1) * 128, :], wr=[fwl])
                O("vector", lambda e, c=c: e.tensor_copy(b1v(8192 + c * 256, 128, 256),
                                                         FA[:, 0:256]),
                  rd=[fwl], wr=[bgw2])
            for m in range(2):
                for k in range(4):
                    O("tensor", lambda e, m=m, k=k: e.matmul(
                        P4[:, 8 + m:9 + m], b1v(8192 + k * 256 + m * 128, 128, 128),
                        b1v(4096 + k, 128, 1), start=(k == 0), stop=(k == 3)),
                      rd=[bgw2, bg1], wr=[pb(P4)], inc=(k == 3))
            for m in range(2):
                O("scalar", lambda e, m=m: e.activation(
                    b1v(4100 + m, 128, 1), P4[:, 8 + m:9 + m], AF.Gelu,
                    bias=gcb2_sb[:, m:m + 1]), rd=[pb(P4), pers["gcb2_sb"]], wr=[bg2])
            for k in range(2):
                O("tensor", lambda e, k=k: e.matmul(
                    P4[0:1, 12:13], gc3_sb[:, k:k + 1], b1v(4100 + k, 128, 1),
                    start=(k == 0), stop=(k == 1)),
                  rd=[pers["gc3_sb"], bg2], wr=[pb(P4)], inc=(k == 1))
            O("scalar", lambda e: e.activation(sc_sb[0:1, 0:1], P4[0:1, 12:13],
                                               AF.Sigmoid, bias=sc_sb[0:1, 5:6]),
              rd=[pb(P4), pers["sc_sb"]], wr=[pers["sc_sb"]])
            O("vector", lambda e: e.tensor_tensor(sc_sb[0:1, 2:3], sc_sb[0:1, 1:2],
                                                  sc_sb[0:1, 0:1], ALU.mult),
              rd=[pers["sc_sb"]], wr=[pers["sc_sb"]])
            O("vector", lambda e: e.tensor_scalar(sc_sb[0:1, 2:3], sc_sb[0:1, 2:3],
                                                  1.4, 0.1, ALU.mult, ALU.add),
              rd=[pers["sc_sb"]], wr=[pers["sc_sb"]])
            O("vector", lambda e: e.tensor_tensor(sc_sb[0:1, 2:3], sc_sb[0:1, 2:3],
                                                  sc_sb[0:1, 1:2], ALU.subtract),
              rd=[pers["sc_sb"]], wr=[pers["sc_sb"]])
            O("vector", lambda e: e.tensor_scalar(sc_sb[0:1, 3:4], sc_sb[0:1, 1:2],
                                                  0.6, None, ALU.mult),
              rd=[pers["sc_sb"]], wr=[pers["sc_sb"]])

        for zt, hs in ((ZBB, ZBB_HS), (ZPB, ZPB_HS)):
            tot = NH * hs
            CH = 128 * 512
            nch = (tot + CH - 1) // CH
            for i in range(nch):
                sz = min(CH, tot - i * CH)
                rows = sz // 512
                if rows:
                    ST(bass.AP(zt.tensor, i * CH, [[512, rows], [1, 512]]),
                       bass.AP(zbig, 0, [[512, 1], [0, rows], [1, 512]]),
                       rd=[pers["zbig"]], wr=[db(zt)])
                if sz % 512:
                    ST(bass.AP(zt.tensor, i * CH + rows * 512, [[1, 1], [1, sz % 512]]),
                       zbig[0:1, 0:sz % 512], rd=[pers["zbig"]], wr=[db(zt)])

        if upto >= 10:
            bcum = AB1.take(0, 4096)
            bone4 = AB1.take(4096, 8192)
            bcit = AB1.take(8192, 8320)
            bwt = AB1.take(8320, 8384)
            brib = AB1.take(8448, 12544)
            bci = AB1.take(12544, 16640)
            bwe = AB2.take(0, 16384)
            f_sig = AFB.take(0, 1024)
            f_sd = AFB.take(1024, 2048)
            f_et = AFB.take(2048, 3072)
            f_rw = AFB.take(4096, 4608)
            O("gpsimd", lambda e: e.memset(b1v(4096, 128, 4096), 1.0), wr=[bone4])
            O("gpsimd", lambda e: e.memset(FB[:, 2048:3072], 0.0), wr=[f_et])
            fri = AFA.take(0, 4096)
            LD(FA[0:1, 0:4096], RIROW, rd=[db(RIROW)], wr=[fri])
            for q8 in range(8):
                O("tensor", lambda e, q8=q8: e.matmul(
                    PY[:, 0:512], ones1[:], FA[0:1, q8 * 512:(q8 + 1) * 512],
                    start=True, stop=True),
                  rd=[fri, pers["ones1"]], wr=[pb(PY)])
                O("vector", lambda e, q8=q8: e.tensor_copy(
                    b1v(8448 + q8 * 512, 128, 512), PY[:, 0:512]),
                  rd=[pb(PY)], wr=[brib])
            f_x1 = AFA.take(0, 4096)
            for dc in range(8):
                po = (dc % 2) * 64
                LD(FA[:, 0:4096], XN1_f[dc * 128:(dc + 1) * 128, :],
                   rd=dbt(XN1_f, 0, T), wr=[f_x1])
                O("vector", lambda e: e.tensor_tensor_scan(
                    b1v(0, 128, 4096), b1v(4096, 128, 4096), FA[:, 0:4096],
                    0.0, ALU.mult, ALU.add), rd=[bone4, f_x1], wr=[bcum])
                O("vector", lambda e: e.drain(), inc=False)
                O("vector", lambda e: e.tensor_tensor(
                    b1v(12544, 128, 4096), b1v(0, 128, 4096), b1v(8448, 128, 4096),
                    ALU.mult), rd=[bcum, brib], wr=[bci])
                if KSUB < 41:
                    continue
                for tt in range(32 if KSUB >= 42 else 0):
                    s8 = tt % 8
                    slot = TR(lambda e, tt=tt, s8=s8: e.transpose(
                        PBm[:, s8 * 128:(s8 + 1) * 128], b1v(12544 + tt * 128, 128, 128),
                        idb[:]), rd=[bci, pers["idb"]], wr=[pb_half[s8 // 4]])
                    slot[0] = O("vector", lambda e, s8=s8: e.tensor_copy(
                        b1v(8192, 128, 128), PBm[:, s8 * 128:(s8 + 1) * 128]),
                      rd=[pb_half[s8 // 4]], wr=[bcit])
                    ST(CIT.rearrange("(a p) d -> a p d", p=128)[tt, :, dc * 128:(dc + 1) * 128],
                       b1v(8192, 128, 128), rd=[bcit], wr=dbt(CIT, tt * 128, tt * 128 + 128))
                for th in range(4 if KSUB >= 43 else 0):
                    for q in range(2):
                        col = th * 1024 + q * 512
                        O("tensor", lambda e, col=col, po=po: e.matmul(
                            P0[po:po + 63, :], wd_sb[:], b1v(12544 + col, 128, 512),
                            start=True, stop=True),
                          rd=[bci, pers["wd_sb"]], wr=[pb(P0)])
                        O("tensor", lambda e, col=col, po=po: e.matmul(
                            P1[po:po + 63, :], wg_sb[:], b1v(12544 + col, 128, 512),
                            start=True, stop=True),
                          rd=[bci, pers["wg_sb"]], wr=[pb(P1)])
                        O("scalar", lambda e, q=q, po=po: e.activation(
                            FB[po:po + 63, q * 512:(q + 1) * 512], P1[po:po + 63, :],
                            AF.Sigmoid, bias=bpe_g[po:po + 63, 0:1]),
                          rd=[pb(P1), pers["bpe_g"]], wr=[f_sig])
                        O("vector", lambda e, q=q, po=po: e.scalar_tensor_tensor(
                            FB[po:po + 63, 1024 + q * 512:1024 + (q + 1) * 512],
                            P0[po:po + 63, :], bpe_d[po:po + 63, 0:1],
                            FB[po:po + 63, q * 512:(q + 1) * 512], ALU.add, ALU.mult),
                          rd=[pb(P0), pers["bpe_d"], f_sig], wr=[f_sd])
                    O("scalar", lambda e, th=th, po=po, dc=dc: e.activation(
                        FB[po:po + 63, 2048:3072], FB[po:po + 63, 1024:2048], AF.Exp,
                        bias=stt_sb[po:po + 63, dc:dc + 1]),
                      rd=[f_sd, pers["stt_sb"]], wr=[f_et])
                    for q in range(2):
                        O("tensor", lambda e, q=q, dc=dc, po=po: e.matmul(
                            P3[0:2, :], ob124[:, (dc % 2) * 2:(dc % 2) * 2 + 2],
                            FB[:, 2048 + q * 512:2048 + (q + 1) * 512],
                            start=True, stop=True),
                          rd=[f_et, pers["ob124"]], wr=[pb(P3)])
                        O("vector", lambda e: e.tensor_copy(FB[0:2, 4096:4608], P3[0:2, :]),
                          rd=[pb(P3)], wr=[f_rw])
                        O("vector", lambda e: e.reciprocal(FB[0:2, 4096:4608],
                                                           FB[0:2, 4096:4608]),
                          rd=[f_rw], wr=[f_rw])
                        obt = ob2t_lo if dc % 2 == 0 else ob2t_hi
                        obtb = pers["ob2t_lo"] if dc % 2 == 0 else pers["ob2t_hi"]
                        O("tensor", lambda e, obt=obt: e.matmul(
                            P2[:, :], obt[:], FB[0:2, 4096:4608], start=True, stop=True),
                          rd=[f_rw, obtb], wr=[pb(P2)])
                        if KSUB >= 44:
                            O("vector", lambda e, q=q, th=th, dc=dc, po=po: e.tensor_tensor(
                                bass.AP(B2, (dc // 2) * 4096 + th * 1024 + q * 512,
                                        [[32768, 128], [1, 512]])[po:po + 63, :],
                                FB[po:po + 63, 2048 + q * 512:2048 + (q + 1) * 512],
                                P2[po:po + 63, :], ALU.mult),
                              rd=[f_et, pb(P2)], wr=[bwe])
                if KSUB < 44:
                    continue
                O("vector", lambda e, dc=dc, po=po: e.tensor_tensor(
                    bass.AP(B2, (dc // 2) * 4096, [[32768, 128], [1, 30]])[po:po + 63, :],
                    bass.AP(B2, (dc // 2) * 4096, [[32768, 128], [1, 30]])[po:po + 63, :],
                    wmask[po:po + 63, :], ALU.mult),
                  rd=[pers["wmask"], bwe], wr=[bwe])
            for dc in ([0, 2, 4, 6, 1, 3, 5, 7] if KSUB >= 45 else []):
                po = (dc % 2) * 64
                for tt in range(32):
                    s8 = tt % 8
                    slot = TR(lambda e, dc=dc, tt=tt, s8=s8, po=po: e.transpose(
                        PBm[0:128, s8 * 128:s8 * 128 + 63],
                        bass.AP(B2, (dc // 2) * 4096 + tt * 128, [[32768, 128], [1, 128]])
                        [po:po + 63, :], idb[po:po + 63, po:po + 63]),
                      rd=[bwe, pers["idb"]], wr=[pb_half[s8 // 4]], base=po)
                    slot[0] = O("vector", lambda e, s8=s8: e.tensor_copy(
                        b1v(8320, 128, 63), PBm[:, s8 * 128:s8 * 128 + 63]),
                      rd=[pb_half[s8 // 4]], wr=[bwt])
                    for hh in range(2 if KSUB >= 47 else 0):
                        h = 2 * dc + hh
                        ST(bass.AP(ZBB.tensor, h * ZBB_HS + 64 + tt * ZBB_BS - 30,
                                   [[161, 128], [1, 31]]),
                           b1v(8320 + hh * 32, 128, 31), rd=[bwt], wr=[db(ZBB)])
                        if KSUB >= 48:
                            ST(bass.AP(ZPB.tensor, h * ZPB_HS + 64 + tt * ZPB_BS + 98,
                                       [[258, 128], [1, 31]]),
                               b1v(8320 + hh * 32, 128, 31), rd=[bwt], wr=[db(ZPB)])

        if upto >= 11:
            bcv = AB1.take(0, 32768)
            bct = [AB2.take(16384 + s_ * 1024, 16384 + (s_ + 1) * 1024) for s_ in range(2)]
            NZS = 4
            bzb = [AB2.take(18432 + s_ * 128, 18432 + (s_ + 1) * 128) for s_ in range(NZS)]
            bzp = [AB2.take(18944 + s_ * 128, 18944 + (s_ + 1) * 128) for s_ in range(NZS)]
            bwbb = AB2.take(19456, 19584)
            bwpb = AB2.take(19584, 19712)
            bwfc = AB2.take(19712, 27904)

            def zload(i):
                b, h = divmod(i, NH)
                zs = i % NZS
                LD(b2v(18432 + zs * 128, 128, 128),
                   bass.AP(ZBB.tensor, h * ZBB_HS + 64 + b * ZBB_BS,
                           [[160, 128], [1, 128]]), rd=[db(ZBB)], wr=[bzb[zs]])
                if b > 0:
                    LD(b2v(18944 + zs * 128, 128, 128),
                       bass.AP(ZPB.tensor, h * ZPB_HS + 64 + b * ZPB_BS,
                               [[257, 128], [1, 128]]), rd=[db(ZPB)], wr=[bzp[zs]])

            for i0 in range(NZS - 1):
                zload(i0)
            for b in range(32):
                sl = b % 2
                LD(b2v(16384 + sl * 1024, 128, 1024),
                   CIT[b * 128:(b + 1) * 128, :], rd=dbt(CIT, b * 128, b * 128 + 128), wr=[bct[sl]])
                for h in range(NH):
                    i = b * NH + h
                    if i + NZS - 1 < 32 * NH:
                        zload(i + NZS - 1)
                    zs = i % NZS
                    pv, pvb = ((P0, pb(P0)), (P1, pb(P1)))[h % 2]
                    pvo = (h % 2) * 64
                    s8 = i % 4
                    slot = TR(lambda e, s8=s8, zs=zs: e.transpose(
                        PBm[:, s8 * 256:s8 * 256 + 128], b2v(18432 + zs * 128, 128, 128),
                        idb[:]), rd=[bzb[zs], pers["idb"]], wr=[pb_half[s8 // 2]])
                    slot[0] = O("vector", lambda e, s8=s8: e.tensor_copy(
                        b2v(19456, 128, 128), PBm[:, s8 * 256:s8 * 256 + 128]),
                      rd=[pb_half[s8 // 2]], wr=[bwbb])
                    if b > 0:
                        slot2 = TR(lambda e, s8=s8, zs=zs: e.transpose(
                            PBm[:, s8 * 256 + 128:s8 * 256 + 256],
                            b2v(18944 + zs * 128, 128, 128),
                            idb[:]), rd=[bzp[zs], pers["idb"]], wr=[pb_half[s8 // 2]])
                        slot2[0] = O("vector", lambda e, s8=s8: e.tensor_copy(
                            b2v(19584, 128, 128), PBm[:, s8 * 256 + 128:s8 * 256 + 256]),
                          rd=[pb_half[s8 // 2]], wr=[bwpb])
                    O("tensor", lambda e, h=h, sl=sl, pv=pv, pvo=pvo, b=b: e.matmul(
                        pv[pvo:pvo + 64, 0:128], b2v(16384 + sl * 1024 + h * 64, 128, 64),
                        b2v(19456, 128, 128), start=True, stop=(b == 0)),
                      rd=[bct[sl], bwbb], wr=[pvb], inc=(b == 0))
                    if b > 0:
                        O("tensor", lambda e, h=h, sl=sl, pv=pv, pvo=pvo: e.matmul(
                            pv[pvo:pvo + 64, 0:128], b2v(16384 + (1 - sl) * 1024 + h * 64, 128, 64),
                            b2v(19584, 128, 128), start=False, stop=True),
                          rd=[bct[1 - sl], bwpb], wr=[pvb])
                    O("vector", lambda e, h=h, b=b, pv=pv, pvo=pvo: e.tensor_copy(
                        b1v((h // 2) * 4096 + b * 128, 128, 128)[pvo:pvo + 64, :],
                        pv[pvo:pvo + 64, 0:128]),
                      rd=[pvb], wr=[bcv])
            fwl = AFA.take(0, 1024)
            for c in range(8):
                LD(FA[:, 0:1024], wfct[c * 128:(c + 1) * 128, :], wr=[fwl])
                O("vector", lambda e, c=c: e.tensor_copy(
                    b2v(19712 + c * 1024, 128, 1024), FA[:, 0:1024]),
                  rd=[fwl], wr=[bwfc])
            f_v = AFA.take(0, 4096)
            f_xn1 = AFB.take(0, 512)
            f_x1t = AFB.take(512, 1024)
            f_sq = AFB.take(1024, 1536)
            f_vx = AFB.take(1536, 2048)
            rows = AFB.take(2048, 4096)
            for tc in range(NTC):
                for m in range(8):
                    bk, bkb = ((P2, pb(P2)), (P3, pb(P3)))[m % 2]
                    for k in range(8):
                        O("tensor", lambda e, m=m, k=k, bk=bk, tc=tc: e.matmul(
                            bk[:, :], b2v(19712 + k * 1024 + m * 128, 128, 128),
                            b1v(k * 4096 + tc * 512, 128, 512),
                            start=(k == 0), stop=(k == 7)),
                          rd=[bwfc, bcv], wr=[bkb], inc=(k == 7))
                    LD(FB[:, 0:512], XN1_f[m * 128:(m + 1) * 128, tc * TC:(tc + 1) * TC],
                       rd=dbt(XN1_f, tc * TC, (tc + 1) * TC), wr=[f_xn1])
                    O("vector", lambda e, bk=bk, m=m: e.tensor_tensor(
                        FA[:, m * 512:(m + 1) * 512], bk[:, :], FB[:, 0:512], ALU.add),
                      rd=[bkb, f_xn1], wr=[f_v])
                for m in range(8):
                    O("scalar", lambda e, m=m: e.activation(
                        FB[:, 1024:1536], FA[:, m * 512:(m + 1) * 512], AF.Square),
                      rd=[f_v], wr=[f_sq])
                    O("tensor", lambda e, m=m: e.matmul(
                        P0[0:1, :], onesv[:], FA[:, m * 512:(m + 1) * 512],
                        start=(m == 0), stop=(m == 7)),
                      rd=[f_v, pers["onesv"]], wr=[pb(P0)], inc=(m == 7))
                    O("tensor", lambda e, m=m: e.matmul(
                        P1[0:1, :], onesv[:], FB[:, 1024:1536],
                        start=(m == 0), stop=(m == 7)),
                      rd=[f_sq, pers["onesv"]], wr=[pb(P1)], inc=(m == 7))
                mu = FB[0:1, 2048:2560]
                var = FB[0:1, 2560:3072]
                sd = FB[0:1, 3072:3584]
                rr = FB[0:1, 3584:4096]
                O("vector", lambda e: e.tensor_copy(mu, P0[0:1, :]), rd=[pb(P0)], wr=[rows])
                O("vector", lambda e: e.tensor_tensor(var, mu, mu, ALU.mult),
                  rd=[rows], wr=[rows])
                O("vector", lambda e: e.tensor_tensor(var, P1[0:1, :], var, ALU.subtract),
                  rd=[pb(P1), rows], wr=[rows])
                O("scalar", lambda e: e.activation(sd, var, AF.Sqrt, bias=epst[0:1, 1:2]),
                  rd=[rows, pers["epst"]], wr=[rows])
                O("vector", lambda e: e.reciprocal(rr, sd), rd=[rows], wr=[rows])
                O("tensor", lambda e: e.matmul(PY[:, 0:512], ones1[:], mu,
                                               start=True, stop=True),
                  rd=[rows, pers["ones1"]], wr=[pb(PY)])
                O("tensor", lambda e: e.matmul(PY[:, 512:1024], ones1[:], rr,
                                               start=True, stop=True),
                  rd=[rows, pers["ones1"]], wr=[pb(PY)])
                for m in range(8):
                    LD(FB[:, 512:1024], X1[m * 128:(m + 1) * 128, tc * TC:(tc + 1) * TC],
                       rd=dbt(X1, tc * TC, (tc + 1) * TC), wr=[f_x1t])
                    O("vector", lambda e, m=m: e.tensor_tensor(
                        FA[:, m * 512:(m + 1) * 512], FA[:, m * 512:(m + 1) * 512],
                        PY[:, 0:512], ALU.subtract), rd=[pb(PY), f_v], wr=[f_v])
                    O("vector", lambda e, m=m: e.scalar_tensor_tensor(
                        FA[:, m * 512:(m + 1) * 512], FA[:, m * 512:(m + 1) * 512],
                        g_col[:, m:m + 1], PY[:, 512:1024], ALU.mult, ALU.mult),
                      rd=[pb(PY), pers["g_col"], f_v], wr=[f_v])
                    O("vector", lambda e, m=m: e.scalar_tensor_tensor(
                        FB[:, 1536:2048], FA[:, m * 512:(m + 1) * 512],
                        bt_col[:, m:m + 1], FB[:, 512:1024], ALU.add, ALU.add),
                      rd=[f_v, pers["bt_col"], f_x1t], wr=[f_vx])
                    ST(VLX[m * 128:(m + 1) * 128, tc * TC:(tc + 1) * TC],
                       FB[:, 1536:2048], rd=[f_vx], wr=dbt(VLX, tc * TC, (tc + 1) * TC))

        if upto >= 12:
            bcn1 = AB1.take(0, 8192)
            bcn2 = AB1.take(8192, 12288)
            bcn3 = AB1.take(12288, 13312)
            bob = [AB1.take(13312, 17408), AB1.take(17408, 21504)]
            bob2 = [AB1.take(28672, 29184), AB1.take(29184, 29696)]
            bh1c = AB1.take(21504, 25600)
            bh2c = AB1.take(25600, 27648)
            bh3c = AB1.take(27648, 28672)
            fwl = AFA.take(0, 1024)
            for c in range(8):
                LD(FA[:, 0:1024], cnw1[c * 128:(c + 1) * 128, :], wr=[fwl])
                O("vector", lambda e, c=c: e.tensor_copy(b1v(c * 1024, 128, 1024),
                                                         FA[:, 0:1024]),
                  rd=[fwl], wr=[bcn1])
            for c in range(8):
                LD(FA[:, 0:512], cnw2[c * 128:(c + 1) * 128, :], wr=[fwl])
                O("vector", lambda e, c=c: e.tensor_copy(b1v(8192 + c * 512, 128, 512),
                                                         FA[:, 0:512]),
                  rd=[fwl], wr=[bcn2])
            for c in range(4):
                LD(FA[:, 0:256], cnw3[c * 128:(c + 1) * 128, :], wr=[fwl])
                O("vector", lambda e, c=c: e.tensor_copy(b1v(12288 + c * 256, 128, 256),
                                                         FA[:, 0:256]),
                  rd=[fwl], wr=[bcn3])
            f_oc2 = [AFA.take(1024, 3072), AFA.take(3072, 5120)]
            f_t2 = [AFA.take(5120, 5632), AFA.take(5632, 6144)]
            f_tf = AFB.take(0, 512)
            twist_src = [4, 5, 6, 7, 0, 1, 2, 3]
            twist_sgn = [1.0, 1.0, -1.0, -1.0, -1.0, -1.0, 1.0, 1.0]
            for cyc in range(3):
                cur_f = (XN1_f, OUTCA, OUTCB)[cyc]
                cur_b = (XN1_b, OUTBA, OUTBB)[cyc]
                nxt_f = (OUTCA, OUTCB, None)[cyc]
                nxt_b = (OUTBA, OUTBB, None)[cyc]
                for tc in range(NTC):
                    slb = tc % 2
                    for h in range(2):
                        LD(b1v(13312 + slb * 4096 + h * 2048, 128, 4, 512, 512, 1),
                           cur_b.rearrange("(a p) t -> p a t", p=128)
                           [:, h * 4:(h + 1) * 4, tc * TC:(tc + 1) * TC],
                           rd=dbt(cur_b, tc * TC, (tc + 1) * TC), wr=[bob[slb]])
                    for m in range(8):
                        bk, bkb = [(P0, pb(P0)), (P1, pb(P1)), (P2, pb(P2)), (P3, pb(P3))][m % 4]
                        for k in range(8):
                            O("tensor", lambda e, m=m, k=k, bk=bk, slb=slb: e.matmul(
                                bk[:, :], b1v(k * 1024 + m * 128, 128, 128),
                                b1v(13312 + slb * 4096 + k * 512, 128, 512),
                                start=(k == 0), stop=(k == 7)),
                              rd=[bcn1, bob[slb]], wr=[bkb], inc=(k == 7))
                        O("scalar", lambda e, bk=bk, m=m: e.activation(
                            b1v(21504 + m * 512, 128, 512), bk[:, :], AF.Gelu,
                            bias=cnb1_sb[:, m:m + 1]),
                          rd=[bkb, pers["cnb1_sb"]], wr=[bh1c])
                    for m in range(4):
                        bk, bkb = ((P2, pb(P2)), (P3, pb(P3)))[m % 2]
                        for k in range(8):
                            O("tensor", lambda e, m=m, k=k, bk=bk: e.matmul(
                                bk[:, :], b1v(8192 + k * 512 + m * 128, 128, 128),
                                b1v(21504 + k * 512, 128, 512),
                                start=(k == 0), stop=(k == 7)),
                              rd=[bcn2, bh1c], wr=[bkb], inc=(k == 7))
                        O("scalar", lambda e, bk=bk, m=m: e.activation(
                            b1v(25600 + m * 512, 128, 512), bk[:, :], AF.Gelu,
                            bias=cnb2_sb[:, m:m + 1]),
                          rd=[bkb, pers["cnb2_sb"]], wr=[bh2c])
                    for m in range(2):
                        bk, bkb = ((P0, pb(P0)), (P1, pb(P1)))[m % 2]
                        for k in range(4):
                            O("tensor", lambda e, m=m, k=k, bk=bk: e.matmul(
                                bk[:, :], b1v(12288 + k * 256 + m * 128, 128, 128),
                                b1v(25600 + k * 512, 128, 512),
                                start=(k == 0), stop=(k == 3)),
                              rd=[bcn3, bh2c], wr=[bkb], inc=(k == 3))
                        O("scalar", lambda e, bk=bk, m=m: e.activation(
                            b1v(27648 + m * 512, 128, 512), bk[:, :], AF.Gelu,
                            bias=cnb3_sb[:, m:m + 1]),
                          rd=[bkb, pers["cnb3_sb"]], wr=[bh3c])
                    for k in range(2):
                        O("tensor", lambda e, k=k: e.matmul(
                            P2[0:1, :], cn4_sb[:, k:k + 1], b1v(27648 + k * 512, 128, 512),
                            start=(k == 0), stop=(k == 1)),
                          rd=[pers["cn4_sb"], bh3c], wr=[pb(P2)], inc=(k == 1))
                    O("scalar", lambda e: e.activation(FB[0:1, 0:512], P2[0:1, :],
                                                       AF.Sigmoid, bias=sc_sb[0:1, 4:5]),
                      rd=[pb(P2), pers["sc_sb"]], wr=[f_tf])
                    O("vector", lambda e: e.tensor_scalar(
                        FB[0:1, 0:512], FB[0:1, 0:512], sc_sb[0:1, 3:4],
                        sc_sb[0:1, 2:3], ALU.mult, ALU.add),
                      rd=[pers["sc_sb"], f_tf], wr=[f_tf])
                    O("tensor", lambda e: e.matmul(P4[:, :], ones1[:], FB[0:1, 0:512],
                                                   start=True, stop=True),
                      rd=[f_tf, pers["ones1"]], wr=[pb(P4)])
                    def upd_load(dc):
                        p = dc % 2
                        LD(FA[:, 1024 + p * 2048:1536 + p * 2048],
                           cur_f[twist_src[dc] * 128:(twist_src[dc] + 1) * 128,
                                 tc * TC:(tc + 1) * TC],
                           rd=dbt(cur_f, tc * TC, (tc + 1) * TC), wr=[f_oc2[p]])
                        LD(FA[:, 1536 + p * 2048:2048 + p * 2048],
                           cur_f[dc * 128:(dc + 1) * 128, tc * TC:(tc + 1) * TC],
                           rd=dbt(cur_f, tc * TC, (tc + 1) * TC), wr=[f_oc2[p]])
                        if cyc == 2:
                            LD(FA[:, 2048 + p * 2048:2560 + p * 2048],
                               VLX[dc * 128:(dc + 1) * 128, tc * TC:(tc + 1) * TC],
                               rd=dbt(VLX, tc * TC, (tc + 1) * TC), wr=[f_oc2[p]])

                    upd_load(0)
                    for dc in range(8):
                        p = dc % 2
                        if dc < 7:
                            upd_load(dc + 1)
                        O("vector", lambda e, dc=dc, p=p: e.scalar_tensor_tensor(
                            FA[:, 5120 + p * 512:5632 + p * 512],
                            FA[:, 1024 + p * 2048:1536 + p * 2048], twist_sgn[dc],
                            P4[:, :], ALU.mult, ALU.mult),
                          rd=[f_oc2[p], pb(P4)], wr=[f_t2[p]])
                        O("vector", lambda e, p=p: e.tensor_tensor(
                            FA[:, 5120 + p * 512:5632 + p * 512],
                            FA[:, 5120 + p * 512:5632 + p * 512],
                            FA[:, 1536 + p * 2048:2048 + p * 2048],
                            ALU.add), rd=[f_oc2[p], f_t2[p]], wr=[f_t2[p]])
                        if cyc < 2:
                            ST(nxt_f[dc * 128:(dc + 1) * 128, tc * TC:(tc + 1) * TC],
                               FA[:, 5120 + p * 512:5632 + p * 512],
                               rd=[f_t2[p]], wr=dbt(nxt_f, tc * TC, (tc + 1) * TC))
                            O("scalar", lambda e, p=p: e.activation(
                                b1v(28672 + p * 512, 128, 512),
                                FA[:, 5120 + p * 512:5632 + p * 512], AF.Copy),
                              rd=[f_t2[p]], wr=[bob2[p]])
                            ST(nxt_b[dc * 128:(dc + 1) * 128, tc * TC:(tc + 1) * TC],
                               b1v(28672 + p * 512, 128, 512), rd=[bob2[p]],
                               wr=dbt(nxt_b, tc * TC, (tc + 1) * TC))
                        else:
                            O("vector", lambda e, p=p: e.tensor_tensor(
                                FA[:, 5120 + p * 512:5632 + p * 512],
                                FA[:, 5120 + p * 512:5632 + p * 512],
                                FA[:, 2048 + p * 2048:2560 + p * 2048],
                                ALU.add), rd=[f_oc2[p], f_t2[p]], wr=[f_t2[p]])
                            ST(out_ext[dc * 128:(dc + 1) * 128, tc * TC:(tc + 1) * TC],
                               FA[:, 5120 + p * 512:5632 + p * 512], rd=[f_t2[p]],
                               wr=dbt(out_ext, tc * TC, (tc + 1) * TC))

        for sm_ in DSEMS:
            if P.cnt[sm_] > 0:
                P.pending["sync"][(sm_, P.seg[sm_])] = P.cnt[sm_]
        O("sync", lambda e: e.nop())

        waited_vals = {}
        for e_ in ENGS:
            for (w_, fn_, sm_, n_, sg_, v_) in P.ops[e_]:
                for ((psm, pseg), pv) in w_:
                    if psm in CSEMS:
                        waited_vals.setdefault((psm, pseg), set()).add(pv)
        import bisect as _bisect
        wsorted = {k: sorted(v) for k, v in waited_vals.items()}

        def remap(sm_, sg_, v_):
            lst = wsorted.get((sm_, sg_), [])
            return _bisect.bisect_right(lst, v_)

        for e_ in ENGS:
            newops = []
            for (w_, fn_, sm_, n_, sg_, v_) in P.ops[e_]:
                w2 = [((psm, pseg), remap(psm, pseg, pv) if psm in CSEMS else pv)
                      for ((psm, pseg), pv) in w_]
                if sm_ in CSEMS:
                    n_ = 1 if v_ in waited_vals.get((sm_, sg_), ()) else 0
                    v_ = remap(sm_, sg_, v_)
                newops.append((w2, fn_, sm_, n_, sg_, v_))
            P.ops[e_] = newops

        def mk_section(ename):
            def body(eng):
                for (waits, fn, sm, n, seg, v_) in P.ops[ename]:
                    for ((psm, pseg), v) in waits:
                        eng.wait_ge(sems[psm][pseg], v)
                    instr = fn(eng)
                    if n:
                        instr.then_inc(sems[sm][seg], n)
            return body

        for ename in ENGS:
            getattr(block, ename)(mk_section(ename))

    return nc


_NC_CACHE = {}


def _wpe_blockdiag(wpe, c0):
    wpe = np.asarray(wpe)
    out = np.zeros((128, 63), np.float32)
    out[0:64, 0:31] = wpe[:, c0:c0 + 31]
    out[64:128, 32:63] = wpe[:, c0:c0 + 31]
    return out


def _stack32(v31):
    out = np.zeros((128, 1), np.float32)
    for j in range(4):
        out[j * 32:j * 32 + 31, 0] = v31
    return out


def _stt_layout(stt):
    stt = np.asarray(stt).reshape(NH, KW)
    out = np.zeros((128, 8), np.float32)
    for dc in range(8):
        for hh in range(2):
            base = (dc % 2) * 64 + hh * 32
            out[base:base + 31, dc] = stt[2 * dc + hh]
    return out


def _col128(v):
    v = np.asarray(v).reshape(-1, 128)
    return np.ascontiguousarray(v.T)


def _chunk128(a, inner):
    a = np.asarray(a).reshape(-1, 128, inner)
    return np.ascontiguousarray(a.transpose(1, 0, 2).reshape(128, -1))


def kernel(**inputs):
    n = 8
    x = np.asarray(inputs["L0_attn_x"])

    def t(a):
        return np.ascontiguousarray(np.asarray(a).T)

    shared = {
        "w_ip": t(inputs["L0_attn_in_proj_w"]),
        "convw": _chunk128(inputs["L0_attn_conv_w"], 4),
        "convb": _col128(inputs["L0_attn_conv_b"]),
        "w_xp": _chunk128(t(inputs["L0_attn_x_proj_w"]), 96),
        "w_dt": t(inputs["L0_attn_dt_proj_w"]),
        "dtb": _col128(inputs["L0_attn_dt_proj_b"]),
        "alog": _chunk128(inputs["L0_attn_A_log"], NST),
        "dvec": _col128(inputs["L0_attn_D"]),
        "w_op": t(inputs["L0_attn_out_proj_w"]),
        "normw": _col128(inputs["L0_attn_norm_w"]),
        "w1t": t(inputs["L0_ffn_w1"]),
        "b1": _col128(inputs["L0_ffn_b1"]),
        "w2t": t(inputs["L0_ffn_w2"]),
        "b2": _col128(inputs["L0_ffn_b2"]),
        "wd2pe": _wpe_blockdiag(inputs["L1_attn_W_pe"], 0),
        "wg2pe": _wpe_blockdiag(inputs["L1_attn_W_pe"], 31),
        "bpe63d": _stack32(np.asarray(inputs["L1_attn_b_pe"])[0:31]),
        "bpe63g": _stack32(np.asarray(inputs["L1_attn_b_pe"])[31:62]),
        "stt63": _stt_layout(inputs["L1_attn_stt_energy"]),
        "wfct": t(inputs["L1_attn_W_fc"]),
        "gamma": _col128(inputs["L1_attn_gamma"]),
        "beta": _col128(inputs["L1_attn_beta"]),
        "cnw1": np.ascontiguousarray(inputs["L1_ffn_cn_w1"]),
        "cnb1": _col128(inputs["L1_ffn_cn_b1"]),
        "cnw2": np.ascontiguousarray(inputs["L1_ffn_cn_w2"]),
        "cnb2": _col128(inputs["L1_ffn_cn_b2"]),
        "cnw3": np.ascontiguousarray(inputs["L1_ffn_cn_w3"]),
        "cnb3": _col128(inputs["L1_ffn_cn_b3"]),
        "cnw4": _col128(inputs["L1_ffn_cn_w4"]),
        "cnb4": np.asarray(inputs["L1_ffn_cn_b4"]).reshape(1, 1),
        "gcw1": np.ascontiguousarray(inputs["L1_ffn_gc_w1"]),
        "gcb1": _col128(inputs["L1_ffn_gc_b1"]),
        "gcw2": np.ascontiguousarray(inputs["L1_ffn_gc_w2"]),
        "gcb2": _col128(inputs["L1_ffn_gc_b2"]),
        "gcw3": _col128(inputs["L1_ffn_gc_w3"]),
        "gcb3": np.asarray(inputs["L1_ffn_gc_b3"]).reshape(1, 1),
        "arange_": np.asarray(inputs["L1_ffn_adaptive_range"]).reshape(1, 1),
    }
    shared = {k: np.ascontiguousarray(v, dtype=np.float32) for k, v in shared.items()}
    in_maps = []
    for c in range(n):
        m = dict(shared)
        m["x"] = np.ascontiguousarray(x[c].T)
        in_maps.append(m)

    nc = build()
    res = run_bass_kernel_spmd(nc, in_maps, core_ids=list(range(n)))
    out = np.stack([np.asarray(r["out"]).T for r in res.results])
    return out.astype(np.float32)
